# version 11
# baseline (speedup 1.0000x reference)
import numpy as np
import concourse.bass as bass
import concourse.mybir as mybir
from concourse.bass_utils import run_bass_kernel_spmd

F32 = mybir.dt.float32
BF16 = mybir.dt.bfloat16
AF = mybir.ActivationFunctionType
ALU = mybir.AluOpType
AX = mybir.AxisListType

D = 1024
H = 16
DH = 64
NDC = D // 128
NP = H // 2
I_FFN = 512
NE = 4
CH = 128


class Rec:

    def __init__(self, nc):
        self.nc = nc
        self.ops = {"t": [], "v": [], "s": [], "g": [], "y": []}
        self.sems = {}
        self.cnt = {}
        self.kind = {"t": None, "v": None, "s": None, "g": None, "y": None}
        self.guard = {"t": None, "v": None, "s": None, "g": None, "y": None}

    def sem(self, name):
        if name not in self.sems:
            self.sems[name] = self.nc.alloc_semaphore(name)
            self.cnt[name] = 0
        return self.sems[name]

    def _op(self, eng, fn, kind="op"):
        if kind != "wait" and self.guard[eng] is not None:
            h, v = self.guard[eng]
            self.guard[eng] = None
            self.ops[eng].append(lambda e: e.wait_ge(h, v))
        else:
            self.guard[eng] = None
        self.ops[eng].append(fn)
        self.kind[eng] = kind

    def t(self, fn): self._op("t", fn)
    def v(self, fn): self._op("v", fn)
    def s(self, fn): self._op("s", fn)
    def g(self, fn): self._op("g", fn)
    def y(self, fn): self._op("y", fn)

    def inc(self, eng, fn, sem, n=1):
        h = self.sem(sem)
        self._op(eng, lambda e: fn(e).then_inc(h, n), "inc")
        self.cnt[sem] += n
        return self.cnt[sem]

    def done(self, eng, sem):
        h = self.sem(sem)
        self.cnt[sem] += 1
        if self.kind[eng] == "op":
            f = self.ops[eng][-1]
            self.ops[eng][-1] = (lambda e, f=f: f(e).then_inc(h, 1))
        else:
            self._op(eng, lambda e: e.drain().then_inc(h, 1))
        self.kind[eng] = "inc"
        self.guard[eng] = (h, self.cnt[sem])
        return self.cnt[sem]

    def wait(self, eng, sem, val=None):
        h = self.sem(sem)
        v = self.cnt[sem] if val is None else val
        self._op(eng, lambda e: e.wait_ge(h, v), "wait")

    def mark(self, sem):
        self.sem(sem)
        return self.cnt[sem]

    def replay(self):
        with self.nc.Block() as block:
            block.tensor(lambda e: [f(e) for f in self.ops["t"]])
            block.vector(lambda e: [f(e) for f in self.ops["v"]])
            block.scalar(lambda e: [f(e) for f in self.ops["s"]])
            block.gpsimd(lambda e: [f(e) for f in self.ops["g"]])
            block.sync(lambda e: [f(e) for f in self.ops["y"]])


def build(T, debug=False):
    NCH = T // CH
    TILE = min(512, T)
    NT = T // TILE
    CPT = TILE // CH
    HWs = T + 4
    nc = bass.Bass("TRN2", target_bir_lowering=False)

    def din(name, shape):
        return nc.dram_tensor(name, list(shape), F32, kind="ExternalInput").ap()

    x_h = din("x", (T, D))
    xhalo_h = din("xhalo", (4, D))
    pmask_h = din("pmask", (4, 1))
    vmask_h = din("vmask", (4, 1))
    wq_h = din("q1_w", (D, D)); bq_h = din("q1_b", (D,))
    wk_h = din("k1_w", (D, D)); bk_h = din("k1_b", (D,))
    wk2_h = din("k2_w", (D, D)); bk2_h = din("k2_b", (D,))
    gw_h = din("gw_w", (DH, 1)); gwb_h = din("gw_b", (1,))
    sw_h = din("sw_w", (DH, 1)); swb_h = din("sw_b", (1,))
    wcp_h = din("cp_w", (D, D)); bcp_h = din("cp_b", (D,))
    wr_h = din("Wr", (D, 8)); rb_h = din("router_bias", (8,))
    wg_h = din("Wg", (NE, D, I_FFN))
    wu_h = din("Wu", (NE, D, I_FFN))
    wd_h = din("Wd", (NE, I_FFN, D))
    cvw_h = din("conv_w", (4, 1, D)); cvb_h = din("conv_b", (D,))

    out_h = nc.dram_tensor("out", [T, D], F32, kind="ExternalOutput").ap()
    dbg = {}
    if debug:
        for nm, shape in [("dbg_xn", (T, D)), ("dbg_gc", (H, T)),
                          ("dbg_y", (T, D)), ("dbg_attn", (T, D)),
                          ("dbg_cw", (T, 8))]:
            dbg[nm] = nc.dram_tensor(nm, list(shape), F32, kind="ExternalOutput").ap()

    wq_bf = nc.dram_tensor("wq_bf", [D, D], BF16)
    wk_bf = nc.dram_tensor("wk_bf", [D, D], BF16)
    wk2_bf = nc.dram_tensor("wk2_bf", [D, D], BF16)
    wcp_bf = nc.dram_tensor("wcp_bf", [D, D], BF16)
    wg_bf = nc.dram_tensor("wg_bf", [NE, D, I_FFN], BF16)
    wu_bf = nc.dram_tensor("wu_bf", [NE, D, I_FFN], BF16)
    wd_bf = nc.dram_tensor("wd_bf", [NE, I_FFN, D], BF16)
    cc0i = nc.dram_tensor("cc0i", [H, 1], F32)
    cc0o = nc.dram_tensor("cc0o", [4 * H, 1], F32)
    cc1i = nc.dram_tensor("cc1i", [128, 512], F32)
    cc1o = nc.dram_tensor("cc1o", [512, 512], F32)
    cc2i = nc.dram_tensor("cc2i", [128, 520], F32)
    cc2o = nc.dram_tensor("cc2o", [512, 520], F32)
    GROUPS = [[0, 1, 2, 3], [4, 5, 6, 7]]

    SB = nc.alloc_sbuf_tensor
    PS = nc.alloc_psum_tensor
    xn_fm = SB("xn_fm", [128, NDC * HWs], BF16)
    xn_tm = SB("xn_tm", [128, NCH * D], BF16)
    y_tm = SB("y_tm", [128, max(NCH * D, 8192)], BF16)
    outacc = y_tm
    s1p = SB("s1p", [128, max(NCH * 512, NDC * D)], BF16)
    cpw_sb = s1p

    lgg = SB("lgg", [16, T], F32)
    rra = SB("rra", [16, T], BF16)
    gc_tm = SB("gc_tm", [128, NCH * 16], F32)
    a_tm = SB("a_tm", [128, NCH * 16], F32)

    id_bf = SB("id_bf", [128, 128], BF16)
    s_sub = SB("s_sub", [128, 128], BF16)
    s_dn = SB("s_dn", [128, 128], BF16)
    oneh = SB("oneh", [1, 256], BF16)
    xh3row = SB("xh3row", [1, D], BF16)
    id_f32 = SB("id_f32", [128, 128], F32)
    tri_f32 = SB("tri_f32", [128, 128], F32)
    ones_bf = SB("ones_bf", [1, 512], BF16)
    ones_f32 = SB("ones_f32", [1, 128], F32)
    gwblk = SB("gwblk", [128, NDC * 16], BF16)
    swblk = SB("swblk", [128, NDC * 16], BF16)
    gwb_tmp = SB("gwb_tmp", [1, 2], BF16)
    gwb16 = SB("gwb16", [1, 16], BF16)
    swb16 = SB("swb16", [1, 16], BF16)
    bias_sb = SB("bias_sb", [1, D], BF16)
    bcp_sb = SB("bcp_sb", [1, D], BF16)
    rb_sb = SB("rb_sb", [1, 8], F32)
    cvb_sb = SB("cvb_sb", [128, 8], F32)
    cvw_sb = SB("cvw_sb", [128, 8 * 4], F32)
    pm_sb = SB("pm_sb", [4, 1], F32)
    pmT = SB("pmT", [1, 4], F32)
    vmT = SB("vmT", [1, 4], F32)
    pmbc = SB("pmbc", [128, 4], F32)
    vmbc = SB("vmbc", [128, 4], F32)
    xh_tm = SB("xh_tm", [4, D], BF16)
    gsum_blk = SB("gsum_blk", [4, 16], F32)
    off_col = SB("off_col", [16, 1], F32)

    s1in = SB("s1in", [128, 512], BF16)
    s2in = SB("s2in", [128, 512], BF16)
    vprev = SB("vprev", [128, 8], BF16)
    s1tot = SB("s1tot", [128, 512], BF16)
    s2tot = SB("s2tot", [128, 512], BF16)

    kt = SB("kt", [128, max(NDC * TILE, 4 * D, NDC * (TILE + 1))], BF16)
    qt = SB("qt", [128, max(NDC * TILE, 4096)], BF16)
    k2t = SB("k2t", [128, NDC * TILE], BF16)
    q2t = SB("q2t", [128, NDC * TILE], BF16)
    q2s = kt
    p_sb = SB("p_sb", [128, H * 128], BF16)
    av_sb = SB("av_sb", [128, D], BF16)
    ktm2 = SB("ktm2", [128, D], BF16)
    vs_sb = SB("vs_sb", [128, 512], BF16)
    e_sb = SB("e_sb", [128, D], BF16)
    o2e = SB("o2e", [128, D], BF16)
    o2row = SB("o2row", [1, D], BF16)
    yf_sb = SB("yf_sb", [128, D], BF16)
    x_tm = SB("x_tm", [128, D], F32)
    stats = SB("stats", [128, 8], F32)
    sq_sb = SB("sq_sb", [128, D], F32)
    rgc_t = SB("rgc_t", [128, 16], F32)
    rr_t = SB("rr_t", [128, 16], F32)
    wr_sb = SB("wr_sb", [128, NDC * 8], BF16)
    biasbc = SB("biasbc", [128, 8], F32)
    sc_fm = SB("sc_fm", [8, 128], F32)
    z_t = SB("z_t", [128, 8], F32)
    z_srt = SB("z_srt", [128, 8], F32)
    z_m = SB("z_m", [128, 8], F32)
    z_rep = SB("z_rep", [128, 8], F32)
    cw_sb = SB("cw_sb", [128, NCH * 8], F32)
    cw46 = SB("cw46", [128, NCH], F32)
    cw7 = SB("cw7", [128, NCH], F32)
    wbuf = SB("wbuf", [128, 2 * NDC * 128], BF16)
    wbig = qt
    wdbuf = kt
    sg_sb = k2t
    hid = q2t
    CW = min(512, T)
    conv_fm = None
    outf = None

    conv_fm = vs_sb
    outf = sq_sb
    outf_dbg = SB("outf_dbg", [128, D], F32) if debug else None
    s1run = sq_sb
    s2run = sq_sb
    pf = [PS(f"pf{i}", [128, 512], F32) for i in range(6)]
    pb = [PS(f"pb{i}", [128, 1024], BF16) for i in range(2)]

    P = Rec(nc)

    def gdma(dst, src, sem="dg"):
        return P.inc("g", lambda e: e.dma_start(out=dst, in_=src), sem, 16)

    def ydma(dst, src, sem="dy"):
        return P.inc("y", lambda e: e.dma_start(out=dst, in_=src), sem, 16)

    def sdma(dst, src, sem="ds"):
        return P.inc("s", lambda e: e.dma_start(out=dst, in_=src), sem, 16)

    def mm(out, lhsT, rhs, start, stop, tp=None):
        P.t(lambda e: e.matmul(out, lhsT, rhs, start=start, stop=stop,
                               tile_position=tp))

    def mkconst(e):
        e.memset(id_bf[:, :], 0.0)
        e.memset(id_f32[:, :], 0.0)
        e.memset(tri_f32[:, :], 0.0)
        e.memset(gwblk[:, :], 0.0)
        e.memset(swblk[:, :], 0.0)
        e.memset(s_sub[:, :], 0.0)
        e.memset(s_dn[:, :], 0.0)
        e.memset(oneh[:, :], 0.0)
        e.drain()
        e.memset(oneh[0:1, 0:1], 1.0)
        e.memset(oneh[0:1, 255:256], 1.0)
        e.drain()
        e.affine_select(out=s_sub[:, :], in_=s_sub[:, :], compare_op=ALU.not_equal,
                        fill=1.0, base=-1, pattern=[[-1, 128]], channel_multiplier=1)
        e.affine_select(out=s_dn[:, :], in_=s_dn[:, :], compare_op=ALU.not_equal,
                        fill=1.0, base=1, pattern=[[-1, 128]], channel_multiplier=1)
        e.affine_select(out=id_bf[:, :], in_=id_bf[:, :], compare_op=ALU.not_equal,
                        fill=1.0, base=0, pattern=[[-1, 128]], channel_multiplier=1)
        e.affine_select(out=id_f32[:, :], in_=id_f32[:, :], compare_op=ALU.not_equal,
                        fill=1.0, base=0, pattern=[[-1, 128]], channel_multiplier=1)
        return e.affine_select(out=tri_f32[:, :], in_=tri_f32[:, :],
                                compare_op=ALU.is_gt, fill=1.0, base=0,
                                pattern=[[-1, 128]], channel_multiplier=1)
    P.g(mkconst)
    P.done("g", "pl")

    P.v(lambda e: e.memset(ones_bf[:, :], 1.0))
    P.v(lambda e: e.memset(ones_f32[:, :], 1.0))
    P.done("v", "ve")

    gdma(bcp_sb[:, :], bcp_h.rearrange("(a b) -> a b", a=1))
    gdma(gwb_tmp[:, 0:1], gwb_h.rearrange("(a b) -> a b", a=1))
    gdma(gwb_tmp[:, 1:2], swb_h.rearrange("(a b) -> a b", a=1))
    ydma(rb_sb[:, :], rb_h.rearrange("(a b) -> a b", a=1))
    ydma(pm_sb[:, :], pmask_h)
    ydma(pmT[:, :], pmask_h.rearrange("a b -> b a"))
    ydma(vmT[:, :], vmask_h.rearrange("a b -> b a"))
    for dc in range(NDC):
        ydma(cvb_sb[:, dc:dc + 1],
             cvb_h.rearrange("(a b) -> a b", a=NDC)[dc, :].unsqueeze(1))
        for k in range(4):
            ydma(cvw_sb[:, dc * 4 + k:dc * 4 + k + 1],
                 cvw_h[k, 0, dc * 128:(dc + 1) * 128].unsqueeze(1))
    P.wait("g", "pl")
    for dc in range(NDC):
        gdma(gwblk[0:64, dc * 16 + 2 * dc:dc * 16 + 2 * dc + 1], gw_h)
        gdma(gwblk[64:128, dc * 16 + 2 * dc + 1:dc * 16 + 2 * dc + 2], gw_h)
        gdma(swblk[0:64, dc * 16 + 2 * dc:dc * 16 + 2 * dc + 1], sw_h)
        gdma(swblk[64:128, dc * 16 + 2 * dc + 1:dc * 16 + 2 * dc + 2], sw_h)
    for dc in range(NDC):
        gdma(wr_sb[:, dc * 8:(dc + 1) * 8], wr_h[dc * 128:(dc + 1) * 128, :])

    P.wait("t", "dg")
    P.wait("t", "dy")
    P.wait("t", "ve")
    mm(pf[4][0:1, 0:16], gwb_tmp[:, 0:1], ones_bf[:, 0:16], True, True)
    mm(pf[4][0:1, 16:32], gwb_tmp[:, 1:2], ones_bf[:, 0:16], True, True)
    for r in range(4):
        mm(pf[4][0:128, 32 + r:33 + r], ones_f32[:, :], pmT[:, r:r + 1], True, True)
        mm(pf[4][0:128, 40 + r:41 + r], ones_f32[:, :], vmT[:, r:r + 1], True, True)
    mm(pf[4][0:128, 48:56], ones_f32[:, :], rb_sb[:, :], True, True)
    td = P.done("t", "te")
    P.wait("v", "te", td)
    P.v(lambda e: e.tensor_copy(out=gwb16[:, :], in_=pf[4][0:1, 0:16]))
    P.v(lambda e: e.tensor_copy(out=swb16[:, :], in_=pf[4][0:1, 16:32]))
    P.v(lambda e: e.tensor_copy(out=pmbc[:, :], in_=pf[4][0:128, 32:36]))
    P.v(lambda e: e.tensor_copy(out=vmbc[:, :], in_=pf[4][0:128, 40:44]))
    P.v(lambda e: e.tensor_copy(out=biasbc[:, :], in_=pf[4][0:128, 48:56]))
    P.done("v", "ve")
    P.wait("t", "ve")

    wunits = []
    for wsrc, wdst in [(wk_h, wk_bf.ap()), (wq_h, wq_bf.ap()),
                       (wk2_h, wk2_bf.ap()), (wcp_h, wcp_bf.ap())]:
        for p in range(8):
            wunits.append((wsrc[p * 128:(p + 1) * 128, :],
                           wdst[p * 128:(p + 1) * 128, :], D))
    for ee in range(NE):
        for p in range(8):
            wunits.append((wg_h[ee, p * 128:(p + 1) * 128, :],
                           wg_bf.ap()[ee, p * 128:(p + 1) * 128, :], I_FFN))
            wunits.append((wu_h[ee, p * 128:(p + 1) * 128, :],
                           wu_bf.ap()[ee, p * 128:(p + 1) * 128, :], I_FFN))
        for p in range(4):
            wunits.append((wd_h[ee, p * 128:(p + 1) * 128, :],
                           wd_bf.ap()[ee, p * 128:(p + 1) * 128, :], D))
    wstate = {"i": 0, "projmark": None}

    def emit_wunits(n):
        for _ in range(n):
            i = wstate["i"]
            if i >= len(wunits):
                return
            wstate["i"] = i + 1
            src, dst, w = wunits[i]
            par = i % 2
            f32v = y_tm[:, par * 4096:par * 4096 + 2048].bitcast(F32)[:, 0:w]
            bfv = y_tm[:, par * 4096 + 2048:par * 4096 + 2048 + w]
            if i >= 2:
                P.wait("y", "ac")
            lded = ydma(f32v, src, "dwl")
            P.wait("s", "dwl", lded)
            if i >= 2:
                P.wait("s", "dws")
            P.s(lambda e, f32v=f32v, bfv=bfv: e.activation(out=bfv, in_=f32v,
                                                           func=AF.Copy))
            cvd = P.done("s", "ac")
            P.wait("y", "ac", cvd)
            ydma(dst, bfv, "dws")
            if wstate["i"] == 32:
                wstate["projmark"] = P.mark("dws")

    def emit_a1_tile(tl):
        P.wait("t", "dg")
        P.wait("t", "ac")
        for dinc in range(NDC):
            mm(pf[4][0:16, 0:TILE], gwblk[:, dinc * 16:(dinc + 1) * 16],
               xn_fm[:, dinc * HWs + 3 + tl * TILE:dinc * HWs + 3 + (tl + 1) * TILE],
               start=(dinc == 0), stop=False)
        mm(pf[4][0:16, 0:TILE], gwb16[:, :], ones_bf[:, 0:TILE], False, True)
        tdone = P.done("t", "te")
        P.wait("s", "te", tdone)
        P.s(lambda e, tl=tl: e.activation(out=lgg[:, tl * TILE:(tl + 1) * TILE],
                                          in_=pf[4][0:16, 0:TILE], func=AF.Sigmoid))
        P.done("s", "ac")

    def ln_rows(xs, nrows, xn_out):
        P.s(lambda e: e.activation(out=sq_sb[0:nrows, :], in_=xs, func=AF.Square))
        P.done("s", "ac")
        P.wait("v", "ac")
        P.v(lambda e: e.reduce_sum(out=stats[0:nrows, 0:1], in_=xs, axis=AX.X))
        P.v(lambda e: e.reduce_sum(out=stats[0:nrows, 1:2], in_=sq_sb[0:nrows, :],
                                   axis=AX.X))
        P.v(lambda e: e.drain())
        P.v(lambda e: e.tensor_scalar(out=stats[0:nrows, 2:3], in0=stats[0:nrows, 0:1],
                                      scalar1=1.0 / D, scalar2=None, op0=ALU.mult))
        P.v(lambda e: e.drain())
        P.v(lambda e: e.tensor_tensor(out=stats[0:nrows, 3:4], in0=stats[0:nrows, 2:3],
                                      in1=stats[0:nrows, 2:3], op=ALU.mult))
        P.v(lambda e: e.drain())
        P.v(lambda e: e.scalar_tensor_tensor(out=stats[0:nrows, 4:5],
                                             in0=stats[0:nrows, 1:2],
                                             scalar=1.0 / D, in1=stats[0:nrows, 3:4],
                                             op0=ALU.mult, op1=ALU.subtract))
        P.v(lambda e: e.drain())
        P.v(lambda e: e.tensor_scalar_add(out=stats[0:nrows, 4:5],
                                          in0=stats[0:nrows, 4:5], scalar1=1e-5))
        P.done("v", "ve")
        P.wait("s", "ve")
        P.s(lambda e: e.activation(out=stats[0:nrows, 5:6], in_=stats[0:nrows, 4:5],
                                   func=AF.Ln))
        P.s(lambda e: e.drain())
        P.s(lambda e: e.activation(out=stats[0:nrows, 6:7], in_=stats[0:nrows, 5:6],
                                   func=AF.Exp, scale=-0.5))
        P.done("s", "ac")
        P.wait("v", "ac")
        P.v(lambda e: e.tensor_scalar(out=xn_out, in0=xs,
                                      scalar1=stats[0:nrows, 2:3],
                                      scalar2=stats[0:nrows, 6:7],
                                      op0=ALU.subtract, op1=ALU.mult))
        return P.done("v", "ve")

    for c in range(NCH):
        emit_wunits(max(2, (len(wunits) + NCH - 1) // NCH))
        P.wait("y", "ve")
        ydma(x_tm[:, :], x_h[c * 128:(c + 1) * 128, :])
        dvy = P.mark("dy")
        P.wait("s", "dy", dvy)
        P.wait("v", "dy", dvy)
        vdone = ln_rows(x_tm[:, :], 128, xn_tm[:, c * D:(c + 1) * D])
        P.wait("t", "ve", vdone)
        P.wait("t", "pl")
        for dc in range(NDC):
            P.t(lambda e, dc=dc, c=c: e.transpose(
                pb[0][:, dc * 128:(dc + 1) * 128],
                xn_tm[:, c * D + dc * 128:c * D + (dc + 1) * 128],
                id_bf[:, :]))
        tdone = P.done("t", "te")
        P.wait("v", "te", tdone)
        P.v(lambda e, c=c: e.tensor_copy(
            out=bass.AP(xn_fm, 3 + c * 128, [[NDC * HWs, 128], [HWs, NDC], [1, 128]]),
            in_=pb[0][:, :].rearrange("p (a b) -> p a b", a=NDC)))
        xnfmd = P.done("v", "ve")
        P.wait("t", "ve", xnfmd)
        if (c + 1) % CPT == 0:
            emit_a1_tile((c + 1) // CPT - 1)

    P.wait("y", "ve")
    ydma(x_tm[0:4, :], xhalo_h)
    hld = P.mark("dy")
    P.wait("s", "dy", hld)
    P.wait("v", "dy", hld)
    hdone = ln_rows(x_tm[0:4, :], 4, xh_tm[:, :])
    P.wait("t", "ve", hdone)
    for dc in range(NDC):
        P.t(lambda e, dc=dc: e.transpose(pb[0][:, dc * 4:dc * 4 + 4],
                                         xh_tm[:, dc * 128:(dc + 1) * 128],
                                         id_bf[0:4, 0:4]))
    tdone = P.done("t", "te")
    P.wait("v", "te", tdone)
    for dc in range(NDC):
        P.v(lambda e, dc=dc: e.tensor_copy(out=xn_fm[:, dc * HWs:dc * HWs + 3],
                                           in_=pb[0][:, dc * 4:dc * 4 + 3]))
        P.v(lambda e, dc=dc: e.tensor_copy(
            out=xn_fm[:, dc * HWs + 3 + T:dc * HWs + 4 + T],
            in_=pb[0][:, dc * 4 + 3:dc * 4 + 4]))
    hv = P.done("v", "ve")
    P.wait("y", "ve", hv)
    ydma(xh3row[:, :], xh_tm[3:4, :])

    if debug:
        for c in range(NCH):
            P.wait("v", "dbo")
            P.v(lambda e, c=c: e.tensor_copy(out=outf_dbg[:, :],
                                             in_=xn_tm[:, c * D:(c + 1) * D]))
            dv = P.done("v", "ve")
            P.wait("y", "ve", dv)
            ydma(dbg["dbg_xn"][c * 128:(c + 1) * 128, :], outf_dbg[:, :], "dbo")

    def stream_proj(wsrc, bias_h, dst_fn, act=None, act_scale=1.0, tilebase=0):
        P.wait("g", "te")
        gdma(bias_sb[:, :], bias_h.rearrange("(a b) -> a b", a=1))
        temarks = {}
        acmarks = {}
        dsm = {}

        def emit_group(doutc):
            par = doutc % 2
            P.wait("t", "ve")
            if doutc >= 2:
                P.wait("t", "ac", acmarks[doutc - 2])
            P.wait("t", "dg")
            P.wait("t", "dsa" if par == 0 else "dsb", dsm[doutc])
            for dinc in range(NDC):
                mm(pf[par][:, 0:TILE],
                   wbuf[:, par * 1024 + dinc * 128:par * 1024 + (dinc + 1) * 128],
                   xn_fm[:, dinc * HWs + 3 + tilebase * TILE:
                         dinc * HWs + 3 + (tilebase + 1) * TILE],
                   start=(dinc == 0), stop=False)
            mm(pf[par][:, 0:TILE], bias_sb[:, doutc * 128:(doutc + 1) * 128],
               ones_bf[:, 0:TILE], start=False, stop=True)
            temarks[doutc] = P.done("t", "te")
            P.wait("s", "te", temarks[doutc])
            tgt = dst_fn(doutc)
            if act is None:
                P.s(lambda e, tgt=tgt, par=par: e.activation(
                    out=tgt, in_=pf[par][:, 0:TILE], func=AF.Copy))
            else:
                P.s(lambda e, tgt=tgt, par=par, a=act, sc=act_scale: e.activation(
                    out=tgt, in_=pf[par][:, 0:TILE], func=a, scale=sc))
            acmarks[doutc] = P.done("s", "ac")

        for doutc in range(NDC):
            par = doutc % 2
            if doutc >= 2:
                P.wait("s", "te", temarks[doutc - 2])
            sdma(wbuf[:, par * 1024:(par + 1) * 1024],
                 bass.AP(wsrc.tensor, doutc * 128,
                         [[D, 128], [128 * D, NDC], [1, 128]]),
                 "dsa" if par == 0 else "dsb")
            dsm[doutc] = P.mark("dsa" if par == 0 else "dsb")
            if doutc >= 1:
                emit_group(doutc - 1)
        emit_group(NDC - 1)
        P.wait("t", "ac")

    P.wait("v", "ac")
    P.v(lambda e: e.tensor_scalar_max(out=lgg[:, :], in0=lgg[:, :], scalar1=1e-6))
    P.done("v", "ve")
    P.wait("s", "ve")
    P.s(lambda e: e.activation(out=lgg[:, :], in_=lgg[:, :], func=AF.Ln))
    P.done("s", "ac")
    P.wait("v", "ac")
    P.v(lambda e: e.tensor_tensor_scan(out=lgg[:, :], data0=lgg[:, :],
                                       data1=lgg[:, :], initial=0.0,
                                       op0=ALU.add, op1=ALU.bypass))
    P.done("v", "ve")

    P.wait("y", "ve")
    ydma(cc0i[:, :], lgg[:, T - 1:T])
    P.wait("g", "dy")
    P.inc("g", lambda e: e.collective_compute(
        "AllGather", ALU.bypass, replica_groups=GROUPS,
        ins=[cc0i[:, :]], outs=[cc0o[:, :]]), "cc", 1)
    P.wait("g", "cc")
    gdma(gsum_blk[:, :], cc0o.ap().rearrange("(r h) one -> r (h one)", r=4), "dgc")
    P.wait("t", "dgc")
    mm(pf[4][0:16, 0:1], gsum_blk[:, :], pm_sb[:, :], True, True)
    tdone = P.done("t", "te")
    P.wait("v", "te", tdone)
    P.v(lambda e: e.tensor_copy(out=off_col[:, :], in_=pf[4][0:16, 0:1]))
    P.v(lambda e: e.drain())
    P.v(lambda e: e.tensor_scalar(out=lgg[:, :], in0=lgg[:, :],
                                  scalar1=off_col[:, :], scalar2=30.0,
                                  op0=ALU.add, op1=ALU.min))
    P.v(lambda e: e.drain())
    P.v(lambda e: e.tensor_scalar_max(out=lgg[:, :], in0=lgg[:, :], scalar1=-30.0))
    P.done("v", "ve")
    P.wait("s", "ve")
    P.s(lambda e: e.activation(out=lgg[:, :], in_=lgg[:, :], func=AF.Exp))
    P.done("s", "ac")
    P.wait("v", "ac")
    P.v(lambda e: e.tensor_scalar_add(out=lgg[:, :], in0=lgg[:, :], scalar1=1e-6))
    gcd = P.done("v", "ve")
    if debug:
        P.wait("y", "ve", gcd)
        ydma(dbg["dbg_gc"][:, :], lgg[:, :], "dbo")

    emit_wunits(len(wunits))
    P.wait("s", "dws", wstate["projmark"])
    for tl in range(NT):
        stream_proj(wk_bf.ap(), bk_h, lambda dc: kt[:, dc * TILE:(dc + 1) * TILE],
                    tilebase=tl)
        P.wait("t", "ve")
        for dinc in range(NDC):
            mm(pf[4][0:16, 0:TILE], swblk[:, dinc * 16:(dinc + 1) * 16],
               kt[:, dinc * TILE:(dinc + 1) * TILE],
               start=(dinc == 0), stop=False)
        mm(pf[4][0:16, 0:TILE], swb16[:, :], ones_bf[:, 0:TILE], False, True)
        tdone = P.done("t", "te")
        P.wait("s", "te", tdone)
        P.s(lambda e, tl=tl: e.activation(out=rra[:, tl * TILE:(tl + 1) * TILE],
                                          in_=pf[4][0:16, 0:TILE], func=AF.Silu))
        rd = P.done("s", "ac")
        P.wait("t", "ac", rd)

        for cc_i in range(CPT):
            c = tl * CPT + cc_i
            P.wait("t", "ve")
            P.t(lambda e, c=c: e.transpose(pf[4][:, 0:16],
                                           lgg[:, c * 128:(c + 1) * 128],
                                           id_f32[0:16, 0:16]))
            P.t(lambda e, c=c: e.transpose(pb[1][:, 0:16],
                                           rra[:, c * 128:(c + 1) * 128],
                                           id_bf[0:16, 0:16]))
            tdone = P.done("t", "te")
            P.wait("v", "te", tdone)
            P.v(lambda e, c=c: e.tensor_copy(out=gc_tm[:, c * 16:(c + 1) * 16],
                                             in_=pf[4][:, 0:16]))
            P.v(lambda e: e.tensor_copy(out=rr_t[:, :], in_=pb[1][:, 0:16]))
            P.v(lambda e: e.drain())
            P.v(lambda e, c=c: e.reciprocal(out=rgc_t[:, :],
                                            in_=gc_tm[:, c * 16:(c + 1) * 16]))
            P.v(lambda e: e.drain())
            P.v(lambda e, c=c: e.tensor_tensor(out=a_tm[:, c * 16:(c + 1) * 16],
                                               in0=rr_t[:, :], in1=rgc_t[:, :],
                                               op=ALU.mult))
            P.v(lambda e: e.drain())
            for h in range(H):
                P.v(lambda e, c=c, h=h: e.tensor_scalar_mul(
                    out=av_sb[:, h * 64:(h + 1) * 64],
                    in0=xn_tm[:, c * D + h * 64:c * D + (h + 1) * 64],
                    scalar1=a_tm[:, c * 16 + h:c * 16 + h + 1]))
            avd = P.done("v", "ve")
            P.wait("t", "ve", avd)
            for dc in range(NDC):
                P.t(lambda e, cc_i=cc_i, dc=dc: e.transpose(
                    pb[0][:, dc * 128:(dc + 1) * 128],
                    kt[:, dc * TILE + cc_i * 128:dc * TILE + (cc_i + 1) * 128],
                    id_bf[:, :]))
            tdone = P.done("t", "te")
            P.wait("v", "te", tdone)
            P.v(lambda e: e.tensor_copy(out=ktm2[:, :], in_=pb[0][:, :]))
            kd = P.done("v", "ve")
            P.wait("t", "ve", kd)
            for h in range(H):
                hp, hh = h // 2, h % 2
                mm(pf[0][hh * 64:(hh + 1) * 64, hp * 64:(hp + 1) * 64],
                   ktm2[:, hp * 128 + hh * 64:hp * 128 + (hh + 1) * 64],
                   av_sb[:, h * 64:(h + 1) * 64],
                   start=True, stop=True, tp=(0, hh * 64))
            tdone = P.done("t", "te")
            P.wait("v", "te", tdone)
            if c == 0:
                P.v(lambda e: e.tensor_copy(out=s1run[:, 0:512], in_=pf[0][:, :]))
            else:
                P.v(lambda e: e.tensor_tensor(out=s1run[:, 0:512],
                                              in0=s1run[:, 0:512],
                                              in1=pf[0][:, :], op=ALU.add))
            P.v(lambda e: e.drain())
            P.v(lambda e, c=c: e.tensor_copy(out=s1p[:, c * 512:(c + 1) * 512],
                                             in_=s1run[:, 0:512]))
            P.done("v", "ve")

    vd = P.mark("ve")
    P.wait("y", "ve", vd)
    ydma(cc1i[:, :], s1run[:, 0:512])
    P.wait("g", "dy")
    P.inc("g", lambda e: e.collective_compute(
        "AllGather", ALU.bypass, replica_groups=GROUPS,
        ins=[cc1i[:, :]], outs=[cc1o[:, :]]), "cc", 1)
    ccend = P.mark("cc")
    P.wait("v", "ve")
    P.v(lambda e: e.memset(x_tm[:, 0:512], 0.0))
    P.done("v", "ve")
    for r in range(4):
        P.wait("g", "cc", ccend)
        P.wait("g", "ve")
        ld = gdma(sq_sb[:, 512:1024], cc1o[r * 128:(r + 1) * 128, :], "dgc")
        P.wait("v", "dgc", ld)
        P.v(lambda e, r=r: e.scalar_tensor_tensor(
            out=x_tm[:, 0:512], in0=sq_sb[:, 512:1024],
            scalar=pmbc[:, r:r + 1], in1=x_tm[:, 0:512],
            op0=ALU.mult, op1=ALU.add))
        P.done("v", "ve")
    P.v(lambda e: e.tensor_copy(out=s1in[:, :], in_=x_tm[:, 0:512]))
    P.done("v", "ve")
    P.wait("t", "ve")

    bstate = {"o2ed": 0, "yadd": 0}
    for tl in range(NT):
        stream_proj(wq_bf.ap(), bq_h, lambda dc: qt[:, dc * TILE:(dc + 1) * TILE],
                    tilebase=tl)
        stream_proj(wk2_bf.ap(), bk2_h, lambda dc: k2t[:, dc * TILE:(dc + 1) * TILE],
                    act=AF.Sigmoid, act_scale=0.02 / 32.0, tilebase=tl)
        stream_proj(wk_bf.ap(), bk_h, lambda dc: kt[:, dc * TILE:(dc + 1) * TILE],
                    tilebase=tl)
        P.wait("v", "ac")
        P.v(lambda e: e.tensor_scalar(out=q2t[:, :], in0=qt[:, 0:NDC * TILE], scalar1=0.0,
                                      scalar2=0.98 / 8.0, op0=ALU.max, op1=ALU.mult))
        P.v(lambda e: e.drain())
        P.v(lambda e: e.scalar_tensor_tensor(out=q2t[:, :], in0=qt[:, 0:NDC * TILE],
                                             scalar=0.125, in1=q2t[:, :],
                                             op0=ALU.mult, op1=ALU.subtract))
        P.done("v", "ve")

        for cc_i in range(CPT):
            c = tl * CPT + cc_i

            def qsl(buf, h, cc_i=cc_i):
                hp, hh = h // 2, h % 2
                return buf[hh * 64:(hh + 1) * 64,
                           hp * TILE + cc_i * 128:hp * TILE + (cc_i + 1) * 128]

            P.wait("v", "te")
            if c == 0:
                P.v(lambda e: e.tensor_copy(out=s1tot[:, :], in_=s1in[:, :]))
                P.v(lambda e: e.memset(s2tot[:, :], 0.0))
            else:
                P.v(lambda e, c=c: e.tensor_tensor(
                    out=s1tot[:, :], in0=s1in[:, :],
                    in1=s1p[:, (c - 1) * 512:c * 512], op=ALU.add))
                P.v(lambda e: e.tensor_copy(out=s2tot[:, :], in_=s2run[:, 512:1024]))
            for h in range(H):
                P.v(lambda e, c=c, h=h: e.tensor_scalar_mul(
                    out=av_sb[:, h * 64:(h + 1) * 64],
                    in0=xn_tm[:, c * D + h * 64:c * D + (h + 1) * 64],
                    scalar1=a_tm[:, c * 16 + h:c * 16 + h + 1]))
            pre = P.done("v", "ve")
            P.wait("t", "ve", pre)
            for half in range(2):
                mm(pf[4 + half][:, :], s_sub[:, :],
                   xn_tm[:, c * D + half * 512:c * D + (half + 1) * 512],
                   start=True, stop=False)
                nrow = (xn_tm[0:1, (c + 1) * D + half * 512:
                              (c + 1) * D + (half + 1) * 512]
                        if c < NCH - 1 else
                        xh3row[0:1, half * 512:(half + 1) * 512])
                mm(pf[4 + half][:, :], oneh[0:1, 128:256], nrow,
                   start=False, stop=True)
            vsmm = P.done("t", "te")

            def rec_pass(kbuf, qbuf, rhs_for_o, stot, obank_off):
                for h in range(H):
                    hp, hh = h // 2, h % 2
                    mm(pf[h % 4][:, (h // 4) * 128:(h // 4) * 128 + 128],
                       qsl(kbuf, h), qsl(qbuf, h),
                       start=True, stop=True, tp=(hh * 64, 0))
                tdone = P.done("t", "te")
                P.wait("v", "te", tdone)
                for h in range(H):
                    P.v(lambda e, h=h: e.tensor_tensor(
                        out=p_sb[:, h * 128:(h + 1) * 128],
                        in0=pf[h % 4][:, (h // 4) * 128:(h // 4) * 128 + 128],
                        in1=tri_f32[:, :], op=ALU.mult))
                pd = P.done("v", "ve")
                P.wait("t", "ve", pd)
                for h in range(H):
                    hp, hh = h // 2, h % 2
                    bank = pf[2 + h // 8]
                    col = (h % 8) * 64
                    mm(bank[:, col:col + 64], p_sb[:, h * 128:(h + 1) * 128],
                       rhs_for_o[:, h * 64:(h + 1) * 64],
                       start=True, stop=False)
                    mm(bank[:, col:col + 64], qsl(qbuf, h),
                       stot[hh * 64:(hh + 1) * 64, hp * 64:(hp + 1) * 64],
                       start=False, stop=True, tp=(hh * 64, 0))
                return P.done("t", "te")

            P.wait("t", "ve", pre)
            o1done = rec_pass(kt, qt, av_sb, s1tot, 0)
            P.wait("v", "te", o1done)
            for h in range(H):
                bank = pf[2 + h // 8]
                col = (h % 8) * 64
                P.v(lambda e, c=c, h=h, bank=bank, col=col: e.tensor_scalar_mul(
                    out=y_tm[:, c * D + h * 64:c * D + (h + 1) * 64],
                    in0=bank[:, col:col + 64],
                    scalar1=gc_tm[:, c * 16 + h:c * 16 + h + 1]))
            P.v(lambda e: e.drain())
            P.wait("v", "te", vsmm)
            for half in range(2):
                P.v(lambda e, c=c, half=half: e.tensor_tensor(
                    out=e_sb[:, half * 512:(half + 1) * 512],
                    in0=pf[4 + half][:, :],
                    in1=y_tm[:, c * D + half * 512:c * D + (half + 1) * 512],
                    op=ALU.subtract))
            ed = P.done("v", "ve")
            P.wait("t", "ve", ed)
            for dc in range(NDC):
                P.t(lambda e, dc=dc, cc_i=cc_i: e.transpose(
                    pb[0][:, dc * 128:(dc + 1) * 128],
                    k2t[:, dc * TILE + cc_i * 128:dc * TILE + (cc_i + 1) * 128],
                    id_bf[:, :]))
            tdone = P.done("t", "te")
            P.wait("v", "te", tdone)
            P.v(lambda e: e.tensor_copy(out=ktm2[:, :], in_=pb[0][:, :]))
            k2d = P.done("v", "ve")
            P.wait("t", "ve", k2d)
            o2done = rec_pass(k2t, q2t, e_sb, s2tot, 0)
            for h in range(H):
                hp, hh = h // 2, h % 2
                mm(pf[5][hh * 64:(hh + 1) * 64, hp * 64:(hp + 1) * 64],
                   ktm2[:, hp * 128 + hh * 64:hp * 128 + (hh + 1) * 64],
                   e_sb[:, h * 64:(h + 1) * 64],
                   start=True, stop=True, tp=(0, hh * 64))
            st2d = P.done("t", "te")
            P.wait("v", "te", st2d)
            if c == 0:
                P.v(lambda e: e.tensor_copy(out=s2run[:, 512:1024], in_=pf[5][:, :]))
            else:
                P.v(lambda e: e.tensor_tensor(out=s2run[:, 512:1024],
                                              in0=s2run[:, 512:1024],
                                              in1=pf[5][:, :], op=ALU.add))
            P.done("v", "ve")
            if c > 0:
                P.wait("y", "ve", bstate["yadd"])
                ydma(o2row[:, :], o2e[127:128, :], "dys")
            rowd = P.mark("dys")
            P.wait("v", "te", o2done)
            P.wait("v", "dys", rowd)
            P.v(lambda e: e.tensor_copy(out=o2e[:, 0:512], in_=pf[2][:, :]))
            P.v(lambda e: e.tensor_copy(out=o2e[:, 512:1024], in_=pf[3][:, :]))
            o2ed = P.done("v", "ve")
            bstate["o2ed"] = o2ed
            P.wait("t", "ve", o2ed)
            for half in range(2):
                mm(pf[4 + half][:, :], s_dn[:, :],
                   o2e[:, half * 512:(half + 1) * 512], start=True, stop=True)
            shmm = P.done("t", "te")
            P.wait("v", "te", shmm)
            for half in range(2):
                P.v(lambda e, c=c, half=half: e.tensor_tensor(
                    out=y_tm[:, c * D + half * 512:c * D + (half + 1) * 512],
                    in0=y_tm[:, c * D + half * 512:c * D + (half + 1) * 512],
                    in1=pf[4 + half][:, :], op=ALU.add))
            if c > 0:
                P.v(lambda e: e.drain())
                for half in range(2):
                    P.v(lambda e, c=c, half=half: e.tensor_tensor(
                        out=y_tm[0:1, c * D + half * 512:c * D + (half + 1) * 512],
                        in0=y_tm[0:1, c * D + half * 512:c * D + (half + 1) * 512],
                        in1=o2row[0:1, half * 512:(half + 1) * 512], op=ALU.add))
            bstate["yadd"] = P.done("v", "ve")

    P.wait("v", "te")
    for dc in range(NDC):
        P.v(lambda e, dc=dc: e.tensor_copy(
            out=z_m[:, dc:dc + 1],
            in_=q2t[:, dc * TILE + TILE - 1:dc * TILE + TILE]))
    vd = P.done("v", "ve")
    P.wait("y", "ve", vd)
    ydma(cc2i[:, 0:512], s2run[:, 512:1024])
    ydma(cc2i[:, 512:520], z_m[:, :])
    P.wait("g", "dy")
    P.inc("g", lambda e: e.collective_compute(
        "AllGather", ALU.bypass, replica_groups=GROUPS,
        ins=[cc2i[:, :]], outs=[cc2o[:, :]]), "cc", 1)
    ccend = P.mark("cc")
    P.wait("v", "ve")
    P.v(lambda e: e.memset(x_tm[:, 0:520], 0.0))
    P.done("v", "ve")
    for r in range(4):
        P.wait("g", "cc", ccend)
        P.wait("g", "ve")
        ld = gdma(sq_sb[:, 512:1024], cc2o[r * 128:(r + 1) * 128, 0:512], "dgc")
        ld = gdma(z_rep[:, :], cc2o[r * 128:(r + 1) * 128, 512:520], "dgc")
        P.wait("v", "dgc", ld)
        P.v(lambda e, r=r: e.scalar_tensor_tensor(
            out=x_tm[:, 0:512], in0=sq_sb[:, 512:1024],
            scalar=pmbc[:, r:r + 1], in1=x_tm[:, 0:512],
            op0=ALU.mult, op1=ALU.add))
        P.v(lambda e, r=r: e.scalar_tensor_tensor(
            out=x_tm[:, 512:520], in0=z_rep[:, :],
            scalar=vmbc[:, r:r + 1], in1=x_tm[:, 512:520],
            op0=ALU.mult, op1=ALU.add))
        P.done("v", "ve")
    P.v(lambda e: e.tensor_copy(out=s2in[:, :], in_=x_tm[:, 0:512]))
    P.v(lambda e: e.tensor_copy(out=vprev[:, :], in_=x_tm[:, 512:520]))
    P.done("v", "ve")

    if debug:
        for c in range(NCH):
            P.wait("v", "dbo")
            P.v(lambda e, c=c: e.tensor_copy(out=outf_dbg[:, :],
                                             in_=y_tm[:, c * D:(c + 1) * D]))
            dv = P.done("v", "ve")
            P.wait("y", "ve", dv)
            ydma(dbg["dbg_y"][c * 128:(c + 1) * 128, :], outf_dbg[:, :], "dbo")

    P.wait("s", "ve")
    sdma(bass.AP(cpw_sb, 0, [[cpw_sb.shape[1], 128], [D, NDC], [1, D]]),
         bass.AP(wcp_bf, 0, [[D, 128], [128 * D, NDC], [1, D]]))
    P.wait("t", "ve")
    for hh in range(2):
        bank = pf[4 + hh]
        for hp in range(NP):
            mm(bank[0:1, hp * 64:(hp + 1) * 64],
               vprev[hh * 64:(hh + 1) * 64, hp:hp + 1],
               s2in[hh * 64:(hh + 1) * 64, hp * 64:(hp + 1) * 64],
               start=True, stop=True, tp=(hh * 64, 0))
    tdone = P.done("t", "te")
    P.wait("v", "te", tdone)
    for hh in range(2):
        bank = pf[4 + hh]
        P.v(lambda e, hh=hh, bank=bank: e.tensor_tensor(
            out=bass.AP(y_tm, hh * 64, [[y_tm.shape[1], 1], [128, 8], [1, 64]]),
            in0=bass.AP(y_tm, hh * 64, [[y_tm.shape[1], 1], [128, 8], [1, 64]]),
            in1=bank[0:1, 0:512].rearrange("p (a b) -> p a b", a=8), op=ALU.add))
        P.v(lambda e: e.drain())
    P.done("v", "ve")
    P.wait("t", "ve")

    for tl in range(NT):
        stream_proj(wq_bf.ap(), bq_h, lambda dc: qt[:, dc * TILE:(dc + 1) * TILE],
                    tilebase=tl)
        P.wait("v", "ac")
        for dc in range(NDC):
            if tl == 0:
                P.v(lambda e, dc=dc: e.memset(
                    q2s[:, dc * (TILE + 1):dc * (TILE + 1) + 1], 0.0))
            else:
                P.v(lambda e, dc=dc: e.tensor_copy(
                    out=q2s[:, dc * (TILE + 1):dc * (TILE + 1) + 1],
                    in_=q2s[:, dc * (TILE + 1) + TILE:dc * (TILE + 1) + TILE + 1]))
        P.v(lambda e: e.drain())
        P.v(lambda e: e.tensor_scalar(out=q2t[:, :], in0=qt[:, 0:NDC * TILE], scalar1=0.0,
                                      scalar2=0.98 / 8.0, op0=ALU.max, op1=ALU.mult))
        P.v(lambda e: e.drain())
        for dc in range(NDC):
            P.v(lambda e, dc=dc: e.scalar_tensor_tensor(
                out=q2s[:, dc * (TILE + 1) + 1:dc * (TILE + 1) + 1 + TILE],
                in0=qt[:, dc * TILE:(dc + 1) * TILE], scalar=0.125,
                in1=q2t[:, dc * TILE:(dc + 1) * TILE],
                op0=ALU.mult, op1=ALU.subtract))
        q2d = P.done("v", "ve")
        P.wait("t", "ve", q2d)
        P.wait("t", "ds")
        for cc_i in range(CPT):
            c = tl * CPT + cc_i
            for hh in range(2):
                bank = pf[2 + hh]
                for hp in range(NP):
                    mm(bank[:, hp * 64:(hp + 1) * 64],
                       q2s[hh * 64:(hh + 1) * 64,
                           hp * (TILE + 1) + cc_i * 128:
                           hp * (TILE + 1) + cc_i * 128 + 128],
                       s2in[hh * 64:(hh + 1) * 64, hp * 64:(hp + 1) * 64],
                       start=True, stop=True, tp=(hh * 64, 0))
            i2done = P.done("t", "te")
            P.wait("v", "te", i2done)
            for hh in range(2):
                bank = pf[2 + hh]
                P.v(lambda e, c=c, hh=hh, bank=bank: e.tensor_tensor(
                    out=bass.AP(y_tm, c * D + hh * 64,
                                [[y_tm.shape[1], 128], [128, 8], [1, 64]]),
                    in0=bass.AP(y_tm, c * D + hh * 64,
                                [[y_tm.shape[1], 128], [128, 8], [1, 64]]),
                    in1=bank[:, 0:512].rearrange("p (a b) -> p a b", a=8),
                    op=ALU.add))
                P.v(lambda e: e.drain())
            yd = P.done("v", "ve")
            P.wait("t", "ve", yd)
            for dc in range(NDC):
                P.t(lambda e, c=c, dc=dc: e.transpose(
                    pb[0][:, dc * 128:(dc + 1) * 128],
                    y_tm[:, c * D + dc * 128:c * D + (dc + 1) * 128], id_bf[:, :]))
            tdone = P.done("t", "te")
            P.wait("v", "te", tdone)
            P.v(lambda e: e.tensor_copy(out=yf_sb[:, :], in_=pb[0][:, :]))
            yfd = P.done("v", "ve")
            P.wait("y", "ve", yfd)
            ydma(x_tm[:, :], x_h[c * 128:(c + 1) * 128, :])
            xld = P.mark("dy")
            P.wait("t", "ve", yfd)
            for dc in range(NDC):
                for dg in range(2):
                    mm(pf[dg][:, :], yf_sb[:, dc * 128:(dc + 1) * 128],
                       cpw_sb[:, dc * D + dg * 512:dc * D + (dg + 1) * 512],
                       start=(dc == 0), stop=False)
            for dg in range(2):
                mm(pf[dg][:, :], ones_bf[:, 0:128],
                   bcp_sb[:, dg * 512:(dg + 1) * 512], start=False, stop=True)
            cpd = P.done("t", "te")
            P.wait("v", "te", cpd)
            P.wait("v", "dy", xld)
            for dg in range(2):
                P.v(lambda e, c=c, dg=dg: e.tensor_tensor(
                    out=outacc[:, c * D + dg * 512:c * D + (dg + 1) * 512],
                    in0=pf[dg][:, :],
                    in1=x_tm[:, dg * 512:(dg + 1) * 512], op=ALU.add))
            vd2 = P.done("v", "ve")
            P.wait("t", "ve", vd2)
            P.wait("y", "ve", vd2)

    if debug:
        for c in range(NCH):
            P.wait("v", "dbo")
            P.v(lambda e, c=c: e.tensor_copy(out=outf_dbg[:, :],
                                             in_=outacc[:, c * D:(c + 1) * D]))
            dv = P.done("v", "ve")
            P.wait("y", "ve", dv)
            ydma(dbg["dbg_attn"][c * 128:(c + 1) * 128, :], outf_dbg[:, :], "dbo")

    for c in range(NCH):
        P.wait("t", "ve")
        P.wait("t", "ac")
        for dinc in range(NDC):
            mm(pf[4][0:8, 0:128], wr_sb[:, dinc * 8:(dinc + 1) * 8],
               xn_fm[:, dinc * HWs + 3 + c * 128:dinc * HWs + 3 + (c + 1) * 128],
               start=(dinc == 0), stop=(dinc == NDC - 1))
        tdone = P.done("t", "te")
        P.wait("v", "te", tdone)
        P.v(lambda e: e.tensor_copy(out=sc_fm[:, :], in_=pf[4][0:8, 0:128]))
        scd = P.done("v", "ve")
        P.wait("t", "ve", scd)
        P.t(lambda e: e.transpose(pf[4][:, 0:8], sc_fm[:, :], id_f32[0:8, 0:8]))
        td2 = P.done("t", "te")
        P.wait("s", "te", td2)
        P.s(lambda e: e.activation(out=z_t[:, :], in_=pf[4][:, 0:8], func=AF.Sigmoid))
        sd2 = P.done("s", "ac")
        P.wait("t", "ac", sd2)
        P.wait("v", "ac", sd2)
        P.v(lambda e: e.tensor_tensor(out=z_m[:, :], in0=z_t[:, :],
                                      in1=biasbc[:, :], op=ALU.add))
        P.v(lambda e: e.drain())
        P.v(lambda e: e.max(out=z_srt[:, :], in_=z_m[:, :]))
        P.v(lambda e: e.memset(z_rep[:, :], -3e38))
        P.v(lambda e: e.drain())
        P.v(lambda e: e.tensor_copy(out=z_rep[:, 0:2], in_=z_srt[:, 0:2]))
        P.v(lambda e: e.drain())
        P.v(lambda e: e.match_replace(out=z_srt[:, :], in_to_replace=z_rep[:, :],
                                      in_values=z_m[:, :], imm_value=-3e38))
        P.v(lambda e: e.drain())
        P.v(lambda e: e.tensor_tensor(out=z_rep[:, :], in0=z_m[:, :],
                                      in1=z_srt[:, :], op=ALU.not_equal))
        P.v(lambda e: e.drain())
        P.v(lambda e: e.tensor_tensor(out=z_rep[:, :], in0=z_rep[:, :],
                                      in1=z_t[:, :], op=ALU.mult))
        P.v(lambda e: e.drain())
        P.v(lambda e: e.reduce_sum(out=z_m[:, 0:1], in_=z_rep[:, :], axis=AX.X))
        P.v(lambda e: e.drain())
        P.v(lambda e: e.tensor_scalar_add(out=z_m[:, 0:1], in0=z_m[:, 0:1],
                                          scalar1=1e-9))
        P.v(lambda e: e.drain())
        P.v(lambda e: e.reciprocal(out=z_m[:, 1:2], in_=z_m[:, 0:1]))
        P.v(lambda e: e.drain())
        P.v(lambda e, c=c: e.tensor_scalar_mul(out=cw_sb[:, c * 8:(c + 1) * 8],
                                               in0=z_rep[:, :], scalar1=z_m[:, 1:2]))
        P.v(lambda e: e.drain())
        P.v(lambda e, c=c: e.tensor_tensor(out=cw46[:, c:c + 1],
                                           in0=cw_sb[:, c * 8 + 4:c * 8 + 5],
                                           in1=cw_sb[:, c * 8 + 6:c * 8 + 7],
                                           op=ALU.add))
        P.v(lambda e, c=c: e.tensor_copy(out=cw7[:, c:c + 1],
                                         in_=cw_sb[:, c * 8 + 7:c * 8 + 8]))
        P.done("v", "ve")
    if debug:
        for c in range(NCH):
            P.wait("v", "dbo")
            P.v(lambda e, c=c: e.tensor_copy(out=outf_dbg[:, 0:8],
                                             in_=cw_sb[:, c * 8:(c + 1) * 8]))
            dv = P.done("v", "ve")
            P.wait("y", "ve", dv)
            ydma(dbg["dbg_cw"][c * 128:(c + 1) * 128, :], outf_dbg[:, 0:8], "dbo")

    NW = T // CW
    for dc in range(NDC):
        base = dc * HWs
        for tw in range(NW):
            P.wait("v", "ve")
            P.v(lambda e, base=base, dc=dc, tw=tw: e.tensor_scalar(
                out=conv_fm[:, 0:CW],
                in0=xn_fm[:, base + tw * CW:base + (tw + 1) * CW],
                scalar1=cvw_sb[:, dc * 4:dc * 4 + 1], scalar2=cvb_sb[:, dc:dc + 1],
                op0=ALU.mult, op1=ALU.add))
            for k in range(1, 4):
                P.v(lambda e: e.drain())
                P.v(lambda e, base=base, dc=dc, k=k, tw=tw: e.scalar_tensor_tensor(
                    out=conv_fm[:, 0:CW],
                    in0=xn_fm[:, base + tw * CW + k:base + (tw + 1) * CW + k],
                    scalar=cvw_sb[:, dc * 4 + k:dc * 4 + k + 1],
                    in1=conv_fm[:, 0:CW], op0=ALU.mult, op1=ALU.add))
            cvd = P.done("v", "ve")
            P.wait("t", "ve", cvd)
            for c4 in range(CW // 128):
                c = tw * (CW // 128) + c4
                par = c4 % 2
                P.wait("t", "ve")
                P.t(lambda e, c4=c4, par=par: e.transpose(
                    pb[par][:, 0:128], conv_fm[:, c4 * 128:(c4 + 1) * 128],
                    id_bf[:, :]))
                td3 = P.done("t", "te")
                P.wait("v", "te", td3)
                P.v(lambda e, c=c, dc=dc, par=par: e.tensor_tensor(
                    out=outacc[:, c * D + dc * 128:c * D + (dc + 1) * 128],
                    in0=outacc[:, c * D + dc * 128:c * D + (dc + 1) * 128],
                    in1=pb[par][:, 0:128], op=ALU.add))
                P.done("v", "ve")

    P.wait("v", "ve")
    for c in range(NCH):
        P.v(lambda e, c=c: e.scalar_tensor_tensor(
            out=outacc[:, c * D:(c + 1) * D], in0=xn_tm[:, c * D:(c + 1) * D],
            scalar=cw46[:, c:c + 1], in1=outacc[:, c * D:(c + 1) * D],
            op0=ALU.mult, op1=ALU.add))
        P.v(lambda e: e.drain())
        P.v(lambda e, c=c: e.tensor_scalar(
            out=e_sb[:, :], in0=xn_tm[:, c * D:(c + 1) * D],
            scalar1=0.0, scalar2=cw7[:, c:c + 1], op0=ALU.max, op1=ALU.mult))
        P.v(lambda e: e.drain())
        P.v(lambda e, c=c: e.tensor_tensor(
            out=outacc[:, c * D:(c + 1) * D], in0=outacc[:, c * D:(c + 1) * D],
            in1=e_sb[:, :], op=ALU.add))
        P.v(lambda e: e.drain())
    P.done("v", "ve")

    P.wait("s", "dws")
    estate = {"hidfree": P.mark("ve")}
    for tl in range(NT):
        for ee in range(NE):
            P.wait("s", "te")
            sdma(bass.AP(wdbuf, 0, [[wdbuf.shape[1], 128], [D, 4], [1, D]]),
                 bass.AP(wd_bf, ee * I_FFN * D,
                         [[D, 128], [128 * D, 4], [1, D]]))
            WB = wbig.shape[1]
            gqm = {}
            gqds = {}

            def gq_load(gq):
                half, quarter = gq // 2, gq % 2
                par = gq % 2
                if gq >= 2:
                    P.wait("s", "te", gqm[gq - 2])
                goff = (ee * D + quarter * 4 * 128) * I_FFN + half * 256
                for ic2 in range(2):
                    sdma(bass.AP(wbig, par * 2048 + ic2 * 128,
                                 [[WB, 128], [512, 4], [1, 128]]),
                         bass.AP(wg_bf, goff + ic2 * 128,
                                 [[I_FFN, 128], [128 * I_FFN, 4], [1, 128]]),
                         "dsa" if par == 0 else "dsb")
                    sdma(bass.AP(wbig, par * 2048 + 256 + ic2 * 128,
                                 [[WB, 128], [512, 4], [1, 128]]),
                         bass.AP(wu_bf, goff + ic2 * 128,
                                 [[I_FFN, 128], [128 * I_FFN, 4], [1, 128]]),
                         "dsa" if par == 0 else "dsb")
                gqds[gq] = P.mark("dsa" if par == 0 else "dsb")

            def gq_mm(gq):
                half, quarter = gq // 2, gq % 2
                par = gq % 2
                if quarter == 0:
                    P.wait("t", "ve", estate["hidfree"])
                P.wait("t", "dsa" if par == 0 else "dsb", gqds[gq])
                for d4 in range(4):
                    dinc = quarter * 4 + d4
                    xs = xn_fm[:, dinc * HWs + 3 + tl * TILE:
                               dinc * HWs + 3 + (tl + 1) * TILE]
                    for ic2 in range(2):
                        b0 = par * 2048 + d4 * 512 + ic2 * 128
                        mm(pf[ic2 * 2][:, 0:TILE], wbig[:, b0:b0 + 128], xs,
                           start=(dinc == 0), stop=(dinc == NDC - 1))
                        mm(pf[ic2 * 2 + 1][:, 0:TILE],
                           wbig[:, b0 + 256:b0 + 384], xs,
                           start=(dinc == 0), stop=(dinc == NDC - 1))
                gqm[gq] = P.done("t", "te")

            for half in range(2):
                gq_load(half * 2)
                gq_load(half * 2 + 1)
                gq_mm(half * 2)
                gq_mm(half * 2 + 1)
                gud = P.mark("te")
                P.wait("s", "te", gud)
                for ic2 in range(2):
                    P.s(lambda e, ic2=ic2: e.activation(
                        out=sg_sb[:, ic2 * 512:ic2 * 512 + TILE],
                        in_=pf[ic2 * 2][:, 0:TILE], func=AF.Silu))
                sgd = P.done("s", "ac")
                P.wait("v", "ac", sgd)
                for ic2 in range(2):
                    ic = half * 2 + ic2
                    P.v(lambda e, ic2=ic2, ic=ic: e.tensor_tensor(
                        out=hid[:, ic * TILE:(ic + 1) * TILE],
                        in0=sg_sb[:, ic2 * 512:ic2 * 512 + TILE],
                        in1=pf[ic2 * 2 + 1][:, 0:TILE], op=ALU.mult))
                estate["hidfree"] = P.done("v", "ve")
            hdall = P.mark("ve")
            P.wait("t", "ve", hdall)
            P.wait("t", "ds")
            for cc_i in range(CPT):
                c = tl * CPT + cc_i
                for ic in range(4):
                    mm(pf[4][:, :],
                       hid[:, ic * TILE + cc_i * 128:ic * TILE + (cc_i + 1) * 128],
                       wdbuf[:, ic * D:ic * D + 512],
                       start=(ic == 0), stop=(ic == 3))
                    mm(pf[5][:, :],
                       hid[:, ic * TILE + cc_i * 128:ic * TILE + (cc_i + 1) * 128],
                       wdbuf[:, ic * D + 512:(ic + 1) * D],
                       start=(ic == 0), stop=(ic == 3))
                wdd = P.done("t", "te")
                P.wait("v", "te", wdd)
                for dg in range(2):
                    bk = pf[4 + dg]
                    P.v(lambda e, c=c, dg=dg, bk=bk, ee=ee: e.scalar_tensor_tensor(
                        out=outacc[:, c * D + dg * 512:c * D + (dg + 1) * 512],
                        in0=bk[:, :], scalar=cw_sb[:, c * 8 + ee:c * 8 + ee + 1],
                        in1=outacc[:, c * D + dg * 512:c * D + (dg + 1) * 512],
                        op0=ALU.mult, op1=ALU.add))
                vdd = P.done("v", "ve")
                P.wait("t", "ve", vdd)

    for c in range(NCH):
        P.wait("v", "dout")
        P.v(lambda e, c=c: e.tensor_copy(out=outf[:, :],
                                         in_=outacc[:, c * D:(c + 1) * D]))
        fd = P.done("v", "ve")
        P.wait("y", "ve", fd)
        ydma(out_h[c * 128:(c + 1) * 128, :], outf[:, :], "dout")
    P.wait("y", "dout")
    if debug:
        P.wait("y", "dbo")

    P.replay()
    return nc


def kernel(**inputs):
    T_FULL = inputs["L0_attn_x"].shape[1]
    B = inputs["L0_attn_x"].shape[0]
    ncore = 8
    per_b = ncore // B
    T = T_FULL // per_b
    x = np.asarray(inputs["L0_attn_x"], np.float32)

    wmap = {
        "q1_w": "L0_attn_q1_w", "q1_b": "L0_attn_q1_b",
        "k1_w": "L0_attn_k1_w", "k1_b": "L0_attn_k1_b",
        "k2_w": "L0_attn_k2_w", "k2_b": "L0_attn_k2_b",
        "gw_w": "L0_attn_gw_w", "gw_b": "L0_attn_gw_b",
        "sw_w": "L0_attn_sw_w", "sw_b": "L0_attn_sw_b",
        "cp_w": "L0_attn_cp_w", "cp_b": "L0_attn_cp_b",
        "Wr": "L0_ffn_Wr", "router_bias": "L0_ffn_router_bias",
        "Wg": "L0_ffn_Wg", "Wu": "L0_ffn_Wu", "Wd": "L0_ffn_Wd",
        "conv_w": "L0_ffn_conv_w", "conv_b": "L0_ffn_conv_b",
    }
    weights = {k: np.ascontiguousarray(np.asarray(inputs[v], np.float32))
               for k, v in wmap.items()}

    in_maps = []
    for core in range(ncore):
        b = core // per_b
        q = core % per_b
        a0 = q * T
        xs = np.ascontiguousarray(x[b, a0:a0 + T, :])
        halo = np.zeros((4, x.shape[2]), np.float32)
        if a0 > 0:
            halo[0:3, :] = x[b, a0 - 3:a0, :]
        if a0 + T < T_FULL:
            halo[3, :] = x[b, a0 + T, :]
        pm = np.zeros((4, 1), np.float32)
        pm[:q, 0] = 1.0
        vm = np.zeros((4, 1), np.float32)
        if q > 0:
            vm[q - 1, 0] = 1.0
        m = {"x": xs, "xhalo": halo, "pmask": pm, "vmask": vm}
        m.update(weights)
        in_maps.append(m)

    nc = build(T, debug=False)
    res = run_bass_kernel_spmd(nc, in_maps, core_ids=list(range(ncore)))
    out = np.zeros((B, T_FULL, x.shape[2]), np.float32)
    for core in range(ncore):
        b = core // per_b
        q = core % per_b
        out[b, q * T:(q + 1) * T, :] = res.results[core]["out"]
    return out


# version 12
# speedup vs baseline: 1.1436x; 1.1436x over previous
import numpy as np
import concourse.bass as bass
import concourse.mybir as mybir
from concourse.bass_utils import run_bass_kernel_spmd

F32 = mybir.dt.float32
BF16 = mybir.dt.bfloat16
AF = mybir.ActivationFunctionType
ALU = mybir.AluOpType
AX = mybir.AxisListType

D = 1024
H = 16
DH = 64
NDC = D // 128
NP = H // 2
I_FFN = 512
NE = 4
CH = 128


class Rec:

    def __init__(self, nc):
        self.nc = nc
        self.ops = {"t": [], "v": [], "s": [], "g": [], "y": []}
        self.sems = {}
        self.cnt = {}
        self.kind = {"t": None, "v": None, "s": None, "g": None, "y": None}
        self.guard = {"t": None, "v": None, "s": None, "g": None, "y": None}

    def sem(self, name):
        if name not in self.sems:
            self.sems[name] = self.nc.alloc_semaphore(name)
            self.cnt[name] = 0
        return self.sems[name]

    def _op(self, eng, fn, kind="op"):
        if kind != "wait" and self.guard[eng] is not None:
            h, v = self.guard[eng]
            self.guard[eng] = None
            self.ops[eng].append(lambda e: e.wait_ge(h, v))
        else:
            self.guard[eng] = None
        self.ops[eng].append(fn)
        self.kind[eng] = kind

    def t(self, fn): self._op("t", fn)
    def v(self, fn): self._op("v", fn)
    def s(self, fn): self._op("s", fn)
    def g(self, fn): self._op("g", fn)
    def y(self, fn): self._op("y", fn)

    def inc(self, eng, fn, sem, n=1):
        h = self.sem(sem)
        self._op(eng, lambda e: fn(e).then_inc(h, n), "inc")
        self.cnt[sem] += n
        return self.cnt[sem]

    def done(self, eng, sem):
        h = self.sem(sem)
        self.cnt[sem] += 1
        if self.kind[eng] == "op":
            f = self.ops[eng][-1]
            self.ops[eng][-1] = (lambda e, f=f: f(e).then_inc(h, 1))
        else:
            self._op(eng, lambda e: e.drain().then_inc(h, 1))
        self.kind[eng] = "inc"
        self.guard[eng] = (h, self.cnt[sem])
        return self.cnt[sem]

    def wait(self, eng, sem, val=None):
        h = self.sem(sem)
        v = self.cnt[sem] if val is None else val
        self._op(eng, lambda e: e.wait_ge(h, v), "wait")

    def mark(self, sem):
        self.sem(sem)
        return self.cnt[sem]

    def replay(self):
        with self.nc.Block() as block:
            block.tensor(lambda e: [f(e) for f in self.ops["t"]])
            block.vector(lambda e: [f(e) for f in self.ops["v"]])
            block.scalar(lambda e: [f(e) for f in self.ops["s"]])
            block.gpsimd(lambda e: [f(e) for f in self.ops["g"]])
            block.sync(lambda e: [f(e) for f in self.ops["y"]])


def build(T, debug=False):
    NCH = T // CH
    TILE = min(512, T)
    NT = T // TILE
    CPT = TILE // CH
    HWs = T + 4
    nc = bass.Bass("TRN2", target_bir_lowering=False)

    def din(name, shape):
        return nc.dram_tensor(name, list(shape), F32, kind="ExternalInput").ap()

    x_h = din("x", (T, D))
    xhalo_h = din("xhalo", (4, D))
    pmask_h = din("pmask", (4, 1))
    vmask_h = din("vmask", (4, 1))
    wq_h = din("q1_w", (D, D)); bq_h = din("q1_b", (D,))
    wk_h = din("k1_w", (D, D)); bk_h = din("k1_b", (D,))
    wk2_h = din("k2_w", (D, D)); bk2_h = din("k2_b", (D,))
    gw_h = din("gw_w", (DH, 1)); gwb_h = din("gw_b", (1,))
    sw_h = din("sw_w", (DH, 1)); swb_h = din("sw_b", (1,))
    wcp_h = din("cp_w", (D, D)); bcp_h = din("cp_b", (D,))
    wr_h = din("Wr", (D, 8)); rb_h = din("router_bias", (8,))
    wg_h = din("Wg", (NE, D, I_FFN))
    wu_h = din("Wu", (NE, D, I_FFN))
    wd_h = din("Wd", (NE, I_FFN, D))
    cvw_h = din("conv_w", (4, 1, D)); cvb_h = din("conv_b", (D,))

    out_h = nc.dram_tensor("out", [T, D], F32, kind="ExternalOutput").ap()
    dbg = {}
    if debug:
        for nm, shape in [("dbg_xn", (T, D)), ("dbg_gc", (H, T)),
                          ("dbg_y", (T, D)), ("dbg_attn", (T, D)),
                          ("dbg_cw", (T, 8))]:
            dbg[nm] = nc.dram_tensor(nm, list(shape), F32, kind="ExternalOutput").ap()

    wq_bf = nc.dram_tensor("wq_bf", [D, D], BF16)
    wk_bf = nc.dram_tensor("wk_bf", [D, D], BF16)
    wk2_bf = nc.dram_tensor("wk2_bf", [D, D], BF16)
    wcp_bf = nc.dram_tensor("wcp_bf", [D, D], BF16)
    wg_bf = nc.dram_tensor("wg_bf", [NE, D, I_FFN], BF16)
    wu_bf = nc.dram_tensor("wu_bf", [NE, D, I_FFN], BF16)
    wd_bf = nc.dram_tensor("wd_bf", [NE, I_FFN, D], BF16)
    cc0i = nc.dram_tensor("cc0i", [H, 1], F32)
    cc0o = nc.dram_tensor("cc0o", [4 * H, 1], F32)
    cc1i = nc.dram_tensor("cc1i", [128, 512], F32)
    cc1o = nc.dram_tensor("cc1o", [512, 512], F32)
    cc2i = nc.dram_tensor("cc2i", [128, 520], F32)
    cc2o = nc.dram_tensor("cc2o", [512, 520], F32)
    GROUPS = [[0, 1, 2, 3], [4, 5, 6, 7]]

    SB = nc.alloc_sbuf_tensor
    PS = nc.alloc_psum_tensor
    xn_fm = SB("xn_fm", [128, NDC * HWs], BF16)
    xn_tm = SB("xn_tm", [128, NCH * D], BF16)
    y_tm = SB("y_tm", [128, max(NCH * D, 8192)], BF16)
    outacc = y_tm
    s1p = SB("s1p", [128, max(NCH * 512, NDC * D)], BF16)
    cpw_sb = s1p

    lgg = SB("lgg", [16, T], F32)
    rra = SB("rra", [16, T], BF16)
    gc_tm = SB("gc_tm", [128, NCH * 16], F32)
    a_tm = SB("a_tm", [128, NCH * 16], F32)

    id_bf = SB("id_bf", [128, 128], BF16)
    id_f32 = SB("id_f32", [128, 128], F32)
    tri_f32 = SB("tri_f32", [128, 128], F32)
    ones_bf = SB("ones_bf", [1, 512], BF16)
    ones_f32 = SB("ones_f32", [1, 128], F32)
    gwblk = SB("gwblk", [128, NDC * 16], BF16)
    swblk = SB("swblk", [128, NDC * 16], BF16)
    gwb_tmp = SB("gwb_tmp", [1, 2], BF16)
    gwb16 = SB("gwb16", [1, 16], BF16)
    swb16 = SB("swb16", [1, 16], BF16)
    bias_sb = SB("bias_sb", [1, D], BF16)
    bcp_sb = SB("bcp_sb", [1, D], BF16)
    rb_sb = SB("rb_sb", [1, 8], F32)
    cvb_sb = SB("cvb_sb", [128, 8], F32)
    cvw_sb = SB("cvw_sb", [128, 8 * 4], F32)
    pm_sb = SB("pm_sb", [4, 1], F32)
    pmT = SB("pmT", [1, 4], F32)
    vmT = SB("vmT", [1, 4], F32)
    pmbc = SB("pmbc", [128, 4], F32)
    vmbc = SB("vmbc", [128, 4], F32)
    xh_tm = SB("xh_tm", [4, D], BF16)
    gsum_blk = SB("gsum_blk", [4, 16], F32)
    off_col = SB("off_col", [16, 1], F32)

    s1in = SB("s1in", [128, 512], BF16)
    s2in = SB("s2in", [128, 512], BF16)
    vprev = SB("vprev", [128, 8], BF16)
    s1tot = SB("s1tot", [128, 512], BF16)
    s2tot = SB("s2tot", [128, 512], BF16)
    ccg = SB("ccg", [128, 520], F32)
    t2pack = ccg

    kt = SB("kt", [128, max(NDC * TILE, 4 * D, NDC * (TILE + 1))], BF16)
    qt = SB("qt", [128, max(NDC * TILE, 4096)], BF16)
    k2t = SB("k2t", [128, NDC * TILE], BF16)
    q2t = SB("q2t", [128, NDC * TILE], BF16)
    q2s = kt
    p_sb = SB("p_sb", [128, H * 128], BF16)
    av_sb = SB("av_sb", [128, D], BF16)
    ktm2 = SB("ktm2", [128, D], BF16)
    vs_sb = SB("vs_sb", [128, D], BF16)
    e_sb = SB("e_sb", [128, D], BF16)
    o2e = SB("o2e", [128, D], BF16)
    o2s = SB("o2s", [128, D], BF16)
    yf_sb = SB("yf_sb", [128, D], BF16)
    x_tm = SB("x_tm", [128, D], F32)
    stats = SB("stats", [128, 8], F32)
    sq_sb = SB("sq_sb", [128, D], F32)
    rgc_t = SB("rgc_t", [128, 16], F32)
    rr_t = SB("rr_t", [128, 16], F32)
    wr_sb = SB("wr_sb", [128, NDC * 8], BF16)
    biasbc = SB("biasbc", [128, 8], F32)
    sc_fm = SB("sc_fm", [8, 128], F32)
    z_t = SB("z_t", [128, 8], F32)
    z_srt = SB("z_srt", [128, 8], F32)
    z_m = SB("z_m", [128, 8], F32)
    z_rep = SB("z_rep", [128, 8], F32)
    cw_sb = SB("cw_sb", [128, NCH * 8], F32)
    cw46 = SB("cw46", [128, NCH], F32)
    cw7 = SB("cw7", [128, NCH], F32)
    wbuf = SB("wbuf", [128, 2 * NDC * 128], BF16)
    wbig = qt
    wdbuf = kt
    sg_sb = k2t
    hid = q2t
    CW = min(512, T)
    conv_fm = None
    outf = None

    conv_fm = vs_sb
    outf = sq_sb
    outf_dbg = SB("outf_dbg", [128, D], F32) if debug else None
    s1run = sq_sb
    s2run = sq_sb
    pf = [PS(f"pf{i}", [128, 512], F32) for i in range(6)]
    pb = [PS(f"pb{i}", [128, 1024], BF16) for i in range(2)]

    P = Rec(nc)

    def gdma(dst, src, sem="dg"):
        return P.inc("g", lambda e: e.dma_start(out=dst, in_=src), sem, 16)

    def ydma(dst, src, sem="dy"):
        return P.inc("y", lambda e: e.dma_start(out=dst, in_=src), sem, 16)

    def sdma(dst, src, sem="ds"):
        return P.inc("s", lambda e: e.dma_start(out=dst, in_=src), sem, 16)

    def mm(out, lhsT, rhs, start, stop, tp=None):
        P.t(lambda e: e.matmul(out, lhsT, rhs, start=start, stop=stop,
                               tile_position=tp))

    def mkconst(e):
        e.memset(id_bf[:, :], 0.0)
        e.memset(id_f32[:, :], 0.0)
        e.memset(tri_f32[:, :], 0.0)
        e.memset(gwblk[:, :], 0.0)
        e.memset(swblk[:, :], 0.0)
        e.drain()
        e.affine_select(out=id_bf[:, :], in_=id_bf[:, :], compare_op=ALU.not_equal,
                        fill=1.0, base=0, pattern=[[-1, 128]], channel_multiplier=1)
        e.affine_select(out=id_f32[:, :], in_=id_f32[:, :], compare_op=ALU.not_equal,
                        fill=1.0, base=0, pattern=[[-1, 128]], channel_multiplier=1)
        return e.affine_select(out=tri_f32[:, :], in_=tri_f32[:, :],
                                compare_op=ALU.is_gt, fill=1.0, base=0,
                                pattern=[[-1, 128]], channel_multiplier=1)
    P.g(mkconst)
    P.done("g", "pl")

    P.v(lambda e: e.memset(ones_bf[:, :], 1.0))
    P.v(lambda e: e.memset(ones_f32[:, :], 1.0))
    P.done("v", "ve")

    gdma(bcp_sb[:, :], bcp_h.rearrange("(a b) -> a b", a=1))
    gdma(gwb_tmp[:, 0:1], gwb_h.rearrange("(a b) -> a b", a=1))
    gdma(gwb_tmp[:, 1:2], swb_h.rearrange("(a b) -> a b", a=1))
    ydma(rb_sb[:, :], rb_h.rearrange("(a b) -> a b", a=1))
    ydma(pm_sb[:, :], pmask_h)
    ydma(pmT[:, :], pmask_h.rearrange("a b -> b a"))
    ydma(vmT[:, :], vmask_h.rearrange("a b -> b a"))
    for dc in range(NDC):
        ydma(cvb_sb[:, dc:dc + 1],
             cvb_h.rearrange("(a b) -> a b", a=NDC)[dc, :].unsqueeze(1))
        for k in range(4):
            ydma(cvw_sb[:, dc * 4 + k:dc * 4 + k + 1],
                 cvw_h[k, 0, dc * 128:(dc + 1) * 128].unsqueeze(1))
    P.wait("g", "pl")
    for dc in range(NDC):
        gdma(gwblk[0:64, dc * 16 + 2 * dc:dc * 16 + 2 * dc + 1], gw_h)
        gdma(gwblk[64:128, dc * 16 + 2 * dc + 1:dc * 16 + 2 * dc + 2], gw_h)
        gdma(swblk[0:64, dc * 16 + 2 * dc:dc * 16 + 2 * dc + 1], sw_h)
        gdma(swblk[64:128, dc * 16 + 2 * dc + 1:dc * 16 + 2 * dc + 2], sw_h)
    for dc in range(NDC):
        gdma(wr_sb[:, dc * 8:(dc + 1) * 8], wr_h[dc * 128:(dc + 1) * 128, :])

    P.wait("t", "dg")
    P.wait("t", "dy")
    P.wait("t", "ve")
    mm(pf[4][0:1, 0:16], gwb_tmp[:, 0:1], ones_bf[:, 0:16], True, True)
    mm(pf[4][0:1, 16:32], gwb_tmp[:, 1:2], ones_bf[:, 0:16], True, True)
    for r in range(4):
        mm(pf[4][0:128, 32 + r:33 + r], ones_f32[:, :], pmT[:, r:r + 1], True, True)
        mm(pf[4][0:128, 40 + r:41 + r], ones_f32[:, :], vmT[:, r:r + 1], True, True)
    mm(pf[4][0:128, 48:56], ones_f32[:, :], rb_sb[:, :], True, True)
    td = P.done("t", "te")
    P.wait("v", "te", td)
    P.v(lambda e: e.tensor_copy(out=gwb16[:, :], in_=pf[4][0:1, 0:16]))
    P.v(lambda e: e.tensor_copy(out=swb16[:, :], in_=pf[4][0:1, 16:32]))
    P.v(lambda e: e.tensor_copy(out=pmbc[:, :], in_=pf[4][0:128, 32:36]))
    P.v(lambda e: e.tensor_copy(out=vmbc[:, :], in_=pf[4][0:128, 40:44]))
    P.v(lambda e: e.tensor_copy(out=biasbc[:, :], in_=pf[4][0:128, 48:56]))
    P.done("v", "ve")
    P.wait("t", "ve")

    wunits = []
    for wsrc, wdst in [(wk_h, wk_bf.ap()), (wq_h, wq_bf.ap()),
                       (wk2_h, wk2_bf.ap()), (wcp_h, wcp_bf.ap())]:
        for p in range(8):
            wunits.append((wsrc[p * 128:(p + 1) * 128, :],
                           wdst[p * 128:(p + 1) * 128, :], D))
    for ee in range(NE):
        for p in range(8):
            wunits.append((wg_h[ee, p * 128:(p + 1) * 128, :],
                           wg_bf.ap()[ee, p * 128:(p + 1) * 128, :], I_FFN))
            wunits.append((wu_h[ee, p * 128:(p + 1) * 128, :],
                           wu_bf.ap()[ee, p * 128:(p + 1) * 128, :], I_FFN))
        for p in range(4):
            wunits.append((wd_h[ee, p * 128:(p + 1) * 128, :],
                           wd_bf.ap()[ee, p * 128:(p + 1) * 128, :], D))
    wstate = {"i": 0, "projmark": None}

    def emit_wunits(n):
        for _ in range(n):
            i = wstate["i"]
            if i >= len(wunits):
                return
            wstate["i"] = i + 1
            src, dst, w = wunits[i]
            par = i % 2
            f32v = y_tm[:, par * 4096:par * 4096 + 2048].bitcast(F32)[:, 0:w]
            bfv = y_tm[:, par * 4096 + 2048:par * 4096 + 2048 + w]
            if i >= 2:
                P.wait("y", "ac")
            lded = ydma(f32v, src, "dwl")
            P.wait("s", "dwl", lded)
            if i >= 2:
                P.wait("s", "dws")
            P.s(lambda e, f32v=f32v, bfv=bfv: e.activation(out=bfv, in_=f32v,
                                                           func=AF.Copy))
            cvd = P.done("s", "ac")
            P.wait("y", "ac", cvd)
            ydma(dst, bfv, "dws")
            if wstate["i"] == 32:
                wstate["projmark"] = P.mark("dws")

    def emit_a1_tile(tl):
        P.wait("t", "dg")
        P.wait("t", "ac")
        for dinc in range(NDC):
            mm(pf[4][0:16, 0:TILE], gwblk[:, dinc * 16:(dinc + 1) * 16],
               xn_fm[:, dinc * HWs + 3 + tl * TILE:dinc * HWs + 3 + (tl + 1) * TILE],
               start=(dinc == 0), stop=False)
        mm(pf[4][0:16, 0:TILE], gwb16[:, :], ones_bf[:, 0:TILE], False, True)
        tdone = P.done("t", "te")
        P.wait("s", "te", tdone)
        P.s(lambda e, tl=tl: e.activation(out=lgg[:, tl * TILE:(tl + 1) * TILE],
                                          in_=pf[4][0:16, 0:TILE], func=AF.Sigmoid))
        P.done("s", "ac")

    def ln_rows(xs, nrows, xn_out):
        P.s(lambda e: e.activation(out=sq_sb[0:nrows, :], in_=xs, func=AF.Square))
        P.done("s", "ac")
        P.wait("v", "ac")
        P.v(lambda e: e.reduce_sum(out=stats[0:nrows, 0:1], in_=xs, axis=AX.X))
        P.v(lambda e: e.reduce_sum(out=stats[0:nrows, 1:2], in_=sq_sb[0:nrows, :],
                                   axis=AX.X))
        P.v(lambda e: e.drain())
        P.v(lambda e: e.tensor_scalar(out=stats[0:nrows, 2:3], in0=stats[0:nrows, 0:1],
                                      scalar1=1.0 / D, scalar2=None, op0=ALU.mult))
        P.v(lambda e: e.drain())
        P.v(lambda e: e.tensor_tensor(out=stats[0:nrows, 3:4], in0=stats[0:nrows, 2:3],
                                      in1=stats[0:nrows, 2:3], op=ALU.mult))
        P.v(lambda e: e.drain())
        P.v(lambda e: e.scalar_tensor_tensor(out=stats[0:nrows, 4:5],
                                             in0=stats[0:nrows, 1:2],
                                             scalar=1.0 / D, in1=stats[0:nrows, 3:4],
                                             op0=ALU.mult, op1=ALU.subtract))
        P.v(lambda e: e.drain())
        P.v(lambda e: e.tensor_scalar_add(out=stats[0:nrows, 4:5],
                                          in0=stats[0:nrows, 4:5], scalar1=1e-5))
        P.done("v", "ve")
        P.wait("s", "ve")
        P.s(lambda e: e.activation(out=stats[0:nrows, 5:6], in_=stats[0:nrows, 4:5],
                                   func=AF.Ln))
        P.s(lambda e: e.drain())
        P.s(lambda e: e.activation(out=stats[0:nrows, 6:7], in_=stats[0:nrows, 5:6],
                                   func=AF.Exp, scale=-0.5))
        P.done("s", "ac")
        P.wait("v", "ac")
        P.v(lambda e: e.tensor_scalar(out=xn_out, in0=xs,
                                      scalar1=stats[0:nrows, 2:3],
                                      scalar2=stats[0:nrows, 6:7],
                                      op0=ALU.subtract, op1=ALU.mult))
        return P.done("v", "ve")

    G = min(4, NCH)
    NG = (NCH + G - 1) // G
    xslots = []
    for buf, lo in ((y_tm, 8192), (qt, 0), (k2t, 0), (q2t, 0)):
        k = lo
        while k + 2048 <= buf.shape[1] and len(xslots) < 2 * G:
            xslots.append(buf[:, k:k + 2048].bitcast(F32))
            k += 2048
    assert len(xslots) >= 2 * G
    amarks = {}
    for g in range(NG):
        c0 = g * G
        gs = min(G, NCH - c0)
        base = (g % 2) * 80
        sl = [xslots[(g % 2) * G + k] for k in range(gs)]
        if g >= 2:
            P.wait("y", "ve", amarks[g - 2])
        dxc = "dxa" if g % 2 == 0 else "dxb"
        for k in range(gs):
            ydma(sl[k][:, :], x_h[(c0 + k) * 128:(c0 + k + 1) * 128, :], dxc)
        dxm = P.mark(dxc)
        emit_wunits((len(wunits) + NG - 1) // NG)
        P.wait("v", dxc, dxm)
        for k in range(gs):
            P.v(lambda e, k=k, s=sl[k], base=base: e.bn_stats(
                out=sq_sb[:, base + k * 12:base + k * 12 + 6], in_=s[:, 0:512]))
            P.v(lambda e, k=k, s=sl[k], base=base: e.bn_stats(
                out=sq_sb[:, base + k * 12 + 6:base + k * 12 + 12],
                in_=s[:, 512:1024]))
        P.v(lambda e: e.drain())
        for k in range(gs):
            P.v(lambda e, k=k, base=base: e.bn_aggr(
                out=sq_sb[:, base + 48 + 2 * k:base + 50 + 2 * k],
                in_=sq_sb[:, base + k * 12:base + k * 12 + 12]))
        P.v(lambda e: e.drain())
        P.v(lambda e, gs=gs, base=base: e.tensor_scalar_add(
            out=sq_sb[:, base + 60:base + 60 + gs],
            in0=bass.AP(sq_sb, base + 49, [[sq_sb.shape[1], 128], [2, gs]]),
            scalar1=1e-5))
        vstat = P.done("v", "ve")
        P.wait("s", "ve", vstat)
        P.s(lambda e, gs=gs, base=base: e.activation(
            out=sq_sb[:, base + 64:base + 64 + gs],
            in_=sq_sb[:, base + 60:base + 60 + gs], func=AF.Ln))
        P.s(lambda e: e.drain())
        P.s(lambda e, gs=gs, base=base: e.activation(
            out=sq_sb[:, base + 64:base + 64 + gs],
            in_=sq_sb[:, base + 64:base + 64 + gs], func=AF.Exp, scale=-0.5))
        sac = P.done("s", "ac")
        P.wait("v", "ac", sac)
        for k in range(gs):
            P.v(lambda e, k=k, c=c0 + k, s=sl[k], base=base: e.tensor_scalar(
                out=xn_tm[:, c * D:(c + 1) * D], in0=s[:, 0:D],
                scalar1=sq_sb[:, base + 48 + 2 * k:base + 49 + 2 * k],
                scalar2=sq_sb[:, base + 64 + k:base + 65 + k],
                op0=ALU.subtract, op1=ALU.mult))
        amarks[g] = P.done("v", "ve")
        P.wait("t", "ve", amarks[g])
        if g == 0:
            P.wait("t", "pl")
        emarks = {}
        for k in range(gs):
            c = c0 + k
            if k >= 2:
                P.wait("t", "ve", emarks[k - 2])
            for dc in range(NDC):
                P.t(lambda e, dc=dc, c=c, k=k: e.transpose(
                    pb[k % 2][:, dc * 128:(dc + 1) * 128],
                    xn_tm[:, c * D + dc * 128:c * D + (dc + 1) * 128],
                    id_bf[:, :]))
            td = P.done("t", "te")
            P.wait("v", "te", td)
            P.v(lambda e, c=c, k=k: e.tensor_copy(
                out=bass.AP(xn_fm, 3 + c * 128,
                            [[NDC * HWs, 128], [HWs, NDC], [1, 128]]),
                in_=pb[k % 2][:, :].rearrange("p (a b) -> p a b", a=NDC)))
            emarks[k] = P.done("v", "ve")
        for c in range(c0, c0 + gs):
            if (c + 1) % CPT == 0:
                P.wait("t", "ve", emarks[gs - 1])
                emit_a1_tile((c + 1) // CPT - 1)
    emit_wunits(len(wunits))

    P.wait("y", "ve")
    ydma(x_tm[0:4, :], xhalo_h)
    hld = P.mark("dy")
    P.wait("s", "dy", hld)
    P.wait("v", "dy", hld)
    hdone = ln_rows(x_tm[0:4, :], 4, xh_tm[:, :])
    P.wait("t", "ve", hdone)
    for dc in range(NDC):
        P.t(lambda e, dc=dc: e.transpose(pb[0][:, dc * 4:dc * 4 + 4],
                                         xh_tm[:, dc * 128:(dc + 1) * 128],
                                         id_bf[0:4, 0:4]))
    tdone = P.done("t", "te")
    P.wait("v", "te", tdone)
    for dc in range(NDC):
        P.v(lambda e, dc=dc: e.tensor_copy(out=xn_fm[:, dc * HWs:dc * HWs + 3],
                                           in_=pb[0][:, dc * 4:dc * 4 + 3]))
        P.v(lambda e, dc=dc: e.tensor_copy(
            out=xn_fm[:, dc * HWs + 3 + T:dc * HWs + 4 + T],
            in_=pb[0][:, dc * 4 + 3:dc * 4 + 4]))
    P.done("v", "ve")

    if debug:
        for c in range(NCH):
            P.wait("v", "dbo")
            P.v(lambda e, c=c: e.tensor_copy(out=outf_dbg[:, :],
                                             in_=xn_tm[:, c * D:(c + 1) * D]))
            dv = P.done("v", "ve")
            P.wait("y", "ve", dv)
            ydma(dbg["dbg_xn"][c * 128:(c + 1) * 128, :], outf_dbg[:, :], "dbo")

    def stream_proj(wsrc, bias_h, dst_fn, act=None, act_scale=1.0, tilebase=0):
        P.wait("g", "te")
        gdma(bias_sb[:, :], bias_h.rearrange("(a b) -> a b", a=1))
        temarks = {}
        acmarks = {}
        dsm = {}

        def emit_group(doutc):
            par = doutc % 2
            P.wait("t", "ve")
            if doutc >= 2:
                P.wait("t", "ac", acmarks[doutc - 2])
            P.wait("t", "dg")
            P.wait("t", "dsa" if par == 0 else "dsb", dsm[doutc])
            for dinc in range(NDC):
                mm(pf[par][:, 0:TILE],
                   wbuf[:, par * 1024 + dinc * 128:par * 1024 + (dinc + 1) * 128],
                   xn_fm[:, dinc * HWs + 3 + tilebase * TILE:
                         dinc * HWs + 3 + (tilebase + 1) * TILE],
                   start=(dinc == 0), stop=False)
            mm(pf[par][:, 0:TILE], bias_sb[:, doutc * 128:(doutc + 1) * 128],
               ones_bf[:, 0:TILE], start=False, stop=True)
            temarks[doutc] = P.done("t", "te")
            P.wait("s", "te", temarks[doutc])
            tgt = dst_fn(doutc)
            if act is None:
                P.s(lambda e, tgt=tgt, par=par: e.activation(
                    out=tgt, in_=pf[par][:, 0:TILE], func=AF.Copy))
            else:
                P.s(lambda e, tgt=tgt, par=par, a=act, sc=act_scale: e.activation(
                    out=tgt, in_=pf[par][:, 0:TILE], func=a, scale=sc))
            acmarks[doutc] = P.done("s", "ac")

        for doutc in range(NDC):
            par = doutc % 2
            if doutc >= 2:
                P.wait("s", "te", temarks[doutc - 2])
            sdma(wbuf[:, par * 1024:(par + 1) * 1024],
                 bass.AP(wsrc.tensor, doutc * 128,
                         [[D, 128], [128 * D, NDC], [1, 128]]),
                 "dsa" if par == 0 else "dsb")
            dsm[doutc] = P.mark("dsa" if par == 0 else "dsb")
            if doutc >= 1:
                emit_group(doutc - 1)
        emit_group(NDC - 1)
        P.wait("t", "ac")

    P.wait("v", "ac")
    P.v(lambda e: e.tensor_scalar_max(out=lgg[:, :], in0=lgg[:, :], scalar1=1e-6))
    P.done("v", "ve")
    P.wait("s", "ve")
    P.s(lambda e: e.activation(out=lgg[:, :], in_=lgg[:, :], func=AF.Ln))
    P.done("s", "ac")
    P.wait("v", "ac")
    P.v(lambda e: e.tensor_tensor_scan(out=lgg[:, :], data0=lgg[:, :],
                                       data1=lgg[:, :], initial=0.0,
                                       op0=ALU.add, op1=ALU.bypass))
    P.done("v", "ve")

    P.wait("y", "ve")
    ydma(cc0i[:, :], lgg[:, T - 1:T])
    P.wait("g", "dy")
    P.inc("g", lambda e: e.collective_compute(
        "AllGather", ALU.bypass, replica_groups=GROUPS,
        ins=[cc0i[:, :]], outs=[cc0o[:, :]]), "cc", 1)
    P.wait("g", "cc")
    gdma(gsum_blk[:, :], cc0o.ap().rearrange("(r h) one -> r (h one)", r=4), "dgc")
    P.wait("t", "dgc")
    mm(pf[4][0:16, 0:1], gsum_blk[:, :], pm_sb[:, :], True, True)
    tdone = P.done("t", "te")
    P.wait("v", "te", tdone)
    P.v(lambda e: e.tensor_copy(out=off_col[:, :], in_=pf[4][0:16, 0:1]))
    P.v(lambda e: e.drain())
    P.v(lambda e: e.tensor_scalar(out=lgg[:, :], in0=lgg[:, :],
                                  scalar1=off_col[:, :], scalar2=30.0,
                                  op0=ALU.add, op1=ALU.min))
    P.v(lambda e: e.drain())
    P.v(lambda e: e.tensor_scalar_max(out=lgg[:, :], in0=lgg[:, :], scalar1=-30.0))
    P.done("v", "ve")
    P.wait("s", "ve")
    P.s(lambda e: e.activation(out=lgg[:, :], in_=lgg[:, :], func=AF.Exp))
    P.done("s", "ac")
    P.wait("v", "ac")
    P.v(lambda e: e.tensor_scalar_add(out=lgg[:, :], in0=lgg[:, :], scalar1=1e-6))
    gcd = P.done("v", "ve")
    if debug:
        P.wait("y", "ve", gcd)
        ydma(dbg["dbg_gc"][:, :], lgg[:, :], "dbo")

    emit_wunits(len(wunits))
    P.wait("s", "dws", wstate["projmark"])
    for tl in range(NT):
        stream_proj(wk_bf.ap(), bk_h, lambda dc: kt[:, dc * TILE:(dc + 1) * TILE],
                    tilebase=tl)
        P.wait("t", "ve")
        for dinc in range(NDC):
            mm(pf[4][0:16, 0:TILE], swblk[:, dinc * 16:(dinc + 1) * 16],
               kt[:, dinc * TILE:(dinc + 1) * TILE],
               start=(dinc == 0), stop=False)
        mm(pf[4][0:16, 0:TILE], swb16[:, :], ones_bf[:, 0:TILE], False, True)
        tdone = P.done("t", "te")
        P.wait("s", "te", tdone)
        P.s(lambda e, tl=tl: e.activation(out=rra[:, tl * TILE:(tl + 1) * TILE],
                                          in_=pf[4][0:16, 0:TILE], func=AF.Silu))
        rd = P.done("s", "ac")
        P.wait("t", "ac", rd)

        for cc_i in range(CPT):
            c = tl * CPT + cc_i
            P.wait("t", "ve")
            P.t(lambda e, c=c: e.transpose(pf[4][:, 0:16],
                                           lgg[:, c * 128:(c + 1) * 128],
                                           id_f32[0:16, 0:16]))
            P.t(lambda e, c=c: e.transpose(pb[1][:, 0:16],
                                           rra[:, c * 128:(c + 1) * 128],
                                           id_bf[0:16, 0:16]))
            tdone = P.done("t", "te")
            P.wait("v", "te", tdone)
            P.v(lambda e, c=c: e.tensor_copy(out=gc_tm[:, c * 16:(c + 1) * 16],
                                             in_=pf[4][:, 0:16]))
            P.v(lambda e: e.tensor_copy(out=rr_t[:, :], in_=pb[1][:, 0:16]))
            P.v(lambda e: e.drain())
            P.v(lambda e, c=c: e.reciprocal(out=rgc_t[:, :],
                                            in_=gc_tm[:, c * 16:(c + 1) * 16]))
            P.v(lambda e: e.drain())
            P.v(lambda e, c=c: e.tensor_tensor(out=a_tm[:, c * 16:(c + 1) * 16],
                                               in0=rr_t[:, :], in1=rgc_t[:, :],
                                               op=ALU.mult))
            P.v(lambda e: e.drain())
            for h in range(H):
                P.v(lambda e, c=c, h=h: e.tensor_scalar_mul(
                    out=av_sb[:, h * 64:(h + 1) * 64],
                    in0=xn_tm[:, c * D + h * 64:c * D + (h + 1) * 64],
                    scalar1=a_tm[:, c * 16 + h:c * 16 + h + 1]))
            avd = P.done("v", "ve")
            P.wait("t", "ve", avd)
            for dc in range(NDC):
                P.t(lambda e, cc_i=cc_i, dc=dc: e.transpose(
                    pb[0][:, dc * 128:(dc + 1) * 128],
                    kt[:, dc * TILE + cc_i * 128:dc * TILE + (cc_i + 1) * 128],
                    id_bf[:, :]))
            tdone = P.done("t", "te")
            P.wait("v", "te", tdone)
            P.v(lambda e: e.tensor_copy(out=ktm2[:, :], in_=pb[0][:, :]))
            kd = P.done("v", "ve")
            P.wait("t", "ve", kd)
            for h in range(H):
                hp, hh = h // 2, h % 2
                mm(pf[0][hh * 64:(hh + 1) * 64, hp * 64:(hp + 1) * 64],
                   ktm2[:, hp * 128 + hh * 64:hp * 128 + (hh + 1) * 64],
                   av_sb[:, h * 64:(h + 1) * 64],
                   start=True, stop=True, tp=(0, hh * 64))
            tdone = P.done("t", "te")
            P.wait("v", "te", tdone)
            if c == 0:
                P.v(lambda e: e.tensor_copy(out=s1run[:, 0:512], in_=pf[0][:, :]))
            else:
                P.v(lambda e: e.tensor_tensor(out=s1run[:, 0:512],
                                              in0=s1run[:, 0:512],
                                              in1=pf[0][:, :], op=ALU.add))
            P.v(lambda e: e.drain())
            P.v(lambda e, c=c: e.tensor_copy(out=s1p[:, c * 512:(c + 1) * 512],
                                             in_=s1run[:, 0:512]))
            P.done("v", "ve")

    vd = P.mark("ve")
    P.wait("y", "ve", vd)
    ydma(cc1i[:, :], s1run[:, 0:512])
    P.wait("g", "dy")
    P.inc("g", lambda e: e.collective_compute(
        "AllGather", ALU.bypass, replica_groups=GROUPS,
        ins=[cc1i[:, :]], outs=[cc1o[:, :]]), "cc", 1)
    ccend = P.mark("cc")
    P.wait("v", "ve")
    P.v(lambda e: e.memset(x_tm[:, 0:512], 0.0))
    P.done("v", "ve")
    for r in range(4):
        P.wait("g", "cc", ccend)
        P.wait("g", "ve")
        ld = gdma(ccg[:, 0:512], cc1o[r * 128:(r + 1) * 128, :], "dgc")
        P.wait("v", "dgc", ld)
        P.v(lambda e, r=r: e.scalar_tensor_tensor(
            out=x_tm[:, 0:512], in0=ccg[:, 0:512],
            scalar=pmbc[:, r:r + 1], in1=x_tm[:, 0:512],
            op0=ALU.mult, op1=ALU.add))
        P.done("v", "ve")
    P.v(lambda e: e.tensor_copy(out=s1in[:, :], in_=x_tm[:, 0:512]))
    P.done("v", "ve")
    P.wait("t", "ve")

    for tl in range(NT):
        stream_proj(wq_bf.ap(), bq_h, lambda dc: qt[:, dc * TILE:(dc + 1) * TILE],
                    tilebase=tl)
        stream_proj(wk2_bf.ap(), bk2_h, lambda dc: k2t[:, dc * TILE:(dc + 1) * TILE],
                    act=AF.Sigmoid, act_scale=0.02 / 32.0, tilebase=tl)
        stream_proj(wk_bf.ap(), bk_h, lambda dc: kt[:, dc * TILE:(dc + 1) * TILE],
                    tilebase=tl)
        P.wait("v", "ac")
        P.v(lambda e: e.tensor_scalar(out=q2t[:, :], in0=qt[:, 0:NDC * TILE], scalar1=0.0,
                                      scalar2=0.98 / 8.0, op0=ALU.max, op1=ALU.mult))
        P.v(lambda e: e.drain())
        P.v(lambda e: e.scalar_tensor_tensor(out=q2t[:, :], in0=qt[:, 0:NDC * TILE],
                                             scalar=0.125, in1=q2t[:, :],
                                             op0=ALU.mult, op1=ALU.subtract))
        P.done("v", "ve")

        for cc_i in range(CPT):
            c = tl * CPT + cc_i

            def qsl(buf, h, cc_i=cc_i):
                hp, hh = h // 2, h % 2
                return buf[hh * 64:(hh + 1) * 64,
                           hp * TILE + cc_i * 128:hp * TILE + (cc_i + 1) * 128]

            P.wait("v", "te")
            if c == 0:
                P.v(lambda e: e.tensor_copy(out=s1tot[:, :], in_=s1in[:, :]))
                P.v(lambda e: e.memset(s2tot[:, :], 0.0))
            else:
                P.v(lambda e, c=c: e.tensor_tensor(
                    out=s1tot[:, :], in0=s1in[:, :],
                    in1=s1p[:, (c - 1) * 512:c * 512], op=ALU.add))
                P.v(lambda e: e.tensor_copy(out=s2tot[:, :], in_=s2run[:, 512:1024]))
            for h in range(H):
                P.v(lambda e, c=c, h=h: e.tensor_scalar_mul(
                    out=av_sb[:, h * 64:(h + 1) * 64],
                    in0=xn_tm[:, c * D + h * 64:c * D + (h + 1) * 64],
                    scalar1=a_tm[:, c * 16 + h:c * 16 + h + 1]))
            pre = P.done("v", "ve")
            P.wait("y", "ve", pre)
            ydma(vs_sb[0:127, :], xn_tm[1:128, c * D:(c + 1) * D])
            if c < NCH - 1:
                ydma(vs_sb[127:128, :], xn_tm[0:1, (c + 1) * D:(c + 1) * D + D])
            else:
                ydma(vs_sb[127:128, :], xh_tm[3:4, :])
            vsld = P.mark("dy")

            def rec_pass(kbuf, qbuf, rhs_for_o, stot, obank_off):
                for h in range(H):
                    hp, hh = h // 2, h % 2
                    mm(pf[h % 4][:, (h // 4) * 128:(h // 4) * 128 + 128],
                       qsl(kbuf, h), qsl(qbuf, h),
                       start=True, stop=True, tp=(hh * 64, 0))
                tdone = P.done("t", "te")
                P.wait("v", "te", tdone)
                for h in range(H):
                    P.v(lambda e, h=h: e.tensor_tensor(
                        out=p_sb[:, h * 128:(h + 1) * 128],
                        in0=pf[h % 4][:, (h // 4) * 128:(h // 4) * 128 + 128],
                        in1=tri_f32[:, :], op=ALU.mult))
                pd = P.done("v", "ve")
                P.wait("t", "ve", pd)
                for h in range(H):
                    hp, hh = h // 2, h % 2
                    bank = pf[2 + h // 8]
                    col = (h % 8) * 64
                    mm(bank[:, col:col + 64], p_sb[:, h * 128:(h + 1) * 128],
                       rhs_for_o[:, h * 64:(h + 1) * 64],
                       start=True, stop=False)
                    mm(bank[:, col:col + 64], qsl(qbuf, h),
                       stot[hh * 64:(hh + 1) * 64, hp * 64:(hp + 1) * 64],
                       start=False, stop=True, tp=(hh * 64, 0))
                return P.done("t", "te")

            P.wait("t", "ve", pre)
            o1done = rec_pass(kt, qt, av_sb, s1tot, 0)
            P.wait("v", "te", o1done)
            for h in range(H):
                bank = pf[2 + h // 8]
                col = (h % 8) * 64
                P.v(lambda e, c=c, h=h, bank=bank, col=col: e.tensor_scalar_mul(
                    out=y_tm[:, c * D + h * 64:c * D + (h + 1) * 64],
                    in0=bank[:, col:col + 64],
                    scalar1=gc_tm[:, c * 16 + h:c * 16 + h + 1]))
            P.v(lambda e: e.drain())
            P.wait("v", "dy", vsld)
            P.v(lambda e, c=c: e.tensor_tensor(out=e_sb[:, :], in0=vs_sb[:, :],
                                               in1=y_tm[:, c * D:(c + 1) * D],
                                               op=ALU.subtract))
            ed = P.done("v", "ve")
            P.wait("t", "ve", ed)
            for dc in range(NDC):
                P.t(lambda e, dc=dc, cc_i=cc_i: e.transpose(
                    pb[0][:, dc * 128:(dc + 1) * 128],
                    k2t[:, dc * TILE + cc_i * 128:dc * TILE + (cc_i + 1) * 128],
                    id_bf[:, :]))
            tdone = P.done("t", "te")
            P.wait("v", "te", tdone)
            P.v(lambda e: e.tensor_copy(out=ktm2[:, :], in_=pb[0][:, :]))
            k2d = P.done("v", "ve")
            P.wait("t", "ve", k2d)
            o2done = rec_pass(k2t, q2t, e_sb, s2tot, 0)
            for h in range(H):
                hp, hh = h // 2, h % 2
                mm(pf[5][hh * 64:(hh + 1) * 64, hp * 64:(hp + 1) * 64],
                   ktm2[:, hp * 128 + hh * 64:hp * 128 + (hh + 1) * 64],
                   e_sb[:, h * 64:(h + 1) * 64],
                   start=True, stop=True, tp=(0, hh * 64))
            st2d = P.done("t", "te")
            P.wait("v", "te", st2d)
            if c == 0:
                P.v(lambda e: e.tensor_copy(out=s2run[:, 512:1024], in_=pf[5][:, :]))
            else:
                P.v(lambda e: e.tensor_tensor(out=s2run[:, 512:1024],
                                              in0=s2run[:, 512:1024],
                                              in1=pf[5][:, :], op=ALU.add))
            P.done("v", "ve")
            if c > 0:
                ydma(o2s[0:1, :], o2e[127:128, :], "dys")
            rowd = P.mark("dys")
            P.wait("v", "te", o2done)
            P.wait("v", "dys", rowd)
            if c == 0:
                P.v(lambda e: e.memset(o2s[0:1, :], 0.0))
                P.v(lambda e: e.drain())
            P.v(lambda e: e.tensor_copy(out=o2e[:, 0:512], in_=pf[2][:, :]))
            P.v(lambda e: e.tensor_copy(out=o2e[:, 512:1024], in_=pf[3][:, :]))
            o2ed = P.done("v", "ve")
            P.wait("y", "ve", o2ed)
            ydma(o2s[1:128, :], o2e[0:127, :], "dys")
            shd = P.mark("dys")
            P.wait("v", "dys", shd)
            P.v(lambda e, c=c: e.tensor_tensor(out=y_tm[:, c * D:(c + 1) * D],
                                               in0=y_tm[:, c * D:(c + 1) * D],
                                               in1=o2s[:, :], op=ALU.add))
            P.done("v", "ve")

    P.wait("v", "te")
    P.v(lambda e: e.tensor_copy(out=t2pack[:, 0:512], in_=s2run[:, 512:1024]))
    for dc in range(NDC):
        P.v(lambda e, dc=dc: e.tensor_copy(
            out=t2pack[:, 512 + dc:513 + dc],
            in_=q2t[:, dc * TILE + TILE - 1:dc * TILE + TILE]))
    vd = P.done("v", "ve")
    P.wait("y", "ve", vd)
    ydma(cc2i[:, :], t2pack[:, :])
    P.wait("g", "dy")
    P.inc("g", lambda e: e.collective_compute(
        "AllGather", ALU.bypass, replica_groups=GROUPS,
        ins=[cc2i[:, :]], outs=[cc2o[:, :]]), "cc", 1)
    ccend = P.mark("cc")
    P.wait("v", "ve")
    P.v(lambda e: e.memset(x_tm[:, 0:520], 0.0))
    P.done("v", "ve")
    for r in range(4):
        P.wait("g", "cc", ccend)
        P.wait("g", "ve")
        ld = gdma(ccg[:, 0:520], cc2o[r * 128:(r + 1) * 128, :], "dgc")
        P.wait("v", "dgc", ld)
        P.v(lambda e, r=r: e.scalar_tensor_tensor(
            out=x_tm[:, 0:512], in0=ccg[:, 0:512],
            scalar=pmbc[:, r:r + 1], in1=x_tm[:, 0:512],
            op0=ALU.mult, op1=ALU.add))
        P.v(lambda e, r=r: e.scalar_tensor_tensor(
            out=x_tm[:, 512:520], in0=ccg[:, 512:520],
            scalar=vmbc[:, r:r + 1], in1=x_tm[:, 512:520],
            op0=ALU.mult, op1=ALU.add))
        P.done("v", "ve")
    P.v(lambda e: e.tensor_copy(out=s2in[:, :], in_=x_tm[:, 0:512]))
    P.v(lambda e: e.tensor_copy(out=vprev[:, :], in_=x_tm[:, 512:520]))
    P.done("v", "ve")

    if debug:
        for c in range(NCH):
            P.wait("v", "dbo")
            P.v(lambda e, c=c: e.tensor_copy(out=outf_dbg[:, :],
                                             in_=y_tm[:, c * D:(c + 1) * D]))
            dv = P.done("v", "ve")
            P.wait("y", "ve", dv)
            ydma(dbg["dbg_y"][c * 128:(c + 1) * 128, :], outf_dbg[:, :], "dbo")

    P.wait("s", "ve")
    sdma(bass.AP(cpw_sb, 0, [[cpw_sb.shape[1], 128], [D, NDC], [1, D]]),
         bass.AP(wcp_bf, 0, [[D, 128], [128 * D, NDC], [1, D]]))
    P.wait("t", "ve")
    for hh in range(2):
        bank = pf[4 + hh]
        for hp in range(NP):
            mm(bank[0:1, hp * 64:(hp + 1) * 64],
               vprev[hh * 64:(hh + 1) * 64, hp:hp + 1],
               s2in[hh * 64:(hh + 1) * 64, hp * 64:(hp + 1) * 64],
               start=True, stop=True, tp=(hh * 64, 0))
    tdone = P.done("t", "te")
    P.wait("v", "te", tdone)
    for hh in range(2):
        bank = pf[4 + hh]
        P.v(lambda e, hh=hh, bank=bank: e.tensor_tensor(
            out=bass.AP(y_tm, hh * 64, [[y_tm.shape[1], 1], [128, 8], [1, 64]]),
            in0=bass.AP(y_tm, hh * 64, [[y_tm.shape[1], 1], [128, 8], [1, 64]]),
            in1=bank[0:1, 0:512].rearrange("p (a b) -> p a b", a=8), op=ALU.add))
        P.v(lambda e: e.drain())
    P.done("v", "ve")
    P.wait("t", "ve")

    for tl in range(NT):
        stream_proj(wq_bf.ap(), bq_h, lambda dc: qt[:, dc * TILE:(dc + 1) * TILE],
                    tilebase=tl)
        P.wait("v", "ac")
        for dc in range(NDC):
            if tl == 0:
                P.v(lambda e, dc=dc: e.memset(
                    q2s[:, dc * (TILE + 1):dc * (TILE + 1) + 1], 0.0))
            else:
                P.v(lambda e, dc=dc: e.tensor_copy(
                    out=q2s[:, dc * (TILE + 1):dc * (TILE + 1) + 1],
                    in_=q2s[:, dc * (TILE + 1) + TILE:dc * (TILE + 1) + TILE + 1]))
        P.v(lambda e: e.drain())
        P.v(lambda e: e.tensor_scalar(out=q2t[:, :], in0=qt[:, 0:NDC * TILE], scalar1=0.0,
                                      scalar2=0.98 / 8.0, op0=ALU.max, op1=ALU.mult))
        P.v(lambda e: e.drain())
        for dc in range(NDC):
            P.v(lambda e, dc=dc: e.scalar_tensor_tensor(
                out=q2s[:, dc * (TILE + 1) + 1:dc * (TILE + 1) + 1 + TILE],
                in0=qt[:, dc * TILE:(dc + 1) * TILE], scalar=0.125,
                in1=q2t[:, dc * TILE:(dc + 1) * TILE],
                op0=ALU.mult, op1=ALU.subtract))
        q2d = P.done("v", "ve")
        P.wait("t", "ve", q2d)
        P.wait("t", "ds")
        for cc_i in range(CPT):
            c = tl * CPT + cc_i
            for hh in range(2):
                bank = pf[2 + hh]
                for hp in range(NP):
                    mm(bank[:, hp * 64:(hp + 1) * 64],
                       q2s[hh * 64:(hh + 1) * 64,
                           hp * (TILE + 1) + cc_i * 128:
                           hp * (TILE + 1) + cc_i * 128 + 128],
                       s2in[hh * 64:(hh + 1) * 64, hp * 64:(hp + 1) * 64],
                       start=True, stop=True, tp=(hh * 64, 0))
            i2done = P.done("t", "te")
            P.wait("v", "te", i2done)
            for hh in range(2):
                bank = pf[2 + hh]
                P.v(lambda e, c=c, hh=hh, bank=bank: e.tensor_tensor(
                    out=bass.AP(y_tm, c * D + hh * 64,
                                [[y_tm.shape[1], 128], [128, 8], [1, 64]]),
                    in0=bass.AP(y_tm, c * D + hh * 64,
                                [[y_tm.shape[1], 128], [128, 8], [1, 64]]),
                    in1=bank[:, 0:512].rearrange("p (a b) -> p a b", a=8),
                    op=ALU.add))
                P.v(lambda e: e.drain())
            yd = P.done("v", "ve")
            P.wait("t", "ve", yd)
            for dc in range(NDC):
                P.t(lambda e, c=c, dc=dc: e.transpose(
                    pb[0][:, dc * 128:(dc + 1) * 128],
                    y_tm[:, c * D + dc * 128:c * D + (dc + 1) * 128], id_bf[:, :]))
            tdone = P.done("t", "te")
            P.wait("v", "te", tdone)
            P.v(lambda e: e.tensor_copy(out=yf_sb[:, :], in_=pb[0][:, :]))
            yfd = P.done("v", "ve")
            P.wait("y", "ve", yfd)
            ydma(x_tm[:, :], x_h[c * 128:(c + 1) * 128, :])
            xld = P.mark("dy")
            P.wait("t", "ve", yfd)
            for dc in range(NDC):
                for dg in range(2):
                    mm(pf[dg][:, :], yf_sb[:, dc * 128:(dc + 1) * 128],
                       cpw_sb[:, dc * D + dg * 512:dc * D + (dg + 1) * 512],
                       start=(dc == 0), stop=False)
            for dg in range(2):
                mm(pf[dg][:, :], ones_bf[:, 0:128],
                   bcp_sb[:, dg * 512:(dg + 1) * 512], start=False, stop=True)
            cpd = P.done("t", "te")
            P.wait("v", "te", cpd)
            P.wait("v", "dy", xld)
            for dg in range(2):
                P.v(lambda e, c=c, dg=dg: e.tensor_tensor(
                    out=outacc[:, c * D + dg * 512:c * D + (dg + 1) * 512],
                    in0=pf[dg][:, :],
                    in1=x_tm[:, dg * 512:(dg + 1) * 512], op=ALU.add))
            vd2 = P.done("v", "ve")
            P.wait("t", "ve", vd2)
            P.wait("y", "ve", vd2)

    if debug:
        for c in range(NCH):
            P.wait("v", "dbo")
            P.v(lambda e, c=c: e.tensor_copy(out=outf_dbg[:, :],
                                             in_=outacc[:, c * D:(c + 1) * D]))
            dv = P.done("v", "ve")
            P.wait("y", "ve", dv)
            ydma(dbg["dbg_attn"][c * 128:(c + 1) * 128, :], outf_dbg[:, :], "dbo")

    for c in range(NCH):
        P.wait("t", "ve")
        P.wait("t", "ac")
        for dinc in range(NDC):
            mm(pf[4][0:8, 0:128], wr_sb[:, dinc * 8:(dinc + 1) * 8],
               xn_fm[:, dinc * HWs + 3 + c * 128:dinc * HWs + 3 + (c + 1) * 128],
               start=(dinc == 0), stop=(dinc == NDC - 1))
        tdone = P.done("t", "te")
        P.wait("v", "te", tdone)
        P.v(lambda e: e.tensor_copy(out=sc_fm[:, :], in_=pf[4][0:8, 0:128]))
        scd = P.done("v", "ve")
        P.wait("t", "ve", scd)
        P.t(lambda e: e.transpose(pf[4][:, 0:8], sc_fm[:, :], id_f32[0:8, 0:8]))
        td2 = P.done("t", "te")
        P.wait("s", "te", td2)
        P.s(lambda e: e.activation(out=z_t[:, :], in_=pf[4][:, 0:8], func=AF.Sigmoid))
        sd2 = P.done("s", "ac")
        P.wait("t", "ac", sd2)
        P.wait("v", "ac", sd2)
        P.v(lambda e: e.tensor_tensor(out=z_m[:, :], in0=z_t[:, :],
                                      in1=biasbc[:, :], op=ALU.add))
        P.v(lambda e: e.drain())
        P.v(lambda e: e.max(out=z_srt[:, :], in_=z_m[:, :]))
        P.v(lambda e: e.memset(z_rep[:, :], -3e38))
        P.v(lambda e: e.drain())
        P.v(lambda e: e.tensor_copy(out=z_rep[:, 0:2], in_=z_srt[:, 0:2]))
        P.v(lambda e: e.drain())
        P.v(lambda e: e.match_replace(out=z_srt[:, :], in_to_replace=z_rep[:, :],
                                      in_values=z_m[:, :], imm_value=-3e38))
        P.v(lambda e: e.drain())
        P.v(lambda e: e.tensor_tensor(out=z_rep[:, :], in0=z_m[:, :],
                                      in1=z_srt[:, :], op=ALU.not_equal))
        P.v(lambda e: e.drain())
        P.v(lambda e: e.tensor_tensor(out=z_rep[:, :], in0=z_rep[:, :],
                                      in1=z_t[:, :], op=ALU.mult))
        P.v(lambda e: e.drain())
        P.v(lambda e: e.reduce_sum(out=z_m[:, 0:1], in_=z_rep[:, :], axis=AX.X))
        P.v(lambda e: e.drain())
        P.v(lambda e: e.tensor_scalar_add(out=z_m[:, 0:1], in0=z_m[:, 0:1],
                                          scalar1=1e-9))
        P.v(lambda e: e.drain())
        P.v(lambda e: e.reciprocal(out=z_m[:, 1:2], in_=z_m[:, 0:1]))
        P.v(lambda e: e.drain())
        P.v(lambda e, c=c: e.tensor_scalar_mul(out=cw_sb[:, c * 8:(c + 1) * 8],
                                               in0=z_rep[:, :], scalar1=z_m[:, 1:2]))
        P.v(lambda e: e.drain())
        P.v(lambda e, c=c: e.tensor_tensor(out=cw46[:, c:c + 1],
                                           in0=cw_sb[:, c * 8 + 4:c * 8 + 5],
                                           in1=cw_sb[:, c * 8 + 6:c * 8 + 7],
                                           op=ALU.add))
        P.v(lambda e, c=c: e.tensor_copy(out=cw7[:, c:c + 1],
                                         in_=cw_sb[:, c * 8 + 7:c * 8 + 8]))
        P.done("v", "ve")
    if debug:
        for c in range(NCH):
            P.wait("v", "dbo")
            P.v(lambda e, c=c: e.tensor_copy(out=outf_dbg[:, 0:8],
                                             in_=cw_sb[:, c * 8:(c + 1) * 8]))
            dv = P.done("v", "ve")
            P.wait("y", "ve", dv)
            ydma(dbg["dbg_cw"][c * 128:(c + 1) * 128, :], outf_dbg[:, 0:8], "dbo")

    NW = T // CW
    for dc in range(NDC):
        base = dc * HWs
        for tw in range(NW):
            P.wait("v", "ve")
            P.v(lambda e, base=base, dc=dc, tw=tw: e.tensor_scalar(
                out=conv_fm[:, 0:CW],
                in0=xn_fm[:, base + tw * CW:base + (tw + 1) * CW],
                scalar1=cvw_sb[:, dc * 4:dc * 4 + 1], scalar2=cvb_sb[:, dc:dc + 1],
                op0=ALU.mult, op1=ALU.add))
            for k in range(1, 4):
                P.v(lambda e: e.drain())
                P.v(lambda e, base=base, dc=dc, k=k, tw=tw: e.scalar_tensor_tensor(
                    out=conv_fm[:, 0:CW],
                    in0=xn_fm[:, base + tw * CW + k:base + (tw + 1) * CW + k],
                    scalar=cvw_sb[:, dc * 4 + k:dc * 4 + k + 1],
                    in1=conv_fm[:, 0:CW], op0=ALU.mult, op1=ALU.add))
            cvd = P.done("v", "ve")
            P.wait("t", "ve", cvd)
            for c4 in range(CW // 128):
                c = tw * (CW // 128) + c4
                par = c4 % 2
                P.wait("t", "ve")
                P.t(lambda e, c4=c4, par=par: e.transpose(
                    pb[par][:, 0:128], conv_fm[:, c4 * 128:(c4 + 1) * 128],
                    id_bf[:, :]))
                td3 = P.done("t", "te")
                P.wait("v", "te", td3)
                P.v(lambda e, c=c, dc=dc, par=par: e.tensor_tensor(
                    out=outacc[:, c * D + dc * 128:c * D + (dc + 1) * 128],
                    in0=outacc[:, c * D + dc * 128:c * D + (dc + 1) * 128],
                    in1=pb[par][:, 0:128], op=ALU.add))
                P.done("v", "ve")

    P.wait("v", "ve")
    for c in range(NCH):
        P.v(lambda e, c=c: e.scalar_tensor_tensor(
            out=outacc[:, c * D:(c + 1) * D], in0=xn_tm[:, c * D:(c + 1) * D],
            scalar=cw46[:, c:c + 1], in1=outacc[:, c * D:(c + 1) * D],
            op0=ALU.mult, op1=ALU.add))
        P.v(lambda e: e.drain())
        P.v(lambda e, c=c: e.tensor_scalar(
            out=e_sb[:, :], in0=xn_tm[:, c * D:(c + 1) * D],
            scalar1=0.0, scalar2=cw7[:, c:c + 1], op0=ALU.max, op1=ALU.mult))
        P.v(lambda e: e.drain())
        P.v(lambda e, c=c: e.tensor_tensor(
            out=outacc[:, c * D:(c + 1) * D], in0=outacc[:, c * D:(c + 1) * D],
            in1=e_sb[:, :], op=ALU.add))
        P.v(lambda e: e.drain())
    P.done("v", "ve")

    P.wait("s", "dws")
    estate = {"hidfree": P.mark("ve")}
    for tl in range(NT):
        for ee in range(NE):
            P.wait("s", "te")
            sdma(bass.AP(wdbuf, 0, [[wdbuf.shape[1], 128], [D, 4], [1, D]]),
                 bass.AP(wd_bf, ee * I_FFN * D,
                         [[D, 128], [128 * D, 4], [1, D]]))
            WB = wbig.shape[1]
            gqm = {}
            gqds = {}

            def gq_load(gq):
                half, quarter = gq // 2, gq % 2
                par = gq % 2
                if gq >= 2:
                    P.wait("s", "te", gqm[gq - 2])
                goff = (ee * D + quarter * 4 * 128) * I_FFN + half * 256
                for ic2 in range(2):
                    sdma(bass.AP(wbig, par * 2048 + ic2 * 128,
                                 [[WB, 128], [512, 4], [1, 128]]),
                         bass.AP(wg_bf, goff + ic2 * 128,
                                 [[I_FFN, 128], [128 * I_FFN, 4], [1, 128]]),
                         "dsa" if par == 0 else "dsb")
                    sdma(bass.AP(wbig, par * 2048 + 256 + ic2 * 128,
                                 [[WB, 128], [512, 4], [1, 128]]),
                         bass.AP(wu_bf, goff + ic2 * 128,
                                 [[I_FFN, 128], [128 * I_FFN, 4], [1, 128]]),
                         "dsa" if par == 0 else "dsb")
                gqds[gq] = P.mark("dsa" if par == 0 else "dsb")

            def gq_mm(gq):
                half, quarter = gq // 2, gq % 2
                par = gq % 2
                if quarter == 0:
                    P.wait("t", "ve", estate["hidfree"])
                P.wait("t", "dsa" if par == 0 else "dsb", gqds[gq])
                for d4 in range(4):
                    dinc = quarter * 4 + d4
                    xs = xn_fm[:, dinc * HWs + 3 + tl * TILE:
                               dinc * HWs + 3 + (tl + 1) * TILE]
                    for ic2 in range(2):
                        b0 = par * 2048 + d4 * 512 + ic2 * 128
                        mm(pf[ic2 * 2][:, 0:TILE], wbig[:, b0:b0 + 128], xs,
                           start=(dinc == 0), stop=(dinc == NDC - 1))
                        mm(pf[ic2 * 2 + 1][:, 0:TILE],
                           wbig[:, b0 + 256:b0 + 384], xs,
                           start=(dinc == 0), stop=(dinc == NDC - 1))
                gqm[gq] = P.done("t", "te")

            for half in range(2):
                gq_load(half * 2)
                gq_load(half * 2 + 1)
                gq_mm(half * 2)
                gq_mm(half * 2 + 1)
                gud = P.mark("te")
                P.wait("s", "te", gud)
                for ic2 in range(2):
                    P.s(lambda e, ic2=ic2: e.activation(
                        out=sg_sb[:, ic2 * 512:ic2 * 512 + TILE],
                        in_=pf[ic2 * 2][:, 0:TILE], func=AF.Silu))
                sgd = P.done("s", "ac")
                P.wait("v", "ac", sgd)
                for ic2 in range(2):
                    ic = half * 2 + ic2
                    P.v(lambda e, ic2=ic2, ic=ic: e.tensor_tensor(
                        out=hid[:, ic * TILE:(ic + 1) * TILE],
                        in0=sg_sb[:, ic2 * 512:ic2 * 512 + TILE],
                        in1=pf[ic2 * 2 + 1][:, 0:TILE], op=ALU.mult))
                estate["hidfree"] = P.done("v", "ve")
            hdall = P.mark("ve")
            P.wait("t", "ve", hdall)
            P.wait("t", "ds")
            for cc_i in range(CPT):
                c = tl * CPT + cc_i
                for ic in range(4):
                    mm(pf[4][:, :],
                       hid[:, ic * TILE + cc_i * 128:ic * TILE + (cc_i + 1) * 128],
                       wdbuf[:, ic * D:ic * D + 512],
                       start=(ic == 0), stop=(ic == 3))
                    mm(pf[5][:, :],
                       hid[:, ic * TILE + cc_i * 128:ic * TILE + (cc_i + 1) * 128],
                       wdbuf[:, ic * D + 512:(ic + 1) * D],
                       start=(ic == 0), stop=(ic == 3))
                wdd = P.done("t", "te")
                P.wait("v", "te", wdd)
                for dg in range(2):
                    bk = pf[4 + dg]
                    P.v(lambda e, c=c, dg=dg, bk=bk, ee=ee: e.scalar_tensor_tensor(
                        out=outacc[:, c * D + dg * 512:c * D + (dg + 1) * 512],
                        in0=bk[:, :], scalar=cw_sb[:, c * 8 + ee:c * 8 + ee + 1],
                        in1=outacc[:, c * D + dg * 512:c * D + (dg + 1) * 512],
                        op0=ALU.mult, op1=ALU.add))
                vdd = P.done("v", "ve")
                P.wait("t", "ve", vdd)

    for c in range(NCH):
        P.wait("v", "dout")
        P.v(lambda e, c=c: e.tensor_copy(out=outf[:, :],
                                         in_=outacc[:, c * D:(c + 1) * D]))
        fd = P.done("v", "ve")
        P.wait("y", "ve", fd)
        ydma(out_h[c * 128:(c + 1) * 128, :], outf[:, :], "dout")
    P.wait("y", "dout")
    if debug:
        P.wait("y", "dbo")

    P.replay()
    return nc


def kernel(**inputs):
    T_FULL = inputs["L0_attn_x"].shape[1]
    B = inputs["L0_attn_x"].shape[0]
    ncore = 8
    per_b = ncore // B
    T = T_FULL // per_b
    x = np.asarray(inputs["L0_attn_x"], np.float32)

    wmap = {
        "q1_w": "L0_attn_q1_w", "q1_b": "L0_attn_q1_b",
        "k1_w": "L0_attn_k1_w", "k1_b": "L0_attn_k1_b",
        "k2_w": "L0_attn_k2_w", "k2_b": "L0_attn_k2_b",
        "gw_w": "L0_attn_gw_w", "gw_b": "L0_attn_gw_b",
        "sw_w": "L0_attn_sw_w", "sw_b": "L0_attn_sw_b",
        "cp_w": "L0_attn_cp_w", "cp_b": "L0_attn_cp_b",
        "Wr": "L0_ffn_Wr", "router_bias": "L0_ffn_router_bias",
        "Wg": "L0_ffn_Wg", "Wu": "L0_ffn_Wu", "Wd": "L0_ffn_Wd",
        "conv_w": "L0_ffn_conv_w", "conv_b": "L0_ffn_conv_b",
    }
    weights = {k: np.ascontiguousarray(np.asarray(inputs[v], np.float32))
               for k, v in wmap.items()}

    in_maps = []
    for core in range(ncore):
        b = core // per_b
        q = core % per_b
        a0 = q * T
        xs = np.ascontiguousarray(x[b, a0:a0 + T, :])
        halo = np.zeros((4, x.shape[2]), np.float32)
        if a0 > 0:
            halo[0:3, :] = x[b, a0 - 3:a0, :]
        if a0 + T < T_FULL:
            halo[3, :] = x[b, a0 + T, :]
        pm = np.zeros((4, 1), np.float32)
        pm[:q, 0] = 1.0
        vm = np.zeros((4, 1), np.float32)
        if q > 0:
            vm[q - 1, 0] = 1.0
        m = {"x": xs, "xhalo": halo, "pmask": pm, "vmask": vm}
        m.update(weights)
        in_maps.append(m)

    nc = build(T, debug=False)
    res = run_bass_kernel_spmd(nc, in_maps, core_ids=list(range(ncore)))
    out = np.zeros((B, T_FULL, x.shape[2]), np.float32)
    for core in range(ncore):
        b = core // per_b
        q = core % per_b
        out[b, q * T:(q + 1) * T, :] = res.results[core]["out"]
    return out


# version 13 (final)
# speedup vs baseline: 1.1638x; 1.0176x over previous
import numpy as np
import concourse.bass as bass
import concourse.mybir as mybir
from concourse.bass_utils import run_bass_kernel_spmd

F32 = mybir.dt.float32
BF16 = mybir.dt.bfloat16
AF = mybir.ActivationFunctionType
ALU = mybir.AluOpType
AX = mybir.AxisListType

D = 1024
H = 16
DH = 64
NDC = D // 128
NP = H // 2
I_FFN = 512
NE = 4
CH = 128


class Rec:

    def __init__(self, nc):
        self.nc = nc
        self.ops = {"t": [], "v": [], "s": [], "g": [], "y": []}
        self.sems = {}
        self.cnt = {}
        self.kind = {"t": None, "v": None, "s": None, "g": None, "y": None}
        self.guard = {"t": None, "v": None, "s": None, "g": None, "y": None}

    def sem(self, name):
        if name not in self.sems:
            self.sems[name] = self.nc.alloc_semaphore(name)
            self.cnt[name] = 0
        return self.sems[name]

    def _op(self, eng, fn, kind="op"):
        if kind != "wait" and self.guard[eng] is not None:
            h, v = self.guard[eng]
            self.guard[eng] = None
            self.ops[eng].append(lambda e: e.wait_ge(h, v))
        else:
            self.guard[eng] = None
        self.ops[eng].append(fn)
        self.kind[eng] = kind

    def t(self, fn): self._op("t", fn)
    def v(self, fn): self._op("v", fn)
    def s(self, fn): self._op("s", fn)
    def g(self, fn): self._op("g", fn)
    def y(self, fn): self._op("y", fn)

    def inc(self, eng, fn, sem, n=1):
        h = self.sem(sem)
        self._op(eng, lambda e: fn(e).then_inc(h, n), "inc")
        self.cnt[sem] += n
        return self.cnt[sem]

    def done(self, eng, sem):
        h = self.sem(sem)
        self.cnt[sem] += 1
        if self.kind[eng] == "op":
            f = self.ops[eng][-1]
            self.ops[eng][-1] = (lambda e, f=f: f(e).then_inc(h, 1))
        else:
            self._op(eng, lambda e: e.drain().then_inc(h, 1))
        self.kind[eng] = "inc"
        self.guard[eng] = (h, self.cnt[sem])
        return self.cnt[sem]

    def wait(self, eng, sem, val=None):
        h = self.sem(sem)
        v = self.cnt[sem] if val is None else val
        self._op(eng, lambda e: e.wait_ge(h, v), "wait")

    def mark(self, sem):
        self.sem(sem)
        return self.cnt[sem]

    def replay(self):
        with self.nc.Block() as block:
            block.tensor(lambda e: [f(e) for f in self.ops["t"]])
            block.vector(lambda e: [f(e) for f in self.ops["v"]])
            block.scalar(lambda e: [f(e) for f in self.ops["s"]])
            block.gpsimd(lambda e: [f(e) for f in self.ops["g"]])
            block.sync(lambda e: [f(e) for f in self.ops["y"]])


def build(T, debug=False):
    NCH = T // CH
    TILE = min(512, T)
    NT = T // TILE
    CPT = TILE // CH
    HWs = T + 4
    nc = bass.Bass("TRN2", target_bir_lowering=False)

    def din(name, shape):
        return nc.dram_tensor(name, list(shape), F32, kind="ExternalInput").ap()

    x_h = din("x", (T, D))
    xhalo_h = din("xhalo", (4, D))
    pmask_h = din("pmask", (4, 1))
    vmask_h = din("vmask", (4, 1))
    wq_h = din("q1_w", (D, D)); bq_h = din("q1_b", (D,))
    wk_h = din("k1_w", (D, D)); bk_h = din("k1_b", (D,))
    wk2_h = din("k2_w", (D, D)); bk2_h = din("k2_b", (D,))
    gw_h = din("gw_w", (DH, 1)); gwb_h = din("gw_b", (1,))
    sw_h = din("sw_w", (DH, 1)); swb_h = din("sw_b", (1,))
    wcp_h = din("cp_w", (D, D)); bcp_h = din("cp_b", (D,))
    wr_h = din("Wr", (D, 8)); rb_h = din("router_bias", (8,))
    wg_h = din("Wg", (NE, D, I_FFN))
    wu_h = din("Wu", (NE, D, I_FFN))
    wd_h = din("Wd", (NE, I_FFN, D))
    cvw_h = din("conv_w", (4, 1, D)); cvb_h = din("conv_b", (D,))

    out_h = nc.dram_tensor("out", [T, D], F32, kind="ExternalOutput").ap()
    dbg = {}
    if debug:
        for nm, shape in [("dbg_xn", (T, D)), ("dbg_gc", (H, T)),
                          ("dbg_y", (T, D)), ("dbg_attn", (T, D)),
                          ("dbg_cw", (T, 8))]:
            dbg[nm] = nc.dram_tensor(nm, list(shape), F32, kind="ExternalOutput").ap()

    wq_bf = nc.dram_tensor("wq_bf", [D, D], BF16)
    wk_bf = nc.dram_tensor("wk_bf", [D, D], BF16)
    wk2_bf = nc.dram_tensor("wk2_bf", [D, D], BF16)
    wcp_bf = nc.dram_tensor("wcp_bf", [D, D], BF16)
    wg_bf = nc.dram_tensor("wg_bf", [NE, D, I_FFN], BF16)
    wu_bf = nc.dram_tensor("wu_bf", [NE, D, I_FFN], BF16)
    wd_bf = nc.dram_tensor("wd_bf", [NE, I_FFN, D], BF16)
    cc0i = nc.dram_tensor("cc0i", [H, 1], F32)
    cc0o = nc.dram_tensor("cc0o", [4 * H, 1], F32)
    cc1i = nc.dram_tensor("cc1i", [128, 512], F32)
    cc1o = nc.dram_tensor("cc1o", [512, 512], F32)
    cc2i = nc.dram_tensor("cc2i", [128, 520], F32)
    cc2o = nc.dram_tensor("cc2o", [512, 520], F32)
    GROUPS = [[0, 1, 2, 3], [4, 5, 6, 7]]

    SB = nc.alloc_sbuf_tensor
    PS = nc.alloc_psum_tensor
    xn_fm = SB("xn_fm", [128, NDC * HWs], BF16)
    xn_tm = SB("xn_tm", [128, NCH * D], BF16)
    y_tm = SB("y_tm", [128, max(NCH * D, 8192)], BF16)
    outacc = y_tm
    s1p = SB("s1p", [128, max(NCH * 512, NDC * D)], BF16)
    cpw_sb = s1p

    lgg = SB("lgg", [16, T], F32)
    rra = SB("rra", [16, T], BF16)
    gc_tm = SB("gc_tm", [128, NCH * 16], F32)
    a_tm = SB("a_tm", [128, NCH * 16], F32)

    id_bf = SB("id_bf", [128, 128], BF16)
    id_f32 = SB("id_f32", [128, 128], F32)
    tri_f32 = SB("tri_f32", [128, 128], F32)
    ones_bf = SB("ones_bf", [1, 512], BF16)
    ones_f32 = SB("ones_f32", [1, 128], F32)
    gwblk = SB("gwblk", [128, NDC * 16], BF16)
    swblk = SB("swblk", [128, NDC * 16], BF16)
    gwb_tmp = SB("gwb_tmp", [1, 2], BF16)
    gwb16 = SB("gwb16", [1, 16], BF16)
    swb16 = SB("swb16", [1, 16], BF16)
    bias_sb = SB("bias_sb", [1, D], BF16)
    bcp_sb = SB("bcp_sb", [1, D], BF16)
    rb_sb = SB("rb_sb", [1, 8], F32)
    cvb_sb = SB("cvb_sb", [128, 8], F32)
    cvw_sb = SB("cvw_sb", [128, 8 * 4], F32)
    pm_sb = SB("pm_sb", [4, 1], F32)
    pmT = SB("pmT", [1, 4], F32)
    vmT = SB("vmT", [1, 4], F32)
    pmbc = SB("pmbc", [128, 4], F32)
    vmbc = SB("vmbc", [128, 4], F32)
    xh_tm = SB("xh_tm", [4, D], BF16)
    gsum_blk = SB("gsum_blk", [4, 16], F32)
    off_col = SB("off_col", [16, 1], F32)

    s1in = SB("s1in", [128, 512], BF16)
    s2in = SB("s2in", [128, 512], BF16)
    vprev = SB("vprev", [128, 8], BF16)
    s1tot = SB("s1tot", [128, 512], BF16)
    s2tot = SB("s2tot", [128, 512], BF16)
    ccg = SB("ccg", [128, 520], F32)
    t2pack = ccg

    kt = SB("kt", [128, max(NDC * TILE, 4 * D, NDC * (TILE + 1))], BF16)
    qt = SB("qt", [128, max(NDC * TILE, 4096)], BF16)
    k2t = SB("k2t", [128, NDC * TILE], BF16)
    q2t = SB("q2t", [128, NDC * TILE], BF16)
    q2s = kt
    p_sb = SB("p_sb", [128, H * 128], BF16)
    av_sb = SB("av_sb", [128, D], BF16)
    ktm2 = SB("ktm2", [128, D], BF16)
    vs_sb = SB("vs_sb", [128, D], BF16)
    e_sb = SB("e_sb", [128, D], BF16)
    o2e = SB("o2e", [128, D], BF16)
    o2s = SB("o2s", [128, D], BF16)
    yf_sb = SB("yf_sb", [128, D], BF16)
    x_tm = SB("x_tm", [128, D], F32)
    stats = SB("stats", [128, 8], F32)
    sq_sb = SB("sq_sb", [128, D], F32)
    rgc_t = SB("rgc_t", [128, 16], F32)
    rr_t = SB("rr_t", [128, 16], F32)
    wr_sb = SB("wr_sb", [128, NDC * 8], BF16)
    biasbc = SB("biasbc", [128, 8], F32)
    sc_fm = SB("sc_fm", [8, 128], F32)
    z_t = SB("z_t", [128, 8], F32)
    z_srt = SB("z_srt", [128, 8], F32)
    z_m = SB("z_m", [128, 8], F32)
    z_rep = SB("z_rep", [128, 8], F32)
    cw_sb = SB("cw_sb", [128, NCH * 8], F32)
    cw46 = SB("cw46", [128, NCH], F32)
    cw7 = SB("cw7", [128, NCH], F32)
    wbuf = SB("wbuf", [128, 2 * NDC * 128], BF16)
    wbig = qt
    wdbuf = kt
    sg_sb = k2t
    hid = q2t
    CW = min(512, T)
    conv_fm = None
    outf = None

    conv_fm = vs_sb
    outf = sq_sb
    outf_dbg = SB("outf_dbg", [128, D], F32) if debug else None
    s1run = sq_sb
    s2run = sq_sb
    pf = [PS(f"pf{i}", [128, 512], F32) for i in range(6)]
    pb = [PS(f"pb{i}", [128, 1024], BF16) for i in range(2)]

    P = Rec(nc)

    def gdma(dst, src, sem="dg"):
        return P.inc("g", lambda e: e.dma_start(out=dst, in_=src), sem, 16)

    def ydma(dst, src, sem="dy"):
        return P.inc("y", lambda e: e.dma_start(out=dst, in_=src), sem, 16)

    def sdma(dst, src, sem="ds"):
        return P.inc("s", lambda e: e.dma_start(out=dst, in_=src), sem, 16)

    def mm(out, lhsT, rhs, start, stop, tp=None):
        P.t(lambda e: e.matmul(out, lhsT, rhs, start=start, stop=stop,
                               tile_position=tp))

    def mkconst(e):
        e.memset(id_bf[:, :], 0.0)
        e.memset(id_f32[:, :], 0.0)
        e.memset(tri_f32[:, :], 0.0)
        e.memset(gwblk[:, :], 0.0)
        e.memset(swblk[:, :], 0.0)
        e.drain()
        e.affine_select(out=id_bf[:, :], in_=id_bf[:, :], compare_op=ALU.not_equal,
                        fill=1.0, base=0, pattern=[[-1, 128]], channel_multiplier=1)
        e.affine_select(out=id_f32[:, :], in_=id_f32[:, :], compare_op=ALU.not_equal,
                        fill=1.0, base=0, pattern=[[-1, 128]], channel_multiplier=1)
        return e.affine_select(out=tri_f32[:, :], in_=tri_f32[:, :],
                                compare_op=ALU.is_gt, fill=1.0, base=0,
                                pattern=[[-1, 128]], channel_multiplier=1)
    P.g(mkconst)
    P.done("g", "pl")

    P.v(lambda e: e.memset(ones_bf[:, :], 1.0))
    P.v(lambda e: e.memset(ones_f32[:, :], 1.0))
    P.done("v", "ve")

    gdma(bcp_sb[:, :], bcp_h.rearrange("(a b) -> a b", a=1))
    gdma(gwb_tmp[:, 0:1], gwb_h.rearrange("(a b) -> a b", a=1))
    gdma(gwb_tmp[:, 1:2], swb_h.rearrange("(a b) -> a b", a=1))
    ydma(rb_sb[:, :], rb_h.rearrange("(a b) -> a b", a=1))
    ydma(pm_sb[:, :], pmask_h)
    ydma(pmT[:, :], pmask_h.rearrange("a b -> b a"))
    ydma(vmT[:, :], vmask_h.rearrange("a b -> b a"))
    for dc in range(NDC):
        ydma(cvb_sb[:, dc:dc + 1],
             cvb_h.rearrange("(a b) -> a b", a=NDC)[dc, :].unsqueeze(1))
        for k in range(4):
            ydma(cvw_sb[:, dc * 4 + k:dc * 4 + k + 1],
                 cvw_h[k, 0, dc * 128:(dc + 1) * 128].unsqueeze(1))
    P.wait("g", "pl")
    for dc in range(NDC):
        gdma(gwblk[0:64, dc * 16 + 2 * dc:dc * 16 + 2 * dc + 1], gw_h)
        gdma(gwblk[64:128, dc * 16 + 2 * dc + 1:dc * 16 + 2 * dc + 2], gw_h)
        gdma(swblk[0:64, dc * 16 + 2 * dc:dc * 16 + 2 * dc + 1], sw_h)
        gdma(swblk[64:128, dc * 16 + 2 * dc + 1:dc * 16 + 2 * dc + 2], sw_h)
    for dc in range(NDC):
        gdma(wr_sb[:, dc * 8:(dc + 1) * 8], wr_h[dc * 128:(dc + 1) * 128, :])

    P.wait("t", "dg")
    P.wait("t", "dy")
    P.wait("t", "ve")
    mm(pf[4][0:1, 0:16], gwb_tmp[:, 0:1], ones_bf[:, 0:16], True, True)
    mm(pf[4][0:1, 16:32], gwb_tmp[:, 1:2], ones_bf[:, 0:16], True, True)
    for r in range(4):
        mm(pf[4][0:128, 32 + r:33 + r], ones_f32[:, :], pmT[:, r:r + 1], True, True)
        mm(pf[4][0:128, 40 + r:41 + r], ones_f32[:, :], vmT[:, r:r + 1], True, True)
    mm(pf[4][0:128, 48:56], ones_f32[:, :], rb_sb[:, :], True, True)
    td = P.done("t", "te")
    P.wait("v", "te", td)
    P.v(lambda e: e.tensor_copy(out=gwb16[:, :], in_=pf[4][0:1, 0:16]))
    P.v(lambda e: e.tensor_copy(out=swb16[:, :], in_=pf[4][0:1, 16:32]))
    P.v(lambda e: e.tensor_copy(out=pmbc[:, :], in_=pf[4][0:128, 32:36]))
    P.v(lambda e: e.tensor_copy(out=vmbc[:, :], in_=pf[4][0:128, 40:44]))
    P.v(lambda e: e.tensor_copy(out=biasbc[:, :], in_=pf[4][0:128, 48:56]))
    P.done("v", "ve")
    P.wait("t", "ve")

    wunits = []
    for wsrc, wdst in [(wk_h, wk_bf.ap()), (wq_h, wq_bf.ap()),
                       (wk2_h, wk2_bf.ap()), (wcp_h, wcp_bf.ap())]:
        for p in range(8):
            wunits.append((wsrc[p * 128:(p + 1) * 128, :],
                           wdst[p * 128:(p + 1) * 128, :], D))
    for ee in range(NE):
        for p in range(8):
            wunits.append((wg_h[ee, p * 128:(p + 1) * 128, :],
                           wg_bf.ap()[ee, p * 128:(p + 1) * 128, :], I_FFN))
            wunits.append((wu_h[ee, p * 128:(p + 1) * 128, :],
                           wu_bf.ap()[ee, p * 128:(p + 1) * 128, :], I_FFN))
        for p in range(4):
            wunits.append((wd_h[ee, p * 128:(p + 1) * 128, :],
                           wd_bf.ap()[ee, p * 128:(p + 1) * 128, :], D))
    wstate = {"i": 0, "projmark": None}

    def emit_wunits(n):
        for _ in range(n):
            i = wstate["i"]
            if i >= len(wunits):
                return
            wstate["i"] = i + 1
            src, dst, w = wunits[i]
            par = i % 2
            f32v = y_tm[:, par * 4096:par * 4096 + 2048].bitcast(F32)[:, 0:w]
            bfv = y_tm[:, par * 4096 + 2048:par * 4096 + 2048 + w]
            if i >= 2:
                P.wait("y", "ac")
            lded = ydma(f32v, src, "dwl")
            P.wait("s", "dwl", lded)
            if i >= 2:
                P.wait("s", "dws")
            P.s(lambda e, f32v=f32v, bfv=bfv: e.activation(out=bfv, in_=f32v,
                                                           func=AF.Copy))
            cvd = P.done("s", "ac")
            P.wait("y", "ac", cvd)
            ydma(dst, bfv, "dws")
            if wstate["i"] == 32:
                wstate["projmark"] = P.mark("dws")

    def emit_a1_tile(tl):
        P.wait("t", "dg")
        P.wait("t", "ac")
        for dinc in range(NDC):
            mm(pf[4][0:16, 0:TILE], gwblk[:, dinc * 16:(dinc + 1) * 16],
               xn_fm[:, dinc * HWs + 3 + tl * TILE:dinc * HWs + 3 + (tl + 1) * TILE],
               start=(dinc == 0), stop=False)
        mm(pf[4][0:16, 0:TILE], gwb16[:, :], ones_bf[:, 0:TILE], False, True)
        tdone = P.done("t", "te")
        P.wait("s", "te", tdone)
        P.s(lambda e, tl=tl: e.activation(out=lgg[:, tl * TILE:(tl + 1) * TILE],
                                          in_=pf[4][0:16, 0:TILE], func=AF.Sigmoid))
        P.done("s", "ac")

    def ln_rows(xs, nrows, xn_out):
        P.s(lambda e: e.activation(out=sq_sb[0:nrows, :], in_=xs, func=AF.Square))
        P.done("s", "ac")
        P.wait("v", "ac")
        P.v(lambda e: e.reduce_sum(out=stats[0:nrows, 0:1], in_=xs, axis=AX.X))
        P.v(lambda e: e.reduce_sum(out=stats[0:nrows, 1:2], in_=sq_sb[0:nrows, :],
                                   axis=AX.X))
        P.v(lambda e: e.drain())
        P.v(lambda e: e.tensor_scalar(out=stats[0:nrows, 2:3], in0=stats[0:nrows, 0:1],
                                      scalar1=1.0 / D, scalar2=None, op0=ALU.mult))
        P.v(lambda e: e.drain())
        P.v(lambda e: e.tensor_tensor(out=stats[0:nrows, 3:4], in0=stats[0:nrows, 2:3],
                                      in1=stats[0:nrows, 2:3], op=ALU.mult))
        P.v(lambda e: e.drain())
        P.v(lambda e: e.scalar_tensor_tensor(out=stats[0:nrows, 4:5],
                                             in0=stats[0:nrows, 1:2],
                                             scalar=1.0 / D, in1=stats[0:nrows, 3:4],
                                             op0=ALU.mult, op1=ALU.subtract))
        P.v(lambda e: e.drain())
        P.v(lambda e: e.tensor_scalar_add(out=stats[0:nrows, 4:5],
                                          in0=stats[0:nrows, 4:5], scalar1=1e-5))
        P.done("v", "ve")
        P.wait("s", "ve")
        P.s(lambda e: e.activation(out=stats[0:nrows, 5:6], in_=stats[0:nrows, 4:5],
                                   func=AF.Ln))
        P.s(lambda e: e.drain())
        P.s(lambda e: e.activation(out=stats[0:nrows, 6:7], in_=stats[0:nrows, 5:6],
                                   func=AF.Exp, scale=-0.5))
        P.done("s", "ac")
        P.wait("v", "ac")
        P.v(lambda e: e.tensor_scalar(out=xn_out, in0=xs,
                                      scalar1=stats[0:nrows, 2:3],
                                      scalar2=stats[0:nrows, 6:7],
                                      op0=ALU.subtract, op1=ALU.mult))
        return P.done("v", "ve")

    G = min(4, NCH)
    NG = (NCH + G - 1) // G
    xslots = []
    for buf, lo in ((y_tm, 8192), (qt, 0), (k2t, 0), (q2t, 0)):
        k = lo
        while k + 2048 <= buf.shape[1] and len(xslots) < 2 * G:
            xslots.append(buf[:, k:k + 2048].bitcast(F32))
            k += 2048
    assert len(xslots) >= 2 * G
    amarks = {}
    for g in range(NG):
        c0 = g * G
        gs = min(G, NCH - c0)
        base = (g % 2) * 80
        sl = [xslots[(g % 2) * G + k] for k in range(gs)]
        if g >= 2:
            P.wait("y", "ve", amarks[g - 2])
        dxc = "dxa" if g % 2 == 0 else "dxb"
        for k in range(gs):
            ydma(sl[k][:, :], x_h[(c0 + k) * 128:(c0 + k + 1) * 128, :], dxc)
        dxm = P.mark(dxc)
        emit_wunits(14 if NG >= 4 else len(wunits))
        P.wait("v", dxc, dxm)
        for k in range(gs):
            P.v(lambda e, k=k, s=sl[k], base=base: e.bn_stats(
                out=sq_sb[:, base + k * 12:base + k * 12 + 6], in_=s[:, 0:512]))
            P.v(lambda e, k=k, s=sl[k], base=base: e.bn_stats(
                out=sq_sb[:, base + k * 12 + 6:base + k * 12 + 12],
                in_=s[:, 512:1024]))
        P.v(lambda e: e.drain())
        for k in range(gs):
            P.v(lambda e, k=k, base=base: e.bn_aggr(
                out=sq_sb[:, base + 48 + 2 * k:base + 50 + 2 * k],
                in_=sq_sb[:, base + k * 12:base + k * 12 + 12]))
        P.v(lambda e: e.drain())
        P.v(lambda e, gs=gs, base=base: e.tensor_scalar_add(
            out=sq_sb[:, base + 60:base + 60 + gs],
            in0=bass.AP(sq_sb, base + 49, [[sq_sb.shape[1], 128], [2, gs]]),
            scalar1=1e-5))
        vstat = P.done("v", "ve")
        P.wait("s", "ve", vstat)
        P.s(lambda e, gs=gs, base=base: e.activation(
            out=sq_sb[:, base + 64:base + 64 + gs],
            in_=sq_sb[:, base + 60:base + 60 + gs], func=AF.Ln))
        P.s(lambda e: e.drain())
        P.s(lambda e, gs=gs, base=base: e.activation(
            out=sq_sb[:, base + 64:base + 64 + gs],
            in_=sq_sb[:, base + 64:base + 64 + gs], func=AF.Exp, scale=-0.5))
        sac = P.done("s", "ac")
        P.wait("v", "ac", sac)
        for k in range(gs):
            P.v(lambda e, k=k, c=c0 + k, s=sl[k], base=base: e.tensor_scalar(
                out=xn_tm[:, c * D:(c + 1) * D], in0=s[:, 0:D],
                scalar1=sq_sb[:, base + 48 + 2 * k:base + 49 + 2 * k],
                scalar2=sq_sb[:, base + 64 + k:base + 65 + k],
                op0=ALU.subtract, op1=ALU.mult))
        amarks[g] = P.done("v", "ve")
        P.wait("t", "ve", amarks[g])
        if g == 0:
            P.wait("t", "pl")
        emarks = {}
        for k in range(gs):
            c = c0 + k
            if k >= 2:
                P.wait("t", "ve", emarks[k - 2])
            for dc in range(NDC):
                P.t(lambda e, dc=dc, c=c, k=k: e.transpose(
                    pb[k % 2][:, dc * 128:(dc + 1) * 128],
                    xn_tm[:, c * D + dc * 128:c * D + (dc + 1) * 128],
                    id_bf[:, :]))
            td = P.done("t", "te")
            P.wait("v", "te", td)
            P.v(lambda e, c=c, k=k: e.tensor_copy(
                out=bass.AP(xn_fm, 3 + c * 128,
                            [[NDC * HWs, 128], [HWs, NDC], [1, 128]]),
                in_=pb[k % 2][:, :].rearrange("p (a b) -> p a b", a=NDC)))
            emarks[k] = P.done("v", "ve")
        for c in range(c0, c0 + gs):
            if (c + 1) % CPT == 0:
                P.wait("t", "ve", emarks[gs - 1])
                emit_a1_tile((c + 1) // CPT - 1)

    P.wait("y", "ve")
    ydma(x_tm[0:4, :], xhalo_h)
    hld = P.mark("dy")
    P.wait("s", "dy", hld)
    P.wait("v", "dy", hld)
    hdone = ln_rows(x_tm[0:4, :], 4, xh_tm[:, :])
    P.wait("t", "ve", hdone)
    for dc in range(NDC):
        P.t(lambda e, dc=dc: e.transpose(pb[0][:, dc * 4:dc * 4 + 4],
                                         xh_tm[:, dc * 128:(dc + 1) * 128],
                                         id_bf[0:4, 0:4]))
    tdone = P.done("t", "te")
    P.wait("v", "te", tdone)
    for dc in range(NDC):
        P.v(lambda e, dc=dc: e.tensor_copy(out=xn_fm[:, dc * HWs:dc * HWs + 3],
                                           in_=pb[0][:, dc * 4:dc * 4 + 3]))
        P.v(lambda e, dc=dc: e.tensor_copy(
            out=xn_fm[:, dc * HWs + 3 + T:dc * HWs + 4 + T],
            in_=pb[0][:, dc * 4 + 3:dc * 4 + 4]))
    P.done("v", "ve")

    if debug:
        for c in range(NCH):
            P.wait("v", "dbo")
            P.v(lambda e, c=c: e.tensor_copy(out=outf_dbg[:, :],
                                             in_=xn_tm[:, c * D:(c + 1) * D]))
            dv = P.done("v", "ve")
            P.wait("y", "ve", dv)
            ydma(dbg["dbg_xn"][c * 128:(c + 1) * 128, :], outf_dbg[:, :], "dbo")

    def stream_proj(wsrc, bias_h, dst_fn, act=None, act_scale=1.0, tilebase=0):
        P.wait("g", "te")
        gdma(bias_sb[:, :], bias_h.rearrange("(a b) -> a b", a=1))
        temarks = {}
        acmarks = {}
        dsm = {}

        def emit_group(doutc):
            par = doutc % 2
            P.wait("t", "ve")
            if doutc >= 2:
                P.wait("t", "ac", acmarks[doutc - 2])
            P.wait("t", "dg")
            P.wait("t", "dsa" if par == 0 else "dsb", dsm[doutc])
            for dinc in range(NDC):
                mm(pf[par][:, 0:TILE],
                   wbuf[:, par * 1024 + dinc * 128:par * 1024 + (dinc + 1) * 128],
                   xn_fm[:, dinc * HWs + 3 + tilebase * TILE:
                         dinc * HWs + 3 + (tilebase + 1) * TILE],
                   start=(dinc == 0), stop=False)
            mm(pf[par][:, 0:TILE], bias_sb[:, doutc * 128:(doutc + 1) * 128],
               ones_bf[:, 0:TILE], start=False, stop=True)
            temarks[doutc] = P.done("t", "te")
            P.wait("s", "te", temarks[doutc])
            tgt = dst_fn(doutc)
            if act is None:
                P.s(lambda e, tgt=tgt, par=par: e.activation(
                    out=tgt, in_=pf[par][:, 0:TILE], func=AF.Copy))
            else:
                P.s(lambda e, tgt=tgt, par=par, a=act, sc=act_scale: e.activation(
                    out=tgt, in_=pf[par][:, 0:TILE], func=a, scale=sc))
            acmarks[doutc] = P.done("s", "ac")

        for doutc in range(NDC):
            par = doutc % 2
            if doutc >= 2:
                P.wait("s", "te", temarks[doutc - 2])
            sdma(wbuf[:, par * 1024:(par + 1) * 1024],
                 bass.AP(wsrc.tensor, doutc * 128,
                         [[D, 128], [128 * D, NDC], [1, 128]]),
                 "dsa" if par == 0 else "dsb")
            dsm[doutc] = P.mark("dsa" if par == 0 else "dsb")
            if doutc >= 1:
                emit_group(doutc - 1)
        emit_group(NDC - 1)
        P.wait("t", "ac")

    P.wait("v", "ac")
    P.v(lambda e: e.tensor_scalar_max(out=lgg[:, :], in0=lgg[:, :], scalar1=1e-6))
    P.done("v", "ve")
    P.wait("s", "ve")
    P.s(lambda e: e.activation(out=lgg[:, :], in_=lgg[:, :], func=AF.Ln))
    P.done("s", "ac")
    P.wait("v", "ac")
    P.v(lambda e: e.tensor_tensor_scan(out=lgg[:, :], data0=lgg[:, :],
                                       data1=lgg[:, :], initial=0.0,
                                       op0=ALU.add, op1=ALU.bypass))
    P.done("v", "ve")

    P.wait("y", "ve")
    ydma(cc0i[:, :], lgg[:, T - 1:T])
    P.wait("g", "dy")
    P.inc("g", lambda e: e.collective_compute(
        "AllGather", ALU.bypass, replica_groups=GROUPS,
        ins=[cc0i[:, :]], outs=[cc0o[:, :]]), "cc", 1)
    P.wait("g", "cc")
    gdma(gsum_blk[:, :], cc0o.ap().rearrange("(r h) one -> r (h one)", r=4), "dgc")
    P.wait("t", "dgc")
    mm(pf[4][0:16, 0:1], gsum_blk[:, :], pm_sb[:, :], True, True)
    tdone = P.done("t", "te")
    P.wait("v", "te", tdone)
    P.v(lambda e: e.tensor_copy(out=off_col[:, :], in_=pf[4][0:16, 0:1]))
    P.v(lambda e: e.drain())
    P.v(lambda e: e.tensor_scalar(out=lgg[:, :], in0=lgg[:, :],
                                  scalar1=off_col[:, :], scalar2=30.0,
                                  op0=ALU.add, op1=ALU.min))
    P.v(lambda e: e.drain())
    P.v(lambda e: e.tensor_scalar_max(out=lgg[:, :], in0=lgg[:, :], scalar1=-30.0))
    P.done("v", "ve")
    P.wait("s", "ve")
    P.s(lambda e: e.activation(out=lgg[:, :], in_=lgg[:, :], func=AF.Exp))
    P.done("s", "ac")
    P.wait("v", "ac")
    P.v(lambda e: e.tensor_scalar_add(out=lgg[:, :], in0=lgg[:, :], scalar1=1e-6))
    gcd = P.done("v", "ve")
    if debug:
        P.wait("y", "ve", gcd)
        ydma(dbg["dbg_gc"][:, :], lgg[:, :], "dbo")

    P.wait("s", "dws", wstate["projmark"])
    for tl in range(NT):
        stream_proj(wk_bf.ap(), bk_h, lambda dc: kt[:, dc * TILE:(dc + 1) * TILE],
                    tilebase=tl)
        P.wait("t", "ve")
        for dinc in range(NDC):
            mm(pf[4][0:16, 0:TILE], swblk[:, dinc * 16:(dinc + 1) * 16],
               kt[:, dinc * TILE:(dinc + 1) * TILE],
               start=(dinc == 0), stop=False)
        mm(pf[4][0:16, 0:TILE], swb16[:, :], ones_bf[:, 0:TILE], False, True)
        tdone = P.done("t", "te")
        P.wait("s", "te", tdone)
        P.s(lambda e, tl=tl: e.activation(out=rra[:, tl * TILE:(tl + 1) * TILE],
                                          in_=pf[4][0:16, 0:TILE], func=AF.Silu))
        rd = P.done("s", "ac")
        P.wait("t", "ac", rd)

        for cc_i in range(CPT):
            c = tl * CPT + cc_i
            emit_wunits(4)
            P.wait("t", "ve")
            P.t(lambda e, c=c: e.transpose(pf[4][:, 0:16],
                                           lgg[:, c * 128:(c + 1) * 128],
                                           id_f32[0:16, 0:16]))
            P.t(lambda e, c=c: e.transpose(pb[1][:, 0:16],
                                           rra[:, c * 128:(c + 1) * 128],
                                           id_bf[0:16, 0:16]))
            tdone = P.done("t", "te")
            P.wait("v", "te", tdone)
            P.v(lambda e, c=c: e.tensor_copy(out=gc_tm[:, c * 16:(c + 1) * 16],
                                             in_=pf[4][:, 0:16]))
            P.v(lambda e: e.tensor_copy(out=rr_t[:, :], in_=pb[1][:, 0:16]))
            P.v(lambda e: e.drain())
            P.v(lambda e, c=c: e.reciprocal(out=rgc_t[:, :],
                                            in_=gc_tm[:, c * 16:(c + 1) * 16]))
            P.v(lambda e: e.drain())
            P.v(lambda e, c=c: e.tensor_tensor(out=a_tm[:, c * 16:(c + 1) * 16],
                                               in0=rr_t[:, :], in1=rgc_t[:, :],
                                               op=ALU.mult))
            P.v(lambda e: e.drain())
            for h in range(H):
                P.v(lambda e, c=c, h=h: e.tensor_scalar_mul(
                    out=av_sb[:, h * 64:(h + 1) * 64],
                    in0=xn_tm[:, c * D + h * 64:c * D + (h + 1) * 64],
                    scalar1=a_tm[:, c * 16 + h:c * 16 + h + 1]))
            avd = P.done("v", "ve")
            P.wait("t", "ve", avd)
            for dc in range(NDC):
                P.t(lambda e, cc_i=cc_i, dc=dc: e.transpose(
                    pb[0][:, dc * 128:(dc + 1) * 128],
                    kt[:, dc * TILE + cc_i * 128:dc * TILE + (cc_i + 1) * 128],
                    id_bf[:, :]))
            tdone = P.done("t", "te")
            P.wait("v", "te", tdone)
            P.v(lambda e: e.tensor_copy(out=ktm2[:, :], in_=pb[0][:, :]))
            kd = P.done("v", "ve")
            P.wait("t", "ve", kd)
            for h in range(H):
                hp, hh = h // 2, h % 2
                mm(pf[0][hh * 64:(hh + 1) * 64, hp * 64:(hp + 1) * 64],
                   ktm2[:, hp * 128 + hh * 64:hp * 128 + (hh + 1) * 64],
                   av_sb[:, h * 64:(h + 1) * 64],
                   start=True, stop=True, tp=(0, hh * 64))
            tdone = P.done("t", "te")
            P.wait("v", "te", tdone)
            if c == 0:
                P.v(lambda e: e.tensor_copy(out=s1run[:, 0:512], in_=pf[0][:, :]))
            else:
                P.v(lambda e: e.tensor_tensor(out=s1run[:, 0:512],
                                              in0=s1run[:, 0:512],
                                              in1=pf[0][:, :], op=ALU.add))
            P.v(lambda e: e.drain())
            P.v(lambda e, c=c: e.tensor_copy(out=s1p[:, c * 512:(c + 1) * 512],
                                             in_=s1run[:, 0:512]))
            P.done("v", "ve")

    emit_wunits(len(wunits))
    vd = P.mark("ve")
    P.wait("y", "ve", vd)
    ydma(cc1i[:, :], s1run[:, 0:512])
    P.wait("g", "dy")
    P.inc("g", lambda e: e.collective_compute(
        "AllGather", ALU.bypass, replica_groups=GROUPS,
        ins=[cc1i[:, :]], outs=[cc1o[:, :]]), "cc", 1)
    ccend = P.mark("cc")
    P.wait("v", "ve")
    P.v(lambda e: e.memset(x_tm[:, 0:512], 0.0))
    P.done("v", "ve")
    for r in range(4):
        P.wait("g", "cc", ccend)
        P.wait("g", "ve")
        ld = gdma(ccg[:, 0:512], cc1o[r * 128:(r + 1) * 128, :], "dgc")
        P.wait("v", "dgc", ld)
        P.v(lambda e, r=r: e.scalar_tensor_tensor(
            out=x_tm[:, 0:512], in0=ccg[:, 0:512],
            scalar=pmbc[:, r:r + 1], in1=x_tm[:, 0:512],
            op0=ALU.mult, op1=ALU.add))
        P.done("v", "ve")
    P.v(lambda e: e.tensor_copy(out=s1in[:, :], in_=x_tm[:, 0:512]))
    P.done("v", "ve")
    P.wait("t", "ve")

    P.wait("v", "dws")
    for tl in range(NT):
        stream_proj(wq_bf.ap(), bq_h, lambda dc: qt[:, dc * TILE:(dc + 1) * TILE],
                    tilebase=tl)
        stream_proj(wk2_bf.ap(), bk2_h, lambda dc: k2t[:, dc * TILE:(dc + 1) * TILE],
                    act=AF.Sigmoid, act_scale=0.02 / 32.0, tilebase=tl)
        stream_proj(wk_bf.ap(), bk_h, lambda dc: kt[:, dc * TILE:(dc + 1) * TILE],
                    tilebase=tl)
        P.wait("v", "ac")
        P.v(lambda e: e.tensor_scalar(out=q2t[:, :], in0=qt[:, 0:NDC * TILE], scalar1=0.0,
                                      scalar2=0.98 / 8.0, op0=ALU.max, op1=ALU.mult))
        P.v(lambda e: e.drain())
        P.v(lambda e: e.scalar_tensor_tensor(out=q2t[:, :], in0=qt[:, 0:NDC * TILE],
                                             scalar=0.125, in1=q2t[:, :],
                                             op0=ALU.mult, op1=ALU.subtract))
        P.done("v", "ve")

        for cc_i in range(CPT):
            c = tl * CPT + cc_i

            def qsl(buf, h, cc_i=cc_i):
                hp, hh = h // 2, h % 2
                return buf[hh * 64:(hh + 1) * 64,
                           hp * TILE + cc_i * 128:hp * TILE + (cc_i + 1) * 128]

            P.wait("v", "te")
            if c == 0:
                P.v(lambda e: e.tensor_copy(out=s1tot[:, :], in_=s1in[:, :]))
                P.v(lambda e: e.memset(s2tot[:, :], 0.0))
            else:
                P.v(lambda e, c=c: e.tensor_tensor(
                    out=s1tot[:, :], in0=s1in[:, :],
                    in1=s1p[:, (c - 1) * 512:c * 512], op=ALU.add))
                P.v(lambda e: e.tensor_copy(out=s2tot[:, :], in_=s2run[:, 512:1024]))
            for h in range(H):
                P.v(lambda e, c=c, h=h: e.tensor_scalar_mul(
                    out=av_sb[:, h * 64:(h + 1) * 64],
                    in0=xn_tm[:, c * D + h * 64:c * D + (h + 1) * 64],
                    scalar1=a_tm[:, c * 16 + h:c * 16 + h + 1]))
            pre = P.done("v", "ve")
            P.wait("y", "ve", pre)
            ydma(vs_sb[0:127, :], xn_tm[1:128, c * D:(c + 1) * D])
            if c < NCH - 1:
                ydma(vs_sb[127:128, :], xn_tm[0:1, (c + 1) * D:(c + 1) * D + D])
            else:
                ydma(vs_sb[127:128, :], xh_tm[3:4, :])
            vsld = P.mark("dy")

            def rec_pass(kbuf, qbuf, rhs_for_o, stot, obank_off):
                for h in range(H):
                    hp, hh = h // 2, h % 2
                    mm(pf[h % 4][:, (h // 4) * 128:(h // 4) * 128 + 128],
                       qsl(kbuf, h), qsl(qbuf, h),
                       start=True, stop=True, tp=(hh * 64, 0))
                tdone = P.done("t", "te")
                P.wait("v", "te", tdone)
                for h in range(H):
                    P.v(lambda e, h=h: e.tensor_tensor(
                        out=p_sb[:, h * 128:(h + 1) * 128],
                        in0=pf[h % 4][:, (h // 4) * 128:(h // 4) * 128 + 128],
                        in1=tri_f32[:, :], op=ALU.mult))
                pd = P.done("v", "ve")
                P.wait("t", "ve", pd)
                for h in range(H):
                    hp, hh = h // 2, h % 2
                    bank = pf[2 + h // 8]
                    col = (h % 8) * 64
                    mm(bank[:, col:col + 64], p_sb[:, h * 128:(h + 1) * 128],
                       rhs_for_o[:, h * 64:(h + 1) * 64],
                       start=True, stop=False)
                    mm(bank[:, col:col + 64], qsl(qbuf, h),
                       stot[hh * 64:(hh + 1) * 64, hp * 64:(hp + 1) * 64],
                       start=False, stop=True, tp=(hh * 64, 0))
                return P.done("t", "te")

            P.wait("t", "ve", pre)
            o1done = rec_pass(kt, qt, av_sb, s1tot, 0)
            P.wait("v", "te", o1done)
            for h in range(H):
                bank = pf[2 + h // 8]
                col = (h % 8) * 64
                P.v(lambda e, c=c, h=h, bank=bank, col=col: e.tensor_scalar_mul(
                    out=y_tm[:, c * D + h * 64:c * D + (h + 1) * 64],
                    in0=bank[:, col:col + 64],
                    scalar1=gc_tm[:, c * 16 + h:c * 16 + h + 1]))
            P.v(lambda e: e.drain())
            P.wait("v", "dy", vsld)
            P.v(lambda e, c=c: e.tensor_tensor(out=e_sb[:, :], in0=vs_sb[:, :],
                                               in1=y_tm[:, c * D:(c + 1) * D],
                                               op=ALU.subtract))
            ed = P.done("v", "ve")
            P.wait("t", "ve", ed)
            for dc in range(NDC):
                P.t(lambda e, dc=dc, cc_i=cc_i: e.transpose(
                    pb[0][:, dc * 128:(dc + 1) * 128],
                    k2t[:, dc * TILE + cc_i * 128:dc * TILE + (cc_i + 1) * 128],
                    id_bf[:, :]))
            tdone = P.done("t", "te")
            P.wait("v", "te", tdone)
            P.v(lambda e: e.tensor_copy(out=ktm2[:, :], in_=pb[0][:, :]))
            k2d = P.done("v", "ve")
            P.wait("t", "ve", k2d)
            o2done = rec_pass(k2t, q2t, e_sb, s2tot, 0)
            for h in range(H):
                hp, hh = h // 2, h % 2
                mm(pf[5][hh * 64:(hh + 1) * 64, hp * 64:(hp + 1) * 64],
                   ktm2[:, hp * 128 + hh * 64:hp * 128 + (hh + 1) * 64],
                   e_sb[:, h * 64:(h + 1) * 64],
                   start=True, stop=True, tp=(0, hh * 64))
            st2d = P.done("t", "te")
            P.wait("v", "te", st2d)
            if c == 0:
                P.v(lambda e: e.tensor_copy(out=s2run[:, 512:1024], in_=pf[5][:, :]))
            else:
                P.v(lambda e: e.tensor_tensor(out=s2run[:, 512:1024],
                                              in0=s2run[:, 512:1024],
                                              in1=pf[5][:, :], op=ALU.add))
            P.done("v", "ve")
            if c > 0:
                ydma(o2s[0:1, :], o2e[127:128, :], "dys")
            rowd = P.mark("dys")
            P.wait("v", "te", o2done)
            P.wait("v", "dys", rowd)
            if c == 0:
                P.v(lambda e: e.memset(o2s[0:1, :], 0.0))
                P.v(lambda e: e.drain())
            P.v(lambda e: e.tensor_copy(out=o2e[:, 0:512], in_=pf[2][:, :]))
            P.v(lambda e: e.tensor_copy(out=o2e[:, 512:1024], in_=pf[3][:, :]))
            o2ed = P.done("v", "ve")
            P.wait("y", "ve", o2ed)
            ydma(o2s[1:128, :], o2e[0:127, :], "dys")
            shd = P.mark("dys")
            P.wait("v", "dys", shd)
            P.v(lambda e, c=c: e.tensor_tensor(out=y_tm[:, c * D:(c + 1) * D],
                                               in0=y_tm[:, c * D:(c + 1) * D],
                                               in1=o2s[:, :], op=ALU.add))
            P.done("v", "ve")

    P.wait("v", "te")
    P.v(lambda e: e.tensor_copy(out=t2pack[:, 0:512], in_=s2run[:, 512:1024]))
    for dc in range(NDC):
        P.v(lambda e, dc=dc: e.tensor_copy(
            out=t2pack[:, 512 + dc:513 + dc],
            in_=q2t[:, dc * TILE + TILE - 1:dc * TILE + TILE]))
    vd = P.done("v", "ve")
    P.wait("y", "ve", vd)
    ydma(cc2i[:, :], t2pack[:, :])
    P.wait("g", "dy")
    P.inc("g", lambda e: e.collective_compute(
        "AllGather", ALU.bypass, replica_groups=GROUPS,
        ins=[cc2i[:, :]], outs=[cc2o[:, :]]), "cc", 1)
    ccend = P.mark("cc")
    P.wait("v", "ve")
    P.v(lambda e: e.memset(x_tm[:, 0:520], 0.0))
    P.done("v", "ve")
    for r in range(4):
        P.wait("g", "cc", ccend)
        P.wait("g", "ve")
        ld = gdma(ccg[:, 0:520], cc2o[r * 128:(r + 1) * 128, :], "dgc")
        P.wait("v", "dgc", ld)
        P.v(lambda e, r=r: e.scalar_tensor_tensor(
            out=x_tm[:, 0:512], in0=ccg[:, 0:512],
            scalar=pmbc[:, r:r + 1], in1=x_tm[:, 0:512],
            op0=ALU.mult, op1=ALU.add))
        P.v(lambda e, r=r: e.scalar_tensor_tensor(
            out=x_tm[:, 512:520], in0=ccg[:, 512:520],
            scalar=vmbc[:, r:r + 1], in1=x_tm[:, 512:520],
            op0=ALU.mult, op1=ALU.add))
        P.done("v", "ve")
    P.v(lambda e: e.tensor_copy(out=s2in[:, :], in_=x_tm[:, 0:512]))
    P.v(lambda e: e.tensor_copy(out=vprev[:, :], in_=x_tm[:, 512:520]))
    P.done("v", "ve")

    if debug:
        for c in range(NCH):
            P.wait("v", "dbo")
            P.v(lambda e, c=c: e.tensor_copy(out=outf_dbg[:, :],
                                             in_=y_tm[:, c * D:(c + 1) * D]))
            dv = P.done("v", "ve")
            P.wait("y", "ve", dv)
            ydma(dbg["dbg_y"][c * 128:(c + 1) * 128, :], outf_dbg[:, :], "dbo")

    P.wait("s", "ve")
    sdma(bass.AP(cpw_sb, 0, [[cpw_sb.shape[1], 128], [D, NDC], [1, D]]),
         bass.AP(wcp_bf, 0, [[D, 128], [128 * D, NDC], [1, D]]))
    P.wait("t", "ve")
    for hh in range(2):
        bank = pf[4 + hh]
        for hp in range(NP):
            mm(bank[0:1, hp * 64:(hp + 1) * 64],
               vprev[hh * 64:(hh + 1) * 64, hp:hp + 1],
               s2in[hh * 64:(hh + 1) * 64, hp * 64:(hp + 1) * 64],
               start=True, stop=True, tp=(hh * 64, 0))
    tdone = P.done("t", "te")
    P.wait("v", "te", tdone)
    for hh in range(2):
        bank = pf[4 + hh]
        P.v(lambda e, hh=hh, bank=bank: e.tensor_tensor(
            out=bass.AP(y_tm, hh * 64, [[y_tm.shape[1], 1], [128, 8], [1, 64]]),
            in0=bass.AP(y_tm, hh * 64, [[y_tm.shape[1], 1], [128, 8], [1, 64]]),
            in1=bank[0:1, 0:512].rearrange("p (a b) -> p a b", a=8), op=ALU.add))
        P.v(lambda e: e.drain())
    P.done("v", "ve")
    P.wait("t", "ve")

    for tl in range(NT):
        stream_proj(wq_bf.ap(), bq_h, lambda dc: qt[:, dc * TILE:(dc + 1) * TILE],
                    tilebase=tl)
        P.wait("v", "ac")
        for dc in range(NDC):
            if tl == 0:
                P.v(lambda e, dc=dc: e.memset(
                    q2s[:, dc * (TILE + 1):dc * (TILE + 1) + 1], 0.0))
            else:
                P.v(lambda e, dc=dc: e.tensor_copy(
                    out=q2s[:, dc * (TILE + 1):dc * (TILE + 1) + 1],
                    in_=q2s[:, dc * (TILE + 1) + TILE:dc * (TILE + 1) + TILE + 1]))
        P.v(lambda e: e.drain())
        P.v(lambda e: e.tensor_scalar(out=q2t[:, :], in0=qt[:, 0:NDC * TILE], scalar1=0.0,
                                      scalar2=0.98 / 8.0, op0=ALU.max, op1=ALU.mult))
        P.v(lambda e: e.drain())
        for dc in range(NDC):
            P.v(lambda e, dc=dc: e.scalar_tensor_tensor(
                out=q2s[:, dc * (TILE + 1) + 1:dc * (TILE + 1) + 1 + TILE],
                in0=qt[:, dc * TILE:(dc + 1) * TILE], scalar=0.125,
                in1=q2t[:, dc * TILE:(dc + 1) * TILE],
                op0=ALU.mult, op1=ALU.subtract))
        q2d = P.done("v", "ve")
        P.wait("t", "ve", q2d)
        P.wait("t", "ds")
        for cc_i in range(CPT):
            c = tl * CPT + cc_i
            for hh in range(2):
                bank = pf[2 + hh]
                for hp in range(NP):
                    mm(bank[:, hp * 64:(hp + 1) * 64],
                       q2s[hh * 64:(hh + 1) * 64,
                           hp * (TILE + 1) + cc_i * 128:
                           hp * (TILE + 1) + cc_i * 128 + 128],
                       s2in[hh * 64:(hh + 1) * 64, hp * 64:(hp + 1) * 64],
                       start=True, stop=True, tp=(hh * 64, 0))
            i2done = P.done("t", "te")
            P.wait("v", "te", i2done)
            for hh in range(2):
                bank = pf[2 + hh]
                P.v(lambda e, c=c, hh=hh, bank=bank: e.tensor_tensor(
                    out=bass.AP(y_tm, c * D + hh * 64,
                                [[y_tm.shape[1], 128], [128, 8], [1, 64]]),
                    in0=bass.AP(y_tm, c * D + hh * 64,
                                [[y_tm.shape[1], 128], [128, 8], [1, 64]]),
                    in1=bank[:, 0:512].rearrange("p (a b) -> p a b", a=8),
                    op=ALU.add))
                P.v(lambda e: e.drain())
            yd = P.done("v", "ve")
            P.wait("t", "ve", yd)
            for dc in range(NDC):
                P.t(lambda e, c=c, dc=dc: e.transpose(
                    pb[0][:, dc * 128:(dc + 1) * 128],
                    y_tm[:, c * D + dc * 128:c * D + (dc + 1) * 128], id_bf[:, :]))
            tdone = P.done("t", "te")
            P.wait("v", "te", tdone)
            P.v(lambda e: e.tensor_copy(out=yf_sb[:, :], in_=pb[0][:, :]))
            yfd = P.done("v", "ve")
            P.wait("y", "ve", yfd)
            ydma(x_tm[:, :], x_h[c * 128:(c + 1) * 128, :])
            xld = P.mark("dy")
            P.wait("t", "ve", yfd)
            for dc in range(NDC):
                for dg in range(2):
                    mm(pf[dg][:, :], yf_sb[:, dc * 128:(dc + 1) * 128],
                       cpw_sb[:, dc * D + dg * 512:dc * D + (dg + 1) * 512],
                       start=(dc == 0), stop=False)
            for dg in range(2):
                mm(pf[dg][:, :], ones_bf[:, 0:128],
                   bcp_sb[:, dg * 512:(dg + 1) * 512], start=False, stop=True)
            cpd = P.done("t", "te")
            P.wait("v", "te", cpd)
            P.wait("v", "dy", xld)
            for dg in range(2):
                P.v(lambda e, c=c, dg=dg: e.tensor_tensor(
                    out=outacc[:, c * D + dg * 512:c * D + (dg + 1) * 512],
                    in0=pf[dg][:, :],
                    in1=x_tm[:, dg * 512:(dg + 1) * 512], op=ALU.add))
            vd2 = P.done("v", "ve")
            P.wait("t", "ve", vd2)
            P.wait("y", "ve", vd2)

    if debug:
        for c in range(NCH):
            P.wait("v", "dbo")
            P.v(lambda e, c=c: e.tensor_copy(out=outf_dbg[:, :],
                                             in_=outacc[:, c * D:(c + 1) * D]))
            dv = P.done("v", "ve")
            P.wait("y", "ve", dv)
            ydma(dbg["dbg_attn"][c * 128:(c + 1) * 128, :], outf_dbg[:, :], "dbo")

    for c in range(NCH):
        P.wait("t", "ve")
        P.wait("t", "ac")
        for dinc in range(NDC):
            mm(pf[4][0:8, 0:128], wr_sb[:, dinc * 8:(dinc + 1) * 8],
               xn_fm[:, dinc * HWs + 3 + c * 128:dinc * HWs + 3 + (c + 1) * 128],
               start=(dinc == 0), stop=(dinc == NDC - 1))
        tdone = P.done("t", "te")
        P.wait("v", "te", tdone)
        P.v(lambda e: e.tensor_copy(out=sc_fm[:, :], in_=pf[4][0:8, 0:128]))
        scd = P.done("v", "ve")
        P.wait("t", "ve", scd)
        P.t(lambda e: e.transpose(pf[4][:, 0:8], sc_fm[:, :], id_f32[0:8, 0:8]))
        td2 = P.done("t", "te")
        P.wait("s", "te", td2)
        P.s(lambda e: e.activation(out=z_t[:, :], in_=pf[4][:, 0:8], func=AF.Sigmoid))
        sd2 = P.done("s", "ac")
        P.wait("t", "ac", sd2)
        P.wait("v", "ac", sd2)
        P.v(lambda e: e.tensor_tensor(out=z_m[:, :], in0=z_t[:, :],
                                      in1=biasbc[:, :], op=ALU.add))
        P.v(lambda e: e.drain())
        P.v(lambda e: e.max(out=z_srt[:, :], in_=z_m[:, :]))
        P.v(lambda e: e.memset(z_rep[:, :], -3e38))
        P.v(lambda e: e.drain())
        P.v(lambda e: e.tensor_copy(out=z_rep[:, 0:2], in_=z_srt[:, 0:2]))
        P.v(lambda e: e.drain())
        P.v(lambda e: e.match_replace(out=z_srt[:, :], in_to_replace=z_rep[:, :],
                                      in_values=z_m[:, :], imm_value=-3e38))
        P.v(lambda e: e.drain())
        P.v(lambda e: e.tensor_tensor(out=z_rep[:, :], in0=z_m[:, :],
                                      in1=z_srt[:, :], op=ALU.not_equal))
        P.v(lambda e: e.drain())
        P.v(lambda e: e.tensor_tensor(out=z_rep[:, :], in0=z_rep[:, :],
                                      in1=z_t[:, :], op=ALU.mult))
        P.v(lambda e: e.drain())
        P.v(lambda e: e.reduce_sum(out=z_m[:, 0:1], in_=z_rep[:, :], axis=AX.X))
        P.v(lambda e: e.drain())
        P.v(lambda e: e.tensor_scalar_add(out=z_m[:, 0:1], in0=z_m[:, 0:1],
                                          scalar1=1e-9))
        P.v(lambda e: e.drain())
        P.v(lambda e: e.reciprocal(out=z_m[:, 1:2], in_=z_m[:, 0:1]))
        P.v(lambda e: e.drain())
        P.v(lambda e, c=c: e.tensor_scalar_mul(out=cw_sb[:, c * 8:(c + 1) * 8],
                                               in0=z_rep[:, :], scalar1=z_m[:, 1:2]))
        P.v(lambda e: e.drain())
        P.v(lambda e, c=c: e.tensor_tensor(out=cw46[:, c:c + 1],
                                           in0=cw_sb[:, c * 8 + 4:c * 8 + 5],
                                           in1=cw_sb[:, c * 8 + 6:c * 8 + 7],
                                           op=ALU.add))
        P.v(lambda e, c=c: e.tensor_copy(out=cw7[:, c:c + 1],
                                         in_=cw_sb[:, c * 8 + 7:c * 8 + 8]))
        P.done("v", "ve")
    if debug:
        for c in range(NCH):
            P.wait("v", "dbo")
            P.v(lambda e, c=c: e.tensor_copy(out=outf_dbg[:, 0:8],
                                             in_=cw_sb[:, c * 8:(c + 1) * 8]))
            dv = P.done("v", "ve")
            P.wait("y", "ve", dv)
            ydma(dbg["dbg_cw"][c * 128:(c + 1) * 128, :], outf_dbg[:, 0:8], "dbo")

    NW = T // CW
    for dc in range(NDC):
        base = dc * HWs
        for tw in range(NW):
            P.wait("v", "ve")
            P.v(lambda e, base=base, dc=dc, tw=tw: e.tensor_scalar(
                out=conv_fm[:, 0:CW],
                in0=xn_fm[:, base + tw * CW:base + (tw + 1) * CW],
                scalar1=cvw_sb[:, dc * 4:dc * 4 + 1], scalar2=cvb_sb[:, dc:dc + 1],
                op0=ALU.mult, op1=ALU.add))
            for k in range(1, 4):
                P.v(lambda e: e.drain())
                P.v(lambda e, base=base, dc=dc, k=k, tw=tw: e.scalar_tensor_tensor(
                    out=conv_fm[:, 0:CW],
                    in0=xn_fm[:, base + tw * CW + k:base + (tw + 1) * CW + k],
                    scalar=cvw_sb[:, dc * 4 + k:dc * 4 + k + 1],
                    in1=conv_fm[:, 0:CW], op0=ALU.mult, op1=ALU.add))
            cvd = P.done("v", "ve")
            P.wait("t", "ve", cvd)
            for c4 in range(CW // 128):
                c = tw * (CW // 128) + c4
                par = c4 % 2
                P.wait("t", "ve")
                P.t(lambda e, c4=c4, par=par: e.transpose(
                    pb[par][:, 0:128], conv_fm[:, c4 * 128:(c4 + 1) * 128],
                    id_bf[:, :]))
                td3 = P.done("t", "te")
                P.wait("v", "te", td3)
                P.v(lambda e, c=c, dc=dc, par=par: e.tensor_tensor(
                    out=outacc[:, c * D + dc * 128:c * D + (dc + 1) * 128],
                    in0=outacc[:, c * D + dc * 128:c * D + (dc + 1) * 128],
                    in1=pb[par][:, 0:128], op=ALU.add))
                P.done("v", "ve")

    P.wait("v", "ve")
    for c in range(NCH):
        P.v(lambda e, c=c: e.scalar_tensor_tensor(
            out=outacc[:, c * D:(c + 1) * D], in0=xn_tm[:, c * D:(c + 1) * D],
            scalar=cw46[:, c:c + 1], in1=outacc[:, c * D:(c + 1) * D],
            op0=ALU.mult, op1=ALU.add))
        P.v(lambda e: e.drain())
        P.v(lambda e, c=c: e.tensor_scalar(
            out=e_sb[:, :], in0=xn_tm[:, c * D:(c + 1) * D],
            scalar1=0.0, scalar2=cw7[:, c:c + 1], op0=ALU.max, op1=ALU.mult))
        P.v(lambda e: e.drain())
        P.v(lambda e, c=c: e.tensor_tensor(
            out=outacc[:, c * D:(c + 1) * D], in0=outacc[:, c * D:(c + 1) * D],
            in1=e_sb[:, :], op=ALU.add))
        P.v(lambda e: e.drain())
    P.done("v", "ve")

    P.wait("s", "dws")
    estate = {"hidfree": P.mark("ve")}
    for tl in range(NT):
        for ee in range(NE):
            P.wait("s", "te")
            sdma(bass.AP(wdbuf, 0, [[wdbuf.shape[1], 128], [D, 4], [1, D]]),
                 bass.AP(wd_bf, ee * I_FFN * D,
                         [[D, 128], [128 * D, 4], [1, D]]))
            WB = wbig.shape[1]
            gqm = {}
            gqds = {}

            def gq_load(gq):
                half, quarter = gq // 2, gq % 2
                par = gq % 2
                if gq >= 2:
                    P.wait("s", "te", gqm[gq - 2])
                goff = (ee * D + quarter * 4 * 128) * I_FFN + half * 256
                for ic2 in range(2):
                    sdma(bass.AP(wbig, par * 2048 + ic2 * 128,
                                 [[WB, 128], [512, 4], [1, 128]]),
                         bass.AP(wg_bf, goff + ic2 * 128,
                                 [[I_FFN, 128], [128 * I_FFN, 4], [1, 128]]),
                         "dsa" if par == 0 else "dsb")
                    sdma(bass.AP(wbig, par * 2048 + 256 + ic2 * 128,
                                 [[WB, 128], [512, 4], [1, 128]]),
                         bass.AP(wu_bf, goff + ic2 * 128,
                                 [[I_FFN, 128], [128 * I_FFN, 4], [1, 128]]),
                         "dsa" if par == 0 else "dsb")
                gqds[gq] = P.mark("dsa" if par == 0 else "dsb")

            def gq_mm(gq):
                half, quarter = gq // 2, gq % 2
                par = gq % 2
                if quarter == 0:
                    P.wait("t", "ve", estate["hidfree"])
                P.wait("t", "dsa" if par == 0 else "dsb", gqds[gq])
                for d4 in range(4):
                    dinc = quarter * 4 + d4
                    xs = xn_fm[:, dinc * HWs + 3 + tl * TILE:
                               dinc * HWs + 3 + (tl + 1) * TILE]
                    for ic2 in range(2):
                        b0 = par * 2048 + d4 * 512 + ic2 * 128
                        mm(pf[ic2 * 2][:, 0:TILE], wbig[:, b0:b0 + 128], xs,
                           start=(dinc == 0), stop=(dinc == NDC - 1))
                        mm(pf[ic2 * 2 + 1][:, 0:TILE],
                           wbig[:, b0 + 256:b0 + 384], xs,
                           start=(dinc == 0), stop=(dinc == NDC - 1))
                gqm[gq] = P.done("t", "te")

            for half in range(2):
                gq_load(half * 2)
                gq_load(half * 2 + 1)
                gq_mm(half * 2)
                gq_mm(half * 2 + 1)
                gud = P.mark("te")
                P.wait("s", "te", gud)
                for ic2 in range(2):
                    P.s(lambda e, ic2=ic2: e.activation(
                        out=sg_sb[:, ic2 * 512:ic2 * 512 + TILE],
                        in_=pf[ic2 * 2][:, 0:TILE], func=AF.Silu))
                sgd = P.done("s", "ac")
                P.wait("v", "ac", sgd)
                for ic2 in range(2):
                    ic = half * 2 + ic2
                    P.v(lambda e, ic2=ic2, ic=ic: e.tensor_tensor(
                        out=hid[:, ic * TILE:(ic + 1) * TILE],
                        in0=sg_sb[:, ic2 * 512:ic2 * 512 + TILE],
                        in1=pf[ic2 * 2 + 1][:, 0:TILE], op=ALU.mult))
                estate["hidfree"] = P.done("v", "ve")
            hdall = P.mark("ve")
            P.wait("t", "ve", hdall)
            P.wait("t", "ds")
            for cc_i in range(CPT):
                c = tl * CPT + cc_i
                for ic in range(4):
                    mm(pf[4][:, :],
                       hid[:, ic * TILE + cc_i * 128:ic * TILE + (cc_i + 1) * 128],
                       wdbuf[:, ic * D:ic * D + 512],
                       start=(ic == 0), stop=(ic == 3))
                    mm(pf[5][:, :],
                       hid[:, ic * TILE + cc_i * 128:ic * TILE + (cc_i + 1) * 128],
                       wdbuf[:, ic * D + 512:(ic + 1) * D],
                       start=(ic == 0), stop=(ic == 3))
                wdd = P.done("t", "te")
                P.wait("v", "te", wdd)
                for dg in range(2):
                    bk = pf[4 + dg]
                    P.v(lambda e, c=c, dg=dg, bk=bk, ee=ee: e.scalar_tensor_tensor(
                        out=outacc[:, c * D + dg * 512:c * D + (dg + 1) * 512],
                        in0=bk[:, :], scalar=cw_sb[:, c * 8 + ee:c * 8 + ee + 1],
                        in1=outacc[:, c * D + dg * 512:c * D + (dg + 1) * 512],
                        op0=ALU.mult, op1=ALU.add))
                vdd = P.done("v", "ve")
                P.wait("t", "ve", vdd)

    for c in range(NCH):
        P.wait("v", "dout")
        P.v(lambda e, c=c: e.tensor_copy(out=outf[:, :],
                                         in_=outacc[:, c * D:(c + 1) * D]))
        fd = P.done("v", "ve")
        P.wait("y", "ve", fd)
        ydma(out_h[c * 128:(c + 1) * 128, :], outf[:, :], "dout")
    P.wait("y", "dout")
    if debug:
        P.wait("y", "dbo")

    P.replay()
    return nc


def kernel(**inputs):
    T_FULL = inputs["L0_attn_x"].shape[1]
    B = inputs["L0_attn_x"].shape[0]
    ncore = 8
    per_b = ncore // B
    T = T_FULL // per_b
    x = np.asarray(inputs["L0_attn_x"], np.float32)

    wmap = {
        "q1_w": "L0_attn_q1_w", "q1_b": "L0_attn_q1_b",
        "k1_w": "L0_attn_k1_w", "k1_b": "L0_attn_k1_b",
        "k2_w": "L0_attn_k2_w", "k2_b": "L0_attn_k2_b",
        "gw_w": "L0_attn_gw_w", "gw_b": "L0_attn_gw_b",
        "sw_w": "L0_attn_sw_w", "sw_b": "L0_attn_sw_b",
        "cp_w": "L0_attn_cp_w", "cp_b": "L0_attn_cp_b",
        "Wr": "L0_ffn_Wr", "router_bias": "L0_ffn_router_bias",
        "Wg": "L0_ffn_Wg", "Wu": "L0_ffn_Wu", "Wd": "L0_ffn_Wd",
        "conv_w": "L0_ffn_conv_w", "conv_b": "L0_ffn_conv_b",
    }
    weights = {k: np.ascontiguousarray(np.asarray(inputs[v], np.float32))
               for k, v in wmap.items()}

    in_maps = []
    for core in range(ncore):
        b = core // per_b
        q = core % per_b
        a0 = q * T
        xs = np.ascontiguousarray(x[b, a0:a0 + T, :])
        halo = np.zeros((4, x.shape[2]), np.float32)
        if a0 > 0:
            halo[0:3, :] = x[b, a0 - 3:a0, :]
        if a0 + T < T_FULL:
            halo[3, :] = x[b, a0 + T, :]
        pm = np.zeros((4, 1), np.float32)
        pm[:q, 0] = 1.0
        vm = np.zeros((4, 1), np.float32)
        if q > 0:
            vm[q - 1, 0] = 1.0
        m = {"x": xs, "xhalo": halo, "pmask": pm, "vmask": vm}
        m.update(weights)
        in_maps.append(m)

    nc = build(T, debug=False)
    res = run_bass_kernel_spmd(nc, in_maps, core_ids=list(range(ncore)))
    out = np.zeros((B, T_FULL, x.shape[2]), np.float32)
    for core in range(ncore):
        b = core // per_b
        q = core % per_b
        out[b, q * T:(q + 1) * T, :] = res.results[core]["out"]
    return out


# version 14
# speedup vs baseline: 1.1792x; 1.0133x over previous
import numpy as np
import concourse.bass as bass
import concourse.mybir as mybir
from concourse.bass_utils import run_bass_kernel_spmd

F32 = mybir.dt.float32
BF16 = mybir.dt.bfloat16
AF = mybir.ActivationFunctionType
ALU = mybir.AluOpType
AX = mybir.AxisListType

D = 1024
H = 16
DH = 64
NDC = D // 128
NP = H // 2
I_FFN = 512
NE = 4
CH = 128


class Rec:

    def __init__(self, nc):
        self.nc = nc
        self.ops = {"t": [], "v": [], "s": [], "g": [], "y": []}
        self.sems = {}
        self.cnt = {}
        self.kind = {"t": None, "v": None, "s": None, "g": None, "y": None}
        self.guard = {"t": None, "v": None, "s": None, "g": None, "y": None}

    def sem(self, name):
        if name not in self.sems:
            self.sems[name] = self.nc.alloc_semaphore(name)
            self.cnt[name] = 0
        return self.sems[name]

    def _op(self, eng, fn, kind="op"):
        if kind != "wait" and self.guard[eng] is not None:
            h, v = self.guard[eng]
            self.guard[eng] = None
            self.ops[eng].append(lambda e: e.wait_ge(h, v))
        else:
            self.guard[eng] = None
        self.ops[eng].append(fn)
        self.kind[eng] = kind

    def t(self, fn): self._op("t", fn)
    def v(self, fn): self._op("v", fn)
    def s(self, fn): self._op("s", fn)
    def g(self, fn): self._op("g", fn)
    def y(self, fn): self._op("y", fn)

    def inc(self, eng, fn, sem, n=1):
        h = self.sem(sem)
        self._op(eng, lambda e: fn(e).then_inc(h, n), "inc")
        self.cnt[sem] += n
        return self.cnt[sem]

    def done(self, eng, sem):
        h = self.sem(sem)
        self.cnt[sem] += 1
        if self.kind[eng] == "op":
            f = self.ops[eng][-1]
            self.ops[eng][-1] = (lambda e, f=f: f(e).then_inc(h, 1))
        else:
            self._op(eng, lambda e: e.drain().then_inc(h, 1))
        self.kind[eng] = "inc"
        self.guard[eng] = (h, self.cnt[sem])
        return self.cnt[sem]

    def wait(self, eng, sem, val=None):
        h = self.sem(sem)
        v = self.cnt[sem] if val is None else val
        self._op(eng, lambda e: e.wait_ge(h, v), "wait")

    def mark(self, sem):
        self.sem(sem)
        return self.cnt[sem]

    def replay(self):
        with self.nc.Block() as block:
            block.tensor(lambda e: [f(e) for f in self.ops["t"]])
            block.vector(lambda e: [f(e) for f in self.ops["v"]])
            block.scalar(lambda e: [f(e) for f in self.ops["s"]])
            block.gpsimd(lambda e: [f(e) for f in self.ops["g"]])
            block.sync(lambda e: [f(e) for f in self.ops["y"]])


def build(T, debug=False):
    NCH = T // CH
    TILE = min(512, T)
    NT = T // TILE
    CPT = TILE // CH
    HWs = T + 4
    nc = bass.Bass("TRN2", target_bir_lowering=False)

    def din(name, shape):
        return nc.dram_tensor(name, list(shape), F32, kind="ExternalInput").ap()

    x_h = din("x", (T, D))
    xhalo_h = din("xhalo", (4, D))
    pmask_h = din("pmask", (4, 1))
    vmask_h = din("vmask", (4, 1))
    wq_h = din("q1_w", (D, D)); bq_h = din("q1_b", (D,))
    wk_h = din("k1_w", (D, D)); bk_h = din("k1_b", (D,))
    wk2_h = din("k2_w", (D, D)); bk2_h = din("k2_b", (D,))
    gw_h = din("gw_w", (DH, 1)); gwb_h = din("gw_b", (1,))
    sw_h = din("sw_w", (DH, 1)); swb_h = din("sw_b", (1,))
    wcp_h = din("cp_w", (D, D)); bcp_h = din("cp_b", (D,))
    wr_h = din("Wr", (D, 8)); rb_h = din("router_bias", (8,))
    wg_h = din("Wg", (NE, D, I_FFN))
    wu_h = din("Wu", (NE, D, I_FFN))
    wd_h = din("Wd", (NE, I_FFN, D))
    cvw_h = din("conv_w", (4, 1, D)); cvb_h = din("conv_b", (D,))

    out_h = nc.dram_tensor("out", [T, D], F32, kind="ExternalOutput").ap()
    dbg = {}
    if debug:
        for nm, shape in [("dbg_xn", (T, D)), ("dbg_gc", (H, T)),
                          ("dbg_y", (T, D)), ("dbg_attn", (T, D)),
                          ("dbg_cw", (T, 8))]:
            dbg[nm] = nc.dram_tensor(nm, list(shape), F32, kind="ExternalOutput").ap()

    wq_bf = nc.dram_tensor("wq_bf", [D, D], BF16)
    wk_bf = nc.dram_tensor("wk_bf", [D, D], BF16)
    wk2_bf = nc.dram_tensor("wk2_bf", [D, D], BF16)
    wcp_bf = nc.dram_tensor("wcp_bf", [D, D], BF16)
    wg_bf = nc.dram_tensor("wg_bf", [NE, D, I_FFN], BF16)
    wu_bf = nc.dram_tensor("wu_bf", [NE, D, I_FFN], BF16)
    wd_bf = nc.dram_tensor("wd_bf", [NE, I_FFN, D], BF16)
    cc0i = nc.dram_tensor("cc0i", [H, 1], F32)
    cc0o = nc.dram_tensor("cc0o", [4 * H, 1], F32)
    cc1i = nc.dram_tensor("cc1i", [128, 512], F32)
    cc1o = nc.dram_tensor("cc1o", [512, 512], F32)
    cc2i = nc.dram_tensor("cc2i", [128, 520], F32)
    cc2o = nc.dram_tensor("cc2o", [512, 520], F32)
    GROUPS = [[0, 1, 2, 3], [4, 5, 6, 7]]

    SB = nc.alloc_sbuf_tensor
    PS = nc.alloc_psum_tensor
    xn_fm = SB("xn_fm", [128, NDC * HWs], BF16)
    xn_tm = SB("xn_tm", [128, NCH * D], BF16)
    y_tm = SB("y_tm", [128, max(NCH * D, 8192)], BF16)
    outacc = y_tm
    s1p = SB("s1p", [128, max(NCH * 512, NDC * D)], BF16)
    cpw_sb = s1p

    lgg = SB("lgg", [16, T], F32)
    rra = SB("rra", [16, T], BF16)
    gc_tm = SB("gc_tm", [128, NCH * 16], F32)
    a_tm = SB("a_tm", [128, NCH * 16], F32)

    id_bf = SB("id_bf", [128, 128], BF16)
    id_f32 = SB("id_f32", [128, 128], F32)
    tri_f32 = SB("tri_f32", [128, 128], F32)
    ones_bf = SB("ones_bf", [1, 512], BF16)
    ones_f32 = SB("ones_f32", [1, 128], F32)
    gwblk = SB("gwblk", [128, NDC * 16], BF16)
    swblk = SB("swblk", [128, NDC * 16], BF16)
    gwb_tmp = SB("gwb_tmp", [1, 2], BF16)
    gwb16 = SB("gwb16", [1, 16], BF16)
    swb16 = SB("swb16", [1, 16], BF16)
    bias_sb = SB("bias_sb", [1, D], BF16)
    bcp_sb = SB("bcp_sb", [1, D], BF16)
    rb_sb = SB("rb_sb", [1, 8], F32)
    cvb_sb = SB("cvb_sb", [128, 8], F32)
    cvw_sb = SB("cvw_sb", [128, 8 * 4], F32)
    pm_sb = SB("pm_sb", [4, 1], F32)
    pmT = SB("pmT", [1, 4], F32)
    vmT = SB("vmT", [1, 4], F32)
    pmbc = SB("pmbc", [128, 4], F32)
    vmbc = SB("vmbc", [128, 4], F32)
    xh_tm = SB("xh_tm", [4, D], BF16)
    gsum_blk = SB("gsum_blk", [4, 16], F32)
    off_col = SB("off_col", [16, 1], F32)

    s1in = SB("s1in", [128, 512], BF16)
    s2in = SB("s2in", [128, 512], BF16)
    vprev = SB("vprev", [128, 8], BF16)
    s1tot = SB("s1tot", [128, 512], BF16)
    s2tot = SB("s2tot", [128, 512], BF16)
    ccg = SB("ccg", [128, 520], F32)
    t2pack = ccg

    kt = SB("kt", [128, max(NDC * TILE, 4 * D, NDC * (TILE + 1))], BF16)
    qt = SB("qt", [128, max(NDC * TILE, 4096)], BF16)
    k2t = SB("k2t", [128, NDC * TILE], BF16)
    q2t = SB("q2t", [128, NDC * TILE], BF16)
    q2s = kt
    p_sb = SB("p_sb", [128, H * 128], BF16)
    av_sb = SB("av_sb", [128, D], BF16)
    ktm2 = SB("ktm2", [128, D], BF16)
    vs_sb = SB("vs_sb", [128, D], BF16)
    e_sb = SB("e_sb", [128, D], BF16)
    o2e = SB("o2e", [128, D], BF16)
    o2s = SB("o2s", [128, D], BF16)
    yf_sb = SB("yf_sb", [128, D], BF16)
    x_tm = SB("x_tm", [128, D], F32)
    stats = SB("stats", [128, 8], F32)
    sq_sb = SB("sq_sb", [128, D], F32)
    rgc_t = SB("rgc_t", [128, 16], F32)
    rr_t = SB("rr_t", [128, 16], F32)
    wr_sb = SB("wr_sb", [128, NDC * 8], BF16)
    biasbc = SB("biasbc", [128, 8], F32)
    sc_fm = SB("sc_fm", [8, 128], F32)
    z_t = SB("z_t", [128, 8], F32)
    z_srt = SB("z_srt", [128, 8], F32)
    z_m = SB("z_m", [128, 8], F32)
    z_rep = SB("z_rep", [128, 8], F32)
    cw_sb = SB("cw_sb", [128, NCH * 8], F32)
    cw46 = SB("cw46", [128, NCH], F32)
    cw7 = SB("cw7", [128, NCH], F32)
    wbuf = SB("wbuf", [128, 2 * NDC * 128], BF16)
    wbig = qt
    wdbuf = kt
    sg_sb = k2t
    hid = q2t
    CW = min(512, T)
    conv_fm = None
    outf = None

    conv_fm = vs_sb
    outf = sq_sb
    outf_dbg = SB("outf_dbg", [128, D], F32) if debug else None
    s1run = sq_sb
    s2run = sq_sb
    pf = [PS(f"pf{i}", [128, 512], F32) for i in range(6)]
    pb = [PS(f"pb{i}", [128, 1024], BF16) for i in range(2)]

    P = Rec(nc)

    def gdma(dst, src, sem="dg"):
        return P.inc("g", lambda e: e.dma_start(out=dst, in_=src), sem, 16)

    def ydma(dst, src, sem="dy"):
        return P.inc("y", lambda e: e.dma_start(out=dst, in_=src), sem, 16)

    def sdma(dst, src, sem="ds"):
        return P.inc("s", lambda e: e.dma_start(out=dst, in_=src), sem, 16)

    def mm(out, lhsT, rhs, start, stop, tp=None):
        P.t(lambda e: e.matmul(out, lhsT, rhs, start=start, stop=stop,
                               tile_position=tp))

    def mkconst(e):
        e.memset(id_bf[:, :], 0.0)
        e.memset(id_f32[:, :], 0.0)
        e.memset(tri_f32[:, :], 0.0)
        e.memset(gwblk[:, :], 0.0)
        e.memset(swblk[:, :], 0.0)
        e.drain()
        e.affine_select(out=id_bf[:, :], in_=id_bf[:, :], compare_op=ALU.not_equal,
                        fill=1.0, base=0, pattern=[[-1, 128]], channel_multiplier=1)
        e.affine_select(out=id_f32[:, :], in_=id_f32[:, :], compare_op=ALU.not_equal,
                        fill=1.0, base=0, pattern=[[-1, 128]], channel_multiplier=1)
        return e.affine_select(out=tri_f32[:, :], in_=tri_f32[:, :],
                                compare_op=ALU.is_gt, fill=1.0, base=0,
                                pattern=[[-1, 128]], channel_multiplier=1)
    P.g(mkconst)
    P.done("g", "pl")

    P.v(lambda e: e.memset(ones_bf[:, :], 1.0))
    P.v(lambda e: e.memset(ones_f32[:, :], 1.0))
    P.done("v", "ve")

    gdma(bcp_sb[:, :], bcp_h.rearrange("(a b) -> a b", a=1))
    gdma(gwb_tmp[:, 0:1], gwb_h.rearrange("(a b) -> a b", a=1))
    gdma(gwb_tmp[:, 1:2], swb_h.rearrange("(a b) -> a b", a=1))
    ydma(rb_sb[:, :], rb_h.rearrange("(a b) -> a b", a=1))
    ydma(pm_sb[:, :], pmask_h)
    ydma(pmT[:, :], pmask_h.rearrange("a b -> b a"))
    ydma(vmT[:, :], vmask_h.rearrange("a b -> b a"))
    for dc in range(NDC):
        ydma(cvb_sb[:, dc:dc + 1],
             cvb_h.rearrange("(a b) -> a b", a=NDC)[dc, :].unsqueeze(1))
        for k in range(4):
            ydma(cvw_sb[:, dc * 4 + k:dc * 4 + k + 1],
                 cvw_h[k, 0, dc * 128:(dc + 1) * 128].unsqueeze(1))
    P.wait("g", "pl")
    for dc in range(NDC):
        gdma(gwblk[0:64, dc * 16 + 2 * dc:dc * 16 + 2 * dc + 1], gw_h)
        gdma(gwblk[64:128, dc * 16 + 2 * dc + 1:dc * 16 + 2 * dc + 2], gw_h)
        gdma(swblk[0:64, dc * 16 + 2 * dc:dc * 16 + 2 * dc + 1], sw_h)
        gdma(swblk[64:128, dc * 16 + 2 * dc + 1:dc * 16 + 2 * dc + 2], sw_h)
    for dc in range(NDC):
        gdma(wr_sb[:, dc * 8:(dc + 1) * 8], wr_h[dc * 128:(dc + 1) * 128, :])

    P.wait("t", "dg")
    P.wait("t", "dy")
    P.wait("t", "ve")
    mm(pf[4][0:1, 0:16], gwb_tmp[:, 0:1], ones_bf[:, 0:16], True, True)
    mm(pf[4][0:1, 16:32], gwb_tmp[:, 1:2], ones_bf[:, 0:16], True, True)
    for r in range(4):
        mm(pf[4][0:128, 32 + r:33 + r], ones_f32[:, :], pmT[:, r:r + 1], True, True)
        mm(pf[4][0:128, 40 + r:41 + r], ones_f32[:, :], vmT[:, r:r + 1], True, True)
    mm(pf[4][0:128, 48:56], ones_f32[:, :], rb_sb[:, :], True, True)
    td = P.done("t", "te")
    P.wait("v", "te", td)
    P.v(lambda e: e.tensor_copy(out=gwb16[:, :], in_=pf[4][0:1, 0:16]))
    P.v(lambda e: e.tensor_copy(out=swb16[:, :], in_=pf[4][0:1, 16:32]))
    P.v(lambda e: e.tensor_copy(out=pmbc[:, :], in_=pf[4][0:128, 32:36]))
    P.v(lambda e: e.tensor_copy(out=vmbc[:, :], in_=pf[4][0:128, 40:44]))
    P.v(lambda e: e.tensor_copy(out=biasbc[:, :], in_=pf[4][0:128, 48:56]))
    P.done("v", "ve")
    P.wait("t", "ve")

    wunits = []
    for wsrc, wdst in [(wk_h, wk_bf.ap()), (wq_h, wq_bf.ap()),
                       (wk2_h, wk2_bf.ap()), (wcp_h, wcp_bf.ap())]:
        for p in range(8):
            wunits.append((wsrc[p * 128:(p + 1) * 128, :],
                           wdst[p * 128:(p + 1) * 128, :], D))
    for ee in range(NE):
        for p in range(8):
            wunits.append((wg_h[ee, p * 128:(p + 1) * 128, :],
                           wg_bf.ap()[ee, p * 128:(p + 1) * 128, :], I_FFN))
            wunits.append((wu_h[ee, p * 128:(p + 1) * 128, :],
                           wu_bf.ap()[ee, p * 128:(p + 1) * 128, :], I_FFN))
        for p in range(4):
            wunits.append((wd_h[ee, p * 128:(p + 1) * 128, :],
                           wd_bf.ap()[ee, p * 128:(p + 1) * 128, :], D))
    wstate = {"i": 0, "projmark": None}

    def emit_wunits(n):
        for _ in range(n):
            i = wstate["i"]
            if i >= len(wunits):
                return
            wstate["i"] = i + 1
            src, dst, w = wunits[i]
            par = i % 2
            f32v = y_tm[:, par * 4096:par * 4096 + 2048].bitcast(F32)[:, 0:w]
            bfv = y_tm[:, par * 4096 + 2048:par * 4096 + 2048 + w]
            if i >= 2:
                P.wait("y", "ac")
            lded = ydma(f32v, src, "dwl")
            P.wait("s", "dwl", lded)
            if i >= 2:
                P.wait("s", "dws")
            P.s(lambda e, f32v=f32v, bfv=bfv: e.activation(out=bfv, in_=f32v,
                                                           func=AF.Copy))
            cvd = P.done("s", "ac")
            P.wait("y", "ac", cvd)
            ydma(dst, bfv, "dws")
            if wstate["i"] == 32:
                wstate["projmark"] = P.mark("dws")

    def emit_a1_tile(tl):
        P.wait("t", "dg")
        P.wait("t", "ac")
        for dinc in range(NDC):
            mm(pf[4][0:16, 0:TILE], gwblk[:, dinc * 16:(dinc + 1) * 16],
               xn_fm[:, dinc * HWs + 3 + tl * TILE:dinc * HWs + 3 + (tl + 1) * TILE],
               start=(dinc == 0), stop=False)
        mm(pf[4][0:16, 0:TILE], gwb16[:, :], ones_bf[:, 0:TILE], False, True)
        tdone = P.done("t", "te")
        P.wait("s", "te", tdone)
        P.s(lambda e, tl=tl: e.activation(out=lgg[:, tl * TILE:(tl + 1) * TILE],
                                          in_=pf[4][0:16, 0:TILE], func=AF.Sigmoid))
        P.done("s", "ac")

    def ln_rows(xs, nrows, xn_out):
        P.s(lambda e: e.activation(out=sq_sb[0:nrows, :], in_=xs, func=AF.Square))
        P.done("s", "ac")
        P.wait("v", "ac")
        P.v(lambda e: e.reduce_sum(out=stats[0:nrows, 0:1], in_=xs, axis=AX.X))
        P.v(lambda e: e.reduce_sum(out=stats[0:nrows, 1:2], in_=sq_sb[0:nrows, :],
                                   axis=AX.X))
        P.v(lambda e: e.drain())
        P.v(lambda e: e.tensor_scalar(out=stats[0:nrows, 2:3], in0=stats[0:nrows, 0:1],
                                      scalar1=1.0 / D, scalar2=None, op0=ALU.mult))
        P.v(lambda e: e.drain())
        P.v(lambda e: e.tensor_tensor(out=stats[0:nrows, 3:4], in0=stats[0:nrows, 2:3],
                                      in1=stats[0:nrows, 2:3], op=ALU.mult))
        P.v(lambda e: e.drain())
        P.v(lambda e: e.scalar_tensor_tensor(out=stats[0:nrows, 4:5],
                                             in0=stats[0:nrows, 1:2],
                                             scalar=1.0 / D, in1=stats[0:nrows, 3:4],
                                             op0=ALU.mult, op1=ALU.subtract))
        P.v(lambda e: e.drain())
        P.v(lambda e: e.tensor_scalar_add(out=stats[0:nrows, 4:5],
                                          in0=stats[0:nrows, 4:5], scalar1=1e-5))
        P.done("v", "ve")
        P.wait("s", "ve")
        P.s(lambda e: e.activation(out=stats[0:nrows, 5:6], in_=stats[0:nrows, 4:5],
                                   func=AF.Ln))
        P.s(lambda e: e.drain())
        P.s(lambda e: e.activation(out=stats[0:nrows, 6:7], in_=stats[0:nrows, 5:6],
                                   func=AF.Exp, scale=-0.5))
        P.done("s", "ac")
        P.wait("v", "ac")
        P.v(lambda e: e.tensor_scalar(out=xn_out, in0=xs,
                                      scalar1=stats[0:nrows, 2:3],
                                      scalar2=stats[0:nrows, 6:7],
                                      op0=ALU.subtract, op1=ALU.mult))
        return P.done("v", "ve")

    G = min(4, NCH)
    NG = (NCH + G - 1) // G
    xslots = []
    for buf, lo in ((y_tm, 8192), (qt, 0), (k2t, 0), (q2t, 0)):
        k = lo
        while k + 2048 <= buf.shape[1] and len(xslots) < 2 * G:
            xslots.append(buf[:, k:k + 2048].bitcast(F32))
            k += 2048
    assert len(xslots) >= 2 * G
    amarks = {}
    for g in range(NG):
        c0 = g * G
        gs = min(G, NCH - c0)
        base = (g % 2) * 80
        sl = [xslots[(g % 2) * G + k] for k in range(gs)]
        if g >= 2:
            P.wait("y", "ve", amarks[g - 2])
        dxc = "dxa" if g % 2 == 0 else "dxb"
        for k in range(gs):
            ydma(sl[k][:, :], x_h[(c0 + k) * 128:(c0 + k + 1) * 128, :], dxc)
        dxm = P.mark(dxc)
        emit_wunits(8 if NG >= 4 else len(wunits))
        P.wait("v", dxc, dxm)
        for k in range(gs):
            P.v(lambda e, k=k, s=sl[k], base=base: e.bn_stats(
                out=sq_sb[:, base + k * 12:base + k * 12 + 6], in_=s[:, 0:512]))
            P.v(lambda e, k=k, s=sl[k], base=base: e.bn_stats(
                out=sq_sb[:, base + k * 12 + 6:base + k * 12 + 12],
                in_=s[:, 512:1024]))
        P.v(lambda e: e.drain())
        for k in range(gs):
            P.v(lambda e, k=k, base=base: e.bn_aggr(
                out=sq_sb[:, base + 48 + 2 * k:base + 50 + 2 * k],
                in_=sq_sb[:, base + k * 12:base + k * 12 + 12]))
        P.v(lambda e: e.drain())
        P.v(lambda e, gs=gs, base=base: e.tensor_scalar_add(
            out=sq_sb[:, base + 60:base + 60 + gs],
            in0=bass.AP(sq_sb, base + 49, [[sq_sb.shape[1], 128], [2, gs]]),
            scalar1=1e-5))
        vstat = P.done("v", "ve")
        P.wait("s", "ve", vstat)
        P.s(lambda e, gs=gs, base=base: e.activation(
            out=sq_sb[:, base + 64:base + 64 + gs],
            in_=sq_sb[:, base + 60:base + 60 + gs], func=AF.Ln))
        P.s(lambda e: e.drain())
        P.s(lambda e, gs=gs, base=base: e.activation(
            out=sq_sb[:, base + 64:base + 64 + gs],
            in_=sq_sb[:, base + 64:base + 64 + gs], func=AF.Exp, scale=-0.5))
        sac = P.done("s", "ac")
        P.wait("v", "ac", sac)
        for k in range(gs):
            P.v(lambda e, k=k, c=c0 + k, s=sl[k], base=base: e.tensor_scalar(
                out=xn_tm[:, c * D:(c + 1) * D], in0=s[:, 0:D],
                scalar1=sq_sb[:, base + 48 + 2 * k:base + 49 + 2 * k],
                scalar2=sq_sb[:, base + 64 + k:base + 65 + k],
                op0=ALU.subtract, op1=ALU.mult))
        amarks[g] = P.done("v", "ve")
        P.wait("t", "ve", amarks[g])
        if g == 0:
            P.wait("t", "pl")
        emarks = {}
        for k in range(gs):
            c = c0 + k
            if k >= 2:
                P.wait("t", "ve", emarks[k - 2])
            for dc in range(NDC):
                P.t(lambda e, dc=dc, c=c, k=k: e.transpose(
                    pb[k % 2][:, dc * 128:(dc + 1) * 128],
                    xn_tm[:, c * D + dc * 128:c * D + (dc + 1) * 128],
                    id_bf[:, :]))
            td = P.done("t", "te")
            P.wait("v", "te", td)
            P.v(lambda e, c=c, k=k: e.tensor_copy(
                out=bass.AP(xn_fm, 3 + c * 128,
                            [[NDC * HWs, 128], [HWs, NDC], [1, 128]]),
                in_=pb[k % 2][:, :].rearrange("p (a b) -> p a b", a=NDC)))
            emarks[k] = P.done("v", "ve")
        for c in range(c0, c0 + gs):
            if (c + 1) % CPT == 0:
                P.wait("t", "ve", emarks[gs - 1])
                emit_a1_tile((c + 1) // CPT - 1)

    P.wait("y", "ve")
    ydma(x_tm[0:4, :], xhalo_h)
    hld = P.mark("dy")
    P.wait("s", "dy", hld)
    P.wait("v", "dy", hld)
    hdone = ln_rows(x_tm[0:4, :], 4, xh_tm[:, :])
    P.wait("t", "ve", hdone)
    for dc in range(NDC):
        P.t(lambda e, dc=dc: e.transpose(pb[0][:, dc * 4:dc * 4 + 4],
                                         xh_tm[:, dc * 128:(dc + 1) * 128],
                                         id_bf[0:4, 0:4]))
    tdone = P.done("t", "te")
    P.wait("v", "te", tdone)
    for dc in range(NDC):
        P.v(lambda e, dc=dc: e.tensor_copy(out=xn_fm[:, dc * HWs:dc * HWs + 3],
                                           in_=pb[0][:, dc * 4:dc * 4 + 3]))
        P.v(lambda e, dc=dc: e.tensor_copy(
            out=xn_fm[:, dc * HWs + 3 + T:dc * HWs + 4 + T],
            in_=pb[0][:, dc * 4 + 3:dc * 4 + 4]))
    P.done("v", "ve")

    if debug:
        for c in range(NCH):
            P.wait("v", "dbo")
            P.v(lambda e, c=c: e.tensor_copy(out=outf_dbg[:, :],
                                             in_=xn_tm[:, c * D:(c + 1) * D]))
            dv = P.done("v", "ve")
            P.wait("y", "ve", dv)
            ydma(dbg["dbg_xn"][c * 128:(c + 1) * 128, :], outf_dbg[:, :], "dbo")

    def stream_proj(wsrc, bias_h, dst_fn, act=None, act_scale=1.0, tilebase=0):
        P.wait("g", "te")
        gdma(bias_sb[:, :], bias_h.rearrange("(a b) -> a b", a=1))
        temarks = {}
        acmarks = {}
        dsm = {}

        def emit_group(doutc):
            par = doutc % 2
            P.wait("t", "ve")
            if doutc >= 2:
                P.wait("t", "ac", acmarks[doutc - 2])
            P.wait("t", "dg")
            P.wait("t", "dsa" if par == 0 else "dsb", dsm[doutc])
            for dinc in range(NDC):
                mm(pf[par][:, 0:TILE],
                   wbuf[:, par * 1024 + dinc * 128:par * 1024 + (dinc + 1) * 128],
                   xn_fm[:, dinc * HWs + 3 + tilebase * TILE:
                         dinc * HWs + 3 + (tilebase + 1) * TILE],
                   start=(dinc == 0), stop=False)
            mm(pf[par][:, 0:TILE], bias_sb[:, doutc * 128:(doutc + 1) * 128],
               ones_bf[:, 0:TILE], start=False, stop=True)
            temarks[doutc] = P.done("t", "te")
            P.wait("s", "te", temarks[doutc])
            tgt = dst_fn(doutc)
            if act is None:
                P.s(lambda e, tgt=tgt, par=par: e.activation(
                    out=tgt, in_=pf[par][:, 0:TILE], func=AF.Copy))
            else:
                P.s(lambda e, tgt=tgt, par=par, a=act, sc=act_scale: e.activation(
                    out=tgt, in_=pf[par][:, 0:TILE], func=a, scale=sc))
            acmarks[doutc] = P.done("s", "ac")

        for doutc in range(NDC):
            par = doutc % 2
            if doutc >= 2:
                P.wait("s", "te", temarks[doutc - 2])
            sdma(wbuf[:, par * 1024:(par + 1) * 1024],
                 bass.AP(wsrc.tensor, doutc * 128,
                         [[D, 128], [128 * D, NDC], [1, 128]]),
                 "dsa" if par == 0 else "dsb")
            dsm[doutc] = P.mark("dsa" if par == 0 else "dsb")
            if doutc >= 1:
                emit_group(doutc - 1)
        emit_group(NDC - 1)
        P.wait("t", "ac")

    P.wait("v", "ac")
    P.v(lambda e: e.tensor_scalar_max(out=lgg[:, :], in0=lgg[:, :], scalar1=1e-6))
    P.done("v", "ve")
    P.wait("s", "ve")
    P.s(lambda e: e.activation(out=lgg[:, :], in_=lgg[:, :], func=AF.Ln))
    P.done("s", "ac")
    P.wait("v", "ac")
    P.v(lambda e: e.tensor_tensor_scan(out=lgg[:, :], data0=lgg[:, :],
                                       data1=lgg[:, :], initial=0.0,
                                       op0=ALU.add, op1=ALU.bypass))
    P.done("v", "ve")

    P.wait("y", "ve")
    ydma(cc0i[:, :], lgg[:, T - 1:T])
    P.wait("g", "dy")
    P.inc("g", lambda e: e.collective_compute(
        "AllGather", ALU.bypass, replica_groups=GROUPS,
        ins=[cc0i[:, :]], outs=[cc0o[:, :]]), "cc", 1)
    P.wait("g", "cc")
    gdma(gsum_blk[:, :], cc0o.ap().rearrange("(r h) one -> r (h one)", r=4), "dgc")
    P.wait("t", "dgc")
    mm(pf[4][0:16, 0:1], gsum_blk[:, :], pm_sb[:, :], True, True)
    tdone = P.done("t", "te")
    P.wait("v", "te", tdone)
    P.v(lambda e: e.tensor_copy(out=off_col[:, :], in_=pf[4][0:16, 0:1]))
    P.v(lambda e: e.drain())
    P.v(lambda e: e.tensor_scalar(out=lgg[:, :], in0=lgg[:, :],
                                  scalar1=off_col[:, :], scalar2=30.0,
                                  op0=ALU.add, op1=ALU.min))
    P.v(lambda e: e.drain())
    P.v(lambda e: e.tensor_scalar_max(out=lgg[:, :], in0=lgg[:, :], scalar1=-30.0))
    P.done("v", "ve")
    P.wait("s", "ve")
    P.s(lambda e: e.activation(out=lgg[:, :], in_=lgg[:, :], func=AF.Exp))
    P.done("s", "ac")
    P.wait("v", "ac")
    P.v(lambda e: e.tensor_scalar_add(out=lgg[:, :], in0=lgg[:, :], scalar1=1e-6))
    gcd = P.done("v", "ve")
    if debug:
        P.wait("y", "ve", gcd)
        ydma(dbg["dbg_gc"][:, :], lgg[:, :], "dbo")

    P.wait("s", "dws", wstate["projmark"])
    for tl in range(NT):
        stream_proj(wk_bf.ap(), bk_h, lambda dc: kt[:, dc * TILE:(dc + 1) * TILE],
                    tilebase=tl)
        P.wait("t", "ve")
        for dinc in range(NDC):
            mm(pf[4][0:16, 0:TILE], swblk[:, dinc * 16:(dinc + 1) * 16],
               kt[:, dinc * TILE:(dinc + 1) * TILE],
               start=(dinc == 0), stop=False)
        mm(pf[4][0:16, 0:TILE], swb16[:, :], ones_bf[:, 0:TILE], False, True)
        tdone = P.done("t", "te")
        P.wait("s", "te", tdone)
        P.s(lambda e, tl=tl: e.activation(out=rra[:, tl * TILE:(tl + 1) * TILE],
                                          in_=pf[4][0:16, 0:TILE], func=AF.Silu))
        rd = P.done("s", "ac")
        P.wait("t", "ac", rd)

        for cc_i in range(CPT):
            c = tl * CPT + cc_i
            emit_wunits(6)
            P.wait("t", "ve")
            P.t(lambda e, c=c: e.transpose(pf[4][:, 0:16],
                                           lgg[:, c * 128:(c + 1) * 128],
                                           id_f32[0:16, 0:16]))
            P.t(lambda e, c=c: e.transpose(pb[1][:, 0:16],
                                           rra[:, c * 128:(c + 1) * 128],
                                           id_bf[0:16, 0:16]))
            tdone = P.done("t", "te")
            P.wait("v", "te", tdone)
            P.v(lambda e, c=c: e.tensor_copy(out=gc_tm[:, c * 16:(c + 1) * 16],
                                             in_=pf[4][:, 0:16]))
            P.v(lambda e: e.tensor_copy(out=rr_t[:, :], in_=pb[1][:, 0:16]))
            P.v(lambda e: e.drain())
            P.v(lambda e, c=c: e.reciprocal(out=rgc_t[:, :],
                                            in_=gc_tm[:, c * 16:(c + 1) * 16]))
            P.v(lambda e: e.drain())
            P.v(lambda e, c=c: e.tensor_tensor(out=a_tm[:, c * 16:(c + 1) * 16],
                                               in0=rr_t[:, :], in1=rgc_t[:, :],
                                               op=ALU.mult))
            P.v(lambda e: e.drain())
            for h in range(H):
                P.v(lambda e, c=c, h=h: e.tensor_scalar_mul(
                    out=av_sb[:, h * 64:(h + 1) * 64],
                    in0=xn_tm[:, c * D + h * 64:c * D + (h + 1) * 64],
                    scalar1=a_tm[:, c * 16 + h:c * 16 + h + 1]))
            avd = P.done("v", "ve")
            P.wait("t", "ve", avd)
            for dc in range(NDC):
                P.t(lambda e, cc_i=cc_i, dc=dc: e.transpose(
                    pb[0][:, dc * 128:(dc + 1) * 128],
                    kt[:, dc * TILE + cc_i * 128:dc * TILE + (cc_i + 1) * 128],
                    id_bf[:, :]))
            tdone = P.done("t", "te")
            P.wait("v", "te", tdone)
            P.v(lambda e: e.tensor_copy(out=ktm2[:, :], in_=pb[0][:, :]))
            kd = P.done("v", "ve")
            P.wait("t", "ve", kd)
            for h in range(H):
                hp, hh = h // 2, h % 2
                mm(pf[0][hh * 64:(hh + 1) * 64, hp * 64:(hp + 1) * 64],
                   ktm2[:, hp * 128 + hh * 64:hp * 128 + (hh + 1) * 64],
                   av_sb[:, h * 64:(h + 1) * 64],
                   start=True, stop=True, tp=(0, hh * 64))
            tdone = P.done("t", "te")
            P.wait("v", "te", tdone)
            if c == 0:
                P.v(lambda e: e.tensor_copy(out=s1run[:, 0:512], in_=pf[0][:, :]))
            else:
                P.v(lambda e: e.tensor_tensor(out=s1run[:, 0:512],
                                              in0=s1run[:, 0:512],
                                              in1=pf[0][:, :], op=ALU.add))
            P.v(lambda e: e.drain())
            P.v(lambda e, c=c: e.tensor_copy(out=s1p[:, c * 512:(c + 1) * 512],
                                             in_=s1run[:, 0:512]))
            P.done("v", "ve")

    emit_wunits(len(wunits))
    vd = P.mark("ve")
    P.wait("y", "ve", vd)
    ydma(cc1i[:, :], s1run[:, 0:512])
    P.wait("g", "dy")
    P.inc("g", lambda e: e.collective_compute(
        "AllGather", ALU.bypass, replica_groups=GROUPS,
        ins=[cc1i[:, :]], outs=[cc1o[:, :]]), "cc", 1)
    ccend = P.mark("cc")
    P.wait("v", "ve")
    P.v(lambda e: e.memset(x_tm[:, 0:512], 0.0))
    P.done("v", "ve")
    for r in range(4):
        P.wait("g", "cc", ccend)
        P.wait("g", "ve")
        ld = gdma(ccg[:, 0:512], cc1o[r * 128:(r + 1) * 128, :], "dgc")
        P.wait("v", "dgc", ld)
        P.v(lambda e, r=r: e.scalar_tensor_tensor(
            out=x_tm[:, 0:512], in0=ccg[:, 0:512],
            scalar=pmbc[:, r:r + 1], in1=x_tm[:, 0:512],
            op0=ALU.mult, op1=ALU.add))
        P.done("v", "ve")
    P.v(lambda e: e.tensor_copy(out=s1in[:, :], in_=x_tm[:, 0:512]))
    P.done("v", "ve")
    P.wait("t", "ve")

    P.wait("v", "dws")
    for tl in range(NT):
        stream_proj(wq_bf.ap(), bq_h, lambda dc: qt[:, dc * TILE:(dc + 1) * TILE],
                    tilebase=tl)
        stream_proj(wk2_bf.ap(), bk2_h, lambda dc: k2t[:, dc * TILE:(dc + 1) * TILE],
                    act=AF.Sigmoid, act_scale=0.02 / 32.0, tilebase=tl)
        stream_proj(wk_bf.ap(), bk_h, lambda dc: kt[:, dc * TILE:(dc + 1) * TILE],
                    tilebase=tl)
        P.wait("v", "ac")
        P.v(lambda e: e.tensor_scalar(out=q2t[:, :], in0=qt[:, 0:NDC * TILE], scalar1=0.0,
                                      scalar2=0.98 / 8.0, op0=ALU.max, op1=ALU.mult))
        P.v(lambda e: e.drain())
        P.v(lambda e: e.scalar_tensor_tensor(out=q2t[:, :], in0=qt[:, 0:NDC * TILE],
                                             scalar=0.125, in1=q2t[:, :],
                                             op0=ALU.mult, op1=ALU.subtract))
        P.done("v", "ve")

        for cc_i in range(CPT):
            c = tl * CPT + cc_i

            def qsl(buf, h, cc_i=cc_i):
                hp, hh = h // 2, h % 2
                return buf[hh * 64:(hh + 1) * 64,
                           hp * TILE + cc_i * 128:hp * TILE + (cc_i + 1) * 128]

            P.wait("v", "te")
            if c == 0:
                P.v(lambda e: e.tensor_copy(out=s1tot[:, :], in_=s1in[:, :]))
                P.v(lambda e: e.memset(s2tot[:, :], 0.0))
            else:
                P.v(lambda e, c=c: e.tensor_tensor(
                    out=s1tot[:, :], in0=s1in[:, :],
                    in1=s1p[:, (c - 1) * 512:c * 512], op=ALU.add))
                P.v(lambda e: e.tensor_copy(out=s2tot[:, :], in_=s2run[:, 512:1024]))
            for h in range(H):
                P.v(lambda e, c=c, h=h: e.tensor_scalar_mul(
                    out=av_sb[:, h * 64:(h + 1) * 64],
                    in0=xn_tm[:, c * D + h * 64:c * D + (h + 1) * 64],
                    scalar1=a_tm[:, c * 16 + h:c * 16 + h + 1]))
            pre = P.done("v", "ve")
            P.wait("y", "ve", pre)
            ydma(vs_sb[0:127, :], xn_tm[1:128, c * D:(c + 1) * D])
            if c < NCH - 1:
                ydma(vs_sb[127:128, :], xn_tm[0:1, (c + 1) * D:(c + 1) * D + D])
            else:
                ydma(vs_sb[127:128, :], xh_tm[3:4, :])
            vsld = P.mark("dy")

            def rec_pass(kbuf, qbuf, rhs_for_o, stot, obank_off):
                for h in range(H):
                    hp, hh = h // 2, h % 2
                    mm(pf[h % 4][:, (h // 4) * 128:(h // 4) * 128 + 128],
                       qsl(kbuf, h), qsl(qbuf, h),
                       start=True, stop=True, tp=(hh * 64, 0))
                tdone = P.done("t", "te")
                P.wait("v", "te", tdone)
                for h in range(H):
                    P.v(lambda e, h=h: e.tensor_tensor(
                        out=p_sb[:, h * 128:(h + 1) * 128],
                        in0=pf[h % 4][:, (h // 4) * 128:(h // 4) * 128 + 128],
                        in1=tri_f32[:, :], op=ALU.mult))
                pd = P.done("v", "ve")
                P.wait("t", "ve", pd)
                for h in range(H):
                    hp, hh = h // 2, h % 2
                    bank = pf[2 + h // 8]
                    col = (h % 8) * 64
                    mm(bank[:, col:col + 64], p_sb[:, h * 128:(h + 1) * 128],
                       rhs_for_o[:, h * 64:(h + 1) * 64],
                       start=True, stop=False)
                    mm(bank[:, col:col + 64], qsl(qbuf, h),
                       stot[hh * 64:(hh + 1) * 64, hp * 64:(hp + 1) * 64],
                       start=False, stop=True, tp=(hh * 64, 0))
                return P.done("t", "te")

            P.wait("t", "ve", pre)
            o1done = rec_pass(kt, qt, av_sb, s1tot, 0)
            P.wait("v", "te", o1done)
            for h in range(H):
                bank = pf[2 + h // 8]
                col = (h % 8) * 64
                P.v(lambda e, c=c, h=h, bank=bank, col=col: e.tensor_scalar_mul(
                    out=y_tm[:, c * D + h * 64:c * D + (h + 1) * 64],
                    in0=bank[:, col:col + 64],
                    scalar1=gc_tm[:, c * 16 + h:c * 16 + h + 1]))
            P.v(lambda e: e.drain())
            P.wait("v", "dy", vsld)
            P.v(lambda e, c=c: e.tensor_tensor(out=e_sb[:, :], in0=vs_sb[:, :],
                                               in1=y_tm[:, c * D:(c + 1) * D],
                                               op=ALU.subtract))
            ed = P.done("v", "ve")
            P.wait("t", "ve", ed)
            for dc in range(NDC):
                P.t(lambda e, dc=dc, cc_i=cc_i: e.transpose(
                    pb[0][:, dc * 128:(dc + 1) * 128],
                    k2t[:, dc * TILE + cc_i * 128:dc * TILE + (cc_i + 1) * 128],
                    id_bf[:, :]))
            tdone = P.done("t", "te")
            P.wait("v", "te", tdone)
            P.v(lambda e: e.tensor_copy(out=ktm2[:, :], in_=pb[0][:, :]))
            k2d = P.done("v", "ve")
            P.wait("t", "ve", k2d)
            o2done = rec_pass(k2t, q2t, e_sb, s2tot, 0)
            for h in range(H):
                hp, hh = h // 2, h % 2
                mm(pf[5][hh * 64:(hh + 1) * 64, hp * 64:(hp + 1) * 64],
                   ktm2[:, hp * 128 + hh * 64:hp * 128 + (hh + 1) * 64],
                   e_sb[:, h * 64:(h + 1) * 64],
                   start=True, stop=True, tp=(0, hh * 64))
            st2d = P.done("t", "te")
            P.wait("v", "te", st2d)
            if c == 0:
                P.v(lambda e: e.tensor_copy(out=s2run[:, 512:1024], in_=pf[5][:, :]))
            else:
                P.v(lambda e: e.tensor_tensor(out=s2run[:, 512:1024],
                                              in0=s2run[:, 512:1024],
                                              in1=pf[5][:, :], op=ALU.add))
            P.done("v", "ve")
            if c > 0:
                ydma(o2s[0:1, :], o2e[127:128, :], "dys")
            rowd = P.mark("dys")
            P.wait("v", "te", o2done)
            P.wait("v", "dys", rowd)
            if c == 0:
                P.v(lambda e: e.memset(o2s[0:1, :], 0.0))
                P.v(lambda e: e.drain())
            P.v(lambda e: e.tensor_copy(out=o2e[:, 0:512], in_=pf[2][:, :]))
            P.v(lambda e: e.tensor_copy(out=o2e[:, 512:1024], in_=pf[3][:, :]))
            o2ed = P.done("v", "ve")
            P.wait("y", "ve", o2ed)
            ydma(o2s[1:128, :], o2e[0:127, :], "dys")
            shd = P.mark("dys")
            P.wait("v", "dys", shd)
            P.v(lambda e, c=c: e.tensor_tensor(out=y_tm[:, c * D:(c + 1) * D],
                                               in0=y_tm[:, c * D:(c + 1) * D],
                                               in1=o2s[:, :], op=ALU.add))
            P.done("v", "ve")

    P.wait("v", "te")
    P.v(lambda e: e.tensor_copy(out=t2pack[:, 0:512], in_=s2run[:, 512:1024]))
    for dc in range(NDC):
        P.v(lambda e, dc=dc: e.tensor_copy(
            out=t2pack[:, 512 + dc:513 + dc],
            in_=q2t[:, dc * TILE + TILE - 1:dc * TILE + TILE]))
    vd = P.done("v", "ve")
    P.wait("y", "ve", vd)
    ydma(cc2i[:, :], t2pack[:, :])
    P.wait("g", "dy")
    P.inc("g", lambda e: e.collective_compute(
        "AllGather", ALU.bypass, replica_groups=GROUPS,
        ins=[cc2i[:, :]], outs=[cc2o[:, :]]), "cc", 1)
    ccend = P.mark("cc")
    P.wait("v", "ve")
    P.v(lambda e: e.memset(x_tm[:, 0:520], 0.0))
    P.done("v", "ve")
    for r in range(4):
        P.wait("g", "cc", ccend)
        P.wait("g", "ve")
        ld = gdma(ccg[:, 0:520], cc2o[r * 128:(r + 1) * 128, :], "dgc")
        P.wait("v", "dgc", ld)
        P.v(lambda e, r=r: e.scalar_tensor_tensor(
            out=x_tm[:, 0:512], in0=ccg[:, 0:512],
            scalar=pmbc[:, r:r + 1], in1=x_tm[:, 0:512],
            op0=ALU.mult, op1=ALU.add))
        P.v(lambda e, r=r: e.scalar_tensor_tensor(
            out=x_tm[:, 512:520], in0=ccg[:, 512:520],
            scalar=vmbc[:, r:r + 1], in1=x_tm[:, 512:520],
            op0=ALU.mult, op1=ALU.add))
        P.done("v", "ve")
    P.v(lambda e: e.tensor_copy(out=s2in[:, :], in_=x_tm[:, 0:512]))
    P.v(lambda e: e.tensor_copy(out=vprev[:, :], in_=x_tm[:, 512:520]))
    P.done("v", "ve")

    if debug:
        for c in range(NCH):
            P.wait("v", "dbo")
            P.v(lambda e, c=c: e.tensor_copy(out=outf_dbg[:, :],
                                             in_=y_tm[:, c * D:(c + 1) * D]))
            dv = P.done("v", "ve")
            P.wait("y", "ve", dv)
            ydma(dbg["dbg_y"][c * 128:(c + 1) * 128, :], outf_dbg[:, :], "dbo")

    P.wait("s", "ve")
    sdma(bass.AP(cpw_sb, 0, [[cpw_sb.shape[1], 128], [D, NDC], [1, D]]),
         bass.AP(wcp_bf, 0, [[D, 128], [128 * D, NDC], [1, D]]))
    P.wait("t", "ve")
    for hh in range(2):
        bank = pf[4 + hh]
        for hp in range(NP):
            mm(bank[0:1, hp * 64:(hp + 1) * 64],
               vprev[hh * 64:(hh + 1) * 64, hp:hp + 1],
               s2in[hh * 64:(hh + 1) * 64, hp * 64:(hp + 1) * 64],
               start=True, stop=True, tp=(hh * 64, 0))
    tdone = P.done("t", "te")
    P.wait("v", "te", tdone)
    for hh in range(2):
        bank = pf[4 + hh]
        P.v(lambda e, hh=hh, bank=bank: e.tensor_tensor(
            out=bass.AP(y_tm, hh * 64, [[y_tm.shape[1], 1], [128, 8], [1, 64]]),
            in0=bass.AP(y_tm, hh * 64, [[y_tm.shape[1], 1], [128, 8], [1, 64]]),
            in1=bank[0:1, 0:512].rearrange("p (a b) -> p a b", a=8), op=ALU.add))
        P.v(lambda e: e.drain())
    P.done("v", "ve")
    P.wait("t", "ve")

    for tl in range(NT):
        stream_proj(wq_bf.ap(), bq_h, lambda dc: qt[:, dc * TILE:(dc + 1) * TILE],
                    tilebase=tl)
        P.wait("v", "ac")
        for dc in range(NDC):
            if tl == 0:
                P.v(lambda e, dc=dc: e.memset(
                    q2s[:, dc * (TILE + 1):dc * (TILE + 1) + 1], 0.0))
            else:
                P.v(lambda e, dc=dc: e.tensor_copy(
                    out=q2s[:, dc * (TILE + 1):dc * (TILE + 1) + 1],
                    in_=q2s[:, dc * (TILE + 1) + TILE:dc * (TILE + 1) + TILE + 1]))
        P.v(lambda e: e.drain())
        P.v(lambda e: e.tensor_scalar(out=q2t[:, :], in0=qt[:, 0:NDC * TILE], scalar1=0.0,
                                      scalar2=0.98 / 8.0, op0=ALU.max, op1=ALU.mult))
        P.v(lambda e: e.drain())
        for dc in range(NDC):
            P.v(lambda e, dc=dc: e.scalar_tensor_tensor(
                out=q2s[:, dc * (TILE + 1) + 1:dc * (TILE + 1) + 1 + TILE],
                in0=qt[:, dc * TILE:(dc + 1) * TILE], scalar=0.125,
                in1=q2t[:, dc * TILE:(dc + 1) * TILE],
                op0=ALU.mult, op1=ALU.subtract))
        q2d = P.done("v", "ve")
        P.wait("t", "ve", q2d)
        P.wait("t", "ds")
        for cc_i in range(CPT):
            c = tl * CPT + cc_i
            for hh in range(2):
                bank = pf[2 + hh]
                for hp in range(NP):
                    mm(bank[:, hp * 64:(hp + 1) * 64],
                       q2s[hh * 64:(hh + 1) * 64,
                           hp * (TILE + 1) + cc_i * 128:
                           hp * (TILE + 1) + cc_i * 128 + 128],
                       s2in[hh * 64:(hh + 1) * 64, hp * 64:(hp + 1) * 64],
                       start=True, stop=True, tp=(hh * 64, 0))
            i2done = P.done("t", "te")
            P.wait("v", "te", i2done)
            for hh in range(2):
                bank = pf[2 + hh]
                P.v(lambda e, c=c, hh=hh, bank=bank: e.tensor_tensor(
                    out=bass.AP(y_tm, c * D + hh * 64,
                                [[y_tm.shape[1], 128], [128, 8], [1, 64]]),
                    in0=bass.AP(y_tm, c * D + hh * 64,
                                [[y_tm.shape[1], 128], [128, 8], [1, 64]]),
                    in1=bank[:, 0:512].rearrange("p (a b) -> p a b", a=8),
                    op=ALU.add))
                P.v(lambda e: e.drain())
            yd = P.done("v", "ve")
            P.wait("t", "ve", yd)
            for dc in range(NDC):
                P.t(lambda e, c=c, dc=dc: e.transpose(
                    pb[0][:, dc * 128:(dc + 1) * 128],
                    y_tm[:, c * D + dc * 128:c * D + (dc + 1) * 128], id_bf[:, :]))
            tdone = P.done("t", "te")
            P.wait("v", "te", tdone)
            P.v(lambda e: e.tensor_copy(out=yf_sb[:, :], in_=pb[0][:, :]))
            yfd = P.done("v", "ve")
            P.wait("y", "ve", yfd)
            ydma(x_tm[:, :], x_h[c * 128:(c + 1) * 128, :])
            xld = P.mark("dy")
            P.wait("t", "ve", yfd)
            for dc in range(NDC):
                for dg in range(2):
                    mm(pf[dg][:, :], yf_sb[:, dc * 128:(dc + 1) * 128],
                       cpw_sb[:, dc * D + dg * 512:dc * D + (dg + 1) * 512],
                       start=(dc == 0), stop=False)
            for dg in range(2):
                mm(pf[dg][:, :], ones_bf[:, 0:128],
                   bcp_sb[:, dg * 512:(dg + 1) * 512], start=False, stop=True)
            cpd = P.done("t", "te")
            P.wait("v", "te", cpd)
            P.wait("v", "dy", xld)
            for dg in range(2):
                P.v(lambda e, c=c, dg=dg: e.tensor_tensor(
                    out=outacc[:, c * D + dg * 512:c * D + (dg + 1) * 512],
                    in0=pf[dg][:, :],
                    in1=x_tm[:, dg * 512:(dg + 1) * 512], op=ALU.add))
            vd2 = P.done("v", "ve")
            P.wait("t", "ve", vd2)
            P.wait("y", "ve", vd2)

    if debug:
        for c in range(NCH):
            P.wait("v", "dbo")
            P.v(lambda e, c=c: e.tensor_copy(out=outf_dbg[:, :],
                                             in_=outacc[:, c * D:(c + 1) * D]))
            dv = P.done("v", "ve")
            P.wait("y", "ve", dv)
            ydma(dbg["dbg_attn"][c * 128:(c + 1) * 128, :], outf_dbg[:, :], "dbo")

    for c in range(NCH):
        P.wait("t", "ve")
        P.wait("t", "ac")
        for dinc in range(NDC):
            mm(pf[4][0:8, 0:128], wr_sb[:, dinc * 8:(dinc + 1) * 8],
               xn_fm[:, dinc * HWs + 3 + c * 128:dinc * HWs + 3 + (c + 1) * 128],
               start=(dinc == 0), stop=(dinc == NDC - 1))
        tdone = P.done("t", "te")
        P.wait("v", "te", tdone)
        P.v(lambda e: e.tensor_copy(out=sc_fm[:, :], in_=pf[4][0:8, 0:128]))
        scd = P.done("v", "ve")
        P.wait("t", "ve", scd)
        P.t(lambda e: e.transpose(pf[4][:, 0:8], sc_fm[:, :], id_f32[0:8, 0:8]))
        td2 = P.done("t", "te")
        P.wait("s", "te", td2)
        P.s(lambda e: e.activation(out=z_t[:, :], in_=pf[4][:, 0:8], func=AF.Sigmoid))
        sd2 = P.done("s", "ac")
        P.wait("t", "ac", sd2)
        P.wait("v", "ac", sd2)
        P.v(lambda e: e.tensor_tensor(out=z_m[:, :], in0=z_t[:, :],
                                      in1=biasbc[:, :], op=ALU.add))
        P.v(lambda e: e.drain())
        P.v(lambda e: e.max(out=z_srt[:, :], in_=z_m[:, :]))
        P.v(lambda e: e.memset(z_rep[:, :], -3e38))
        P.v(lambda e: e.drain())
        P.v(lambda e: e.tensor_copy(out=z_rep[:, 0:2], in_=z_srt[:, 0:2]))
        P.v(lambda e: e.drain())
        P.v(lambda e: e.match_replace(out=z_srt[:, :], in_to_replace=z_rep[:, :],
                                      in_values=z_m[:, :], imm_value=-3e38))
        P.v(lambda e: e.drain())
        P.v(lambda e: e.tensor_tensor(out=z_rep[:, :], in0=z_m[:, :],
                                      in1=z_srt[:, :], op=ALU.not_equal))
        P.v(lambda e: e.drain())
        P.v(lambda e: e.tensor_tensor(out=z_rep[:, :], in0=z_rep[:, :],
                                      in1=z_t[:, :], op=ALU.mult))
        P.v(lambda e: e.drain())
        P.v(lambda e: e.reduce_sum(out=z_m[:, 0:1], in_=z_rep[:, :], axis=AX.X))
        P.v(lambda e: e.drain())
        P.v(lambda e: e.tensor_scalar_add(out=z_m[:, 0:1], in0=z_m[:, 0:1],
                                          scalar1=1e-9))
        P.v(lambda e: e.drain())
        P.v(lambda e: e.reciprocal(out=z_m[:, 1:2], in_=z_m[:, 0:1]))
        P.v(lambda e: e.drain())
        P.v(lambda e, c=c: e.tensor_scalar_mul(out=cw_sb[:, c * 8:(c + 1) * 8],
                                               in0=z_rep[:, :], scalar1=z_m[:, 1:2]))
        P.v(lambda e: e.drain())
        P.v(lambda e, c=c: e.tensor_tensor(out=cw46[:, c:c + 1],
                                           in0=cw_sb[:, c * 8 + 4:c * 8 + 5],
                                           in1=cw_sb[:, c * 8 + 6:c * 8 + 7],
                                           op=ALU.add))
        P.v(lambda e, c=c: e.tensor_copy(out=cw7[:, c:c + 1],
                                         in_=cw_sb[:, c * 8 + 7:c * 8 + 8]))
        P.done("v", "ve")
    if debug:
        for c in range(NCH):
            P.wait("v", "dbo")
            P.v(lambda e, c=c: e.tensor_copy(out=outf_dbg[:, 0:8],
                                             in_=cw_sb[:, c * 8:(c + 1) * 8]))
            dv = P.done("v", "ve")
            P.wait("y", "ve", dv)
            ydma(dbg["dbg_cw"][c * 128:(c + 1) * 128, :], outf_dbg[:, 0:8], "dbo")

    NW = T // CW
    for dc in range(NDC):
        base = dc * HWs
        for tw in range(NW):
            P.wait("v", "ve")
            P.v(lambda e, base=base, dc=dc, tw=tw: e.tensor_scalar(
                out=conv_fm[:, 0:CW],
                in0=xn_fm[:, base + tw * CW:base + (tw + 1) * CW],
                scalar1=cvw_sb[:, dc * 4:dc * 4 + 1], scalar2=cvb_sb[:, dc:dc + 1],
                op0=ALU.mult, op1=ALU.add))
            for k in range(1, 4):
                P.v(lambda e: e.drain())
                P.v(lambda e, base=base, dc=dc, k=k, tw=tw: e.scalar_tensor_tensor(
                    out=conv_fm[:, 0:CW],
                    in0=xn_fm[:, base + tw * CW + k:base + (tw + 1) * CW + k],
                    scalar=cvw_sb[:, dc * 4 + k:dc * 4 + k + 1],
                    in1=conv_fm[:, 0:CW], op0=ALU.mult, op1=ALU.add))
            cvd = P.done("v", "ve")
            P.wait("t", "ve", cvd)
            for c4 in range(CW // 128):
                c = tw * (CW // 128) + c4
                par = c4 % 2
                P.wait("t", "ve")
                P.t(lambda e, c4=c4, par=par: e.transpose(
                    pb[par][:, 0:128], conv_fm[:, c4 * 128:(c4 + 1) * 128],
                    id_bf[:, :]))
                td3 = P.done("t", "te")
                P.wait("v", "te", td3)
                P.v(lambda e, c=c, dc=dc, par=par: e.tensor_tensor(
                    out=outacc[:, c * D + dc * 128:c * D + (dc + 1) * 128],
                    in0=outacc[:, c * D + dc * 128:c * D + (dc + 1) * 128],
                    in1=pb[par][:, 0:128], op=ALU.add))
                P.done("v", "ve")

    P.wait("v", "ve")
    for c in range(NCH):
        P.v(lambda e, c=c: e.scalar_tensor_tensor(
            out=outacc[:, c * D:(c + 1) * D], in0=xn_tm[:, c * D:(c + 1) * D],
            scalar=cw46[:, c:c + 1], in1=outacc[:, c * D:(c + 1) * D],
            op0=ALU.mult, op1=ALU.add))
        P.v(lambda e: e.drain())
        P.v(lambda e, c=c: e.tensor_scalar(
            out=e_sb[:, :], in0=xn_tm[:, c * D:(c + 1) * D],
            scalar1=0.0, scalar2=cw7[:, c:c + 1], op0=ALU.max, op1=ALU.mult))
        P.v(lambda e: e.drain())
        P.v(lambda e, c=c: e.tensor_tensor(
            out=outacc[:, c * D:(c + 1) * D], in0=outacc[:, c * D:(c + 1) * D],
            in1=e_sb[:, :], op=ALU.add))
        P.v(lambda e: e.drain())
    P.done("v", "ve")

    P.wait("s", "dws")
    estate = {"hidfree": P.mark("ve")}
    for tl in range(NT):
        for ee in range(NE):
            P.wait("s", "te")
            sdma(bass.AP(wdbuf, 0, [[wdbuf.shape[1], 128], [D, 4], [1, D]]),
                 bass.AP(wd_bf, ee * I_FFN * D,
                         [[D, 128], [128 * D, 4], [1, D]]))
            WB = wbig.shape[1]
            gqm = {}
            gqds = {}

            def gq_load(gq):
                half, quarter = gq // 2, gq % 2
                par = gq % 2
                if gq >= 2:
                    P.wait("s", "te", gqm[gq - 2])
                goff = (ee * D + quarter * 4 * 128) * I_FFN + half * 256
                for ic2 in range(2):
                    sdma(bass.AP(wbig, par * 2048 + ic2 * 128,
                                 [[WB, 128], [512, 4], [1, 128]]),
                         bass.AP(wg_bf, goff + ic2 * 128,
                                 [[I_FFN, 128], [128 * I_FFN, 4], [1, 128]]),
                         "dsa" if par == 0 else "dsb")
                    sdma(bass.AP(wbig, par * 2048 + 256 + ic2 * 128,
                                 [[WB, 128], [512, 4], [1, 128]]),
                         bass.AP(wu_bf, goff + ic2 * 128,
                                 [[I_FFN, 128], [128 * I_FFN, 4], [1, 128]]),
                         "dsa" if par == 0 else "dsb")
                gqds[gq] = P.mark("dsa" if par == 0 else "dsb")

            def gq_mm(gq):
                half, quarter = gq // 2, gq % 2
                par = gq % 2
                if quarter == 0:
                    P.wait("t", "ve", estate["hidfree"])
                P.wait("t", "dsa" if par == 0 else "dsb", gqds[gq])
                for d4 in range(4):
                    dinc = quarter * 4 + d4
                    xs = xn_fm[:, dinc * HWs + 3 + tl * TILE:
                               dinc * HWs + 3 + (tl + 1) * TILE]
                    for ic2 in range(2):
                        b0 = par * 2048 + d4 * 512 + ic2 * 128
                        mm(pf[ic2 * 2][:, 0:TILE], wbig[:, b0:b0 + 128], xs,
                           start=(dinc == 0), stop=(dinc == NDC - 1))
                        mm(pf[ic2 * 2 + 1][:, 0:TILE],
                           wbig[:, b0 + 256:b0 + 384], xs,
                           start=(dinc == 0), stop=(dinc == NDC - 1))
                gqm[gq] = P.done("t", "te")

            for half in range(2):
                gq_load(half * 2)
                gq_load(half * 2 + 1)
                gq_mm(half * 2)
                gq_mm(half * 2 + 1)
                gud = P.mark("te")
                P.wait("s", "te", gud)
                for ic2 in range(2):
                    P.s(lambda e, ic2=ic2: e.activation(
                        out=sg_sb[:, ic2 * 512:ic2 * 512 + TILE],
                        in_=pf[ic2 * 2][:, 0:TILE], func=AF.Silu))
                sgd = P.done("s", "ac")
                P.wait("v", "ac", sgd)
                for ic2 in range(2):
                    ic = half * 2 + ic2
                    P.v(lambda e, ic2=ic2, ic=ic: e.tensor_tensor(
                        out=hid[:, ic * TILE:(ic + 1) * TILE],
                        in0=sg_sb[:, ic2 * 512:ic2 * 512 + TILE],
                        in1=pf[ic2 * 2 + 1][:, 0:TILE], op=ALU.mult))
                estate["hidfree"] = P.done("v", "ve")
            hdall = P.mark("ve")
            P.wait("t", "ve", hdall)
            P.wait("t", "ds")
            for cc_i in range(CPT):
                c = tl * CPT + cc_i
                for ic in range(4):
                    mm(pf[4][:, :],
                       hid[:, ic * TILE + cc_i * 128:ic * TILE + (cc_i + 1) * 128],
                       wdbuf[:, ic * D:ic * D + 512],
                       start=(ic == 0), stop=(ic == 3))
                    mm(pf[5][:, :],
                       hid[:, ic * TILE + cc_i * 128:ic * TILE + (cc_i + 1) * 128],
                       wdbuf[:, ic * D + 512:(ic + 1) * D],
                       start=(ic == 0), stop=(ic == 3))
                wdd = P.done("t", "te")
                P.wait("v", "te", wdd)
                for dg in range(2):
                    bk = pf[4 + dg]
                    P.v(lambda e, c=c, dg=dg, bk=bk, ee=ee: e.scalar_tensor_tensor(
                        out=outacc[:, c * D + dg * 512:c * D + (dg + 1) * 512],
                        in0=bk[:, :], scalar=cw_sb[:, c * 8 + ee:c * 8 + ee + 1],
                        in1=outacc[:, c * D + dg * 512:c * D + (dg + 1) * 512],
                        op0=ALU.mult, op1=ALU.add))
                vdd = P.done("v", "ve")
                P.wait("t", "ve", vdd)

    for c in range(NCH):
        P.wait("v", "dout")
        P.v(lambda e, c=c: e.tensor_copy(out=outf[:, :],
                                         in_=outacc[:, c * D:(c + 1) * D]))
        fd = P.done("v", "ve")
        P.wait("y", "ve", fd)
        ydma(out_h[c * 128:(c + 1) * 128, :], outf[:, :], "dout")
    P.wait("y", "dout")
    if debug:
        P.wait("y", "dbo")

    P.replay()
    return nc


def kernel(**inputs):
    T_FULL = inputs["L0_attn_x"].shape[1]
    B = inputs["L0_attn_x"].shape[0]
    ncore = 8
    per_b = ncore // B
    T = T_FULL // per_b
    x = np.asarray(inputs["L0_attn_x"], np.float32)

    wmap = {
        "q1_w": "L0_attn_q1_w", "q1_b": "L0_attn_q1_b",
        "k1_w": "L0_attn_k1_w", "k1_b": "L0_attn_k1_b",
        "k2_w": "L0_attn_k2_w", "k2_b": "L0_attn_k2_b",
        "gw_w": "L0_attn_gw_w", "gw_b": "L0_attn_gw_b",
        "sw_w": "L0_attn_sw_w", "sw_b": "L0_attn_sw_b",
        "cp_w": "L0_attn_cp_w", "cp_b": "L0_attn_cp_b",
        "Wr": "L0_ffn_Wr", "router_bias": "L0_ffn_router_bias",
        "Wg": "L0_ffn_Wg", "Wu": "L0_ffn_Wu", "Wd": "L0_ffn_Wd",
        "conv_w": "L0_ffn_conv_w", "conv_b": "L0_ffn_conv_b",
    }
    weights = {k: np.ascontiguousarray(np.asarray(inputs[v], np.float32))
               for k, v in wmap.items()}

    in_maps = []
    for core in range(ncore):
        b = core // per_b
        q = core % per_b
        a0 = q * T
        xs = np.ascontiguousarray(x[b, a0:a0 + T, :])
        halo = np.zeros((4, x.shape[2]), np.float32)
        if a0 > 0:
            halo[0:3, :] = x[b, a0 - 3:a0, :]
        if a0 + T < T_FULL:
            halo[3, :] = x[b, a0 + T, :]
        pm = np.zeros((4, 1), np.float32)
        pm[:q, 0] = 1.0
        vm = np.zeros((4, 1), np.float32)
        if q > 0:
            vm[q - 1, 0] = 1.0
        m = {"x": xs, "xhalo": halo, "pmask": pm, "vmask": vm}
        m.update(weights)
        in_maps.append(m)

    nc = build(T, debug=False)
    res = run_bass_kernel_spmd(nc, in_maps, core_ids=list(range(ncore)))
    out = np.zeros((B, T_FULL, x.shape[2]), np.float32)
    for core in range(ncore):
        b = core // per_b
        q = core % per_b
        out[b, q * T:(q + 1) * T, :] = res.results[core]["out"]
    return out
